# version 11
# baseline (speedup 1.0000x reference)
import contextlib
import numpy as np
import ml_dtypes
import concourse.bass as bass
import concourse.mybir as mybir
from concourse.bass_utils import run_bass_kernel_spmd

F32 = mybir.dt.float32
BF16 = mybir.dt.bfloat16
I32 = mybir.dt.int32
AF = mybir.ActivationFunctionType
ALU = mybir.AluOpType
AX = mybir.AxisListType

NCORES = 8
TOK = 1024
D = 4096
KC = 32
TT = 8
NSUP = 16
NW = 256
GRP = 8
NG = KC // GRP
INV_SQRT_DK = 0.17677669529663687
LN_EPS = 1e-5

ENGS = ("sync", "gpsimd", "scalar", "vector", "tensor")


class Prog:

    def __init__(self):
        self.ops = {e: [] for e in ENGS}
        self.cnt = {}

    def val(self, sem):
        return self.cnt.get(sem, 0)

    def op(self, eng, build, waits=(), inc=None, inc_n=1):
        tgt = None
        if inc is not None:
            self.cnt[inc] = self.cnt.get(inc, 0) + inc_n
            tgt = self.cnt[inc]
        self.ops[eng].append((tuple(waits), build, inc, inc_n))
        return tgt

    def chain(self, eng, build, sem, inc_n=1, waits=()):
        return self.op(eng, build, waits=[(sem, self.val(sem))] + list(waits),
                       inc=sem, inc_n=inc_n)

    def replay(self, eng_name, eng):
        for waits, build, inc, inc_n in self.ops[eng_name]:
            for sem, v in waits:
                if v > 0:
                    eng.wait_ge(sem, v)
            inst = build(eng) if build is not None else None
            if inc is not None:
                if inst is None:
                    inst = eng.nop()
                inst.then_inc(inc, inc_n)


class NS:
    pass



def emit_consts(P, t, sem):
    C = sem.consts

    def _ident_ops(ap, n):
        yield lambda g: g.memset(ap, 0.0)
        yield lambda g: g.affine_select(out=ap, in_=ap,
                                        compare_op=ALU.not_equal, fill=1.0,
                                        base=0, pattern=[[-1, n]],
                                        channel_multiplier=1)

    ops = []
    ops += list(_ident_ops(t.ident_bf[:], 128))
    ops += list(_ident_ops(t.ident_f32[:], 128))
    ops.append(lambda g: g.memset(t.ones_f32[:], 1.0))
    ops.append(lambda g: g.memset(t.zrow_bf[:], 0.0))
    ops.append(lambda g: g.memset(t.ones_bf[:], 1.0))
    ops.append(lambda g: g.memset(t.cst[:, 0:1], INV_SQRT_DK))
    ops.append(lambda g: g.memset(t.cst[:, 1:2], 2.0))
    ops.append(lambda g: g.memset(t.cst[:, 2:3], 0.5))
    ops.append(lambda g: g.memset(t.E32[:], 1.0))
    ops.append(lambda g: g.affine_select(out=t.E32[:], in_=t.E32[:],
                                         compare_op=ALU.is_ge, fill=0.0,
                                         base=0, pattern=[[1, D]],
                                         channel_multiplier=-32))
    ops.append(lambda g: g.affine_select(out=t.E32[:], in_=t.E32[:],
                                         compare_op=ALU.is_ge, fill=0.0,
                                         base=31, pattern=[[-1, D]],
                                         channel_multiplier=32))
    for f in ops:
        P.chain("gpsimd", f, C)


def emit_mask2_group(P, t, sem, maskT2, bt):
    mt = [t.mbig.bitcast(BF16)[:, i * 1024:(i + 1) * 1024] for i in range(4)]
    strip_banks = [t.pgA, t.pgB, t.pgC, t.pgD]
    for qc in range(8):
        i = bt * 8 + qc
        par = i % 4
        waits = []
        if i >= 4:
            waits.append((sem.mpe, P.val(sem.mpe) - 3))
        P.op("gpsimd",
             (lambda g, bt=bt, qc=qc, par=par: g.dma_start(
                 out=mt[par][:],
                 in_=maskT2[qc * 128:(qc + 1) * 128,
                            bt * 1024:(bt + 1) * 1024])),
             waits=waits, inc=sem.mdma[par], inc_n=16)
        dmav = P.val(sem.mdma[par])

        def _mm(te, bt=bt, qc=qc, par=par):
            last = None
            for j in range(2):
                last = te.matmul(
                    strip_banks[(2 * bt + j) % 4][0:1, 0:512],
                    t.ones_bf[:, 0:1],
                    mt[par][:, j * 512:(j + 1) * 512],
                    start=(qc == 0), stop=(qc == 7))
            return last
        mmw = [(sem.mdma[par], dmav)]
        if qc == 0:
            mmw.append((sem.consts, P.val(sem.consts)))
            if bt >= 2:
                mmw.append((sem.mevac, P.val(sem.mevac)))
        P.op("tensor", _mm, waits=mmw, inc=sem.mpe)
    for j in range(2):
        sidx = 2 * bt + j
        par = sidx % 2
        waits = [(sem.mpe, P.val(sem.mpe))]
        if sidx >= 2:
            waits.append((sem.msc[par], P.val(sem.msc[par])))
        P.op("scalar",
             (lambda s, bt=bt, j=j, par=par: s.copy(
                 t.mscat[0:1, par * 512:(par + 1) * 512],
                 strip_banks[(2 * bt + j) % 4][0:1, 0:512])),
             waits=waits, inc=sem.mevac)
        P.op("gpsimd",
             (lambda g, sidx=sidx, par=par: g.dma_start(
                 out=t.bmq_d[4 * sidx:4 * (sidx + 1), :]
                     .rearrange("(x a) n -> x a n", x=1),
                 in_=t.mscat[0:1, par * 512:(par + 1) * 512]
                     .rearrange("x (a n) -> x a n", a=4))),
             waits=[(sem.mevac, P.val(sem.mevac))],
             inc=sem.msc[par], inc_n=16)


def emit_bmp(P, t, sem):
    P.op("gpsimd", lambda g: g.dma_start(out=t.bmQ[:], in_=t.bmq_d[:, :]),
         waits=[(sem.msc[0], P.val(sem.msc[0])),
                (sem.msc[1], P.val(sem.msc[1]))],
         inc=sem.msc[0], inc_n=16)
    P.op("vector",
         lambda v: v.tensor_scalar(out=t.Qbf[:], in0=t.bmQ[:], scalar1=1.0,
                                   scalar2=None, op0=ALU.is_ge),
         waits=[(sem.msc[0], P.val(sem.msc[0]))],
         inc=sem.mred)
    qready = P.val(sem.mred)
    for c in range(KC):
        waits = [(sem.mred, qready)]
        if c >= 2:
            waits.append((sem.mevac, P.val(sem.mevac) - 1))
        psq = t.pgA if c % 2 == 0 else t.pgC
        P.op("tensor",
             (lambda te, c=c, psq=psq: te.matmul(
                 psq[:, 0:128], t.E32[:, c * 128:(c + 1) * 128],
                 t.Qbf[:], start=True, stop=True)),
             waits=waits, inc=sem.mpe)
        P.op("scalar",
             (lambda s, c=c, psq=psq: s.copy(
                 t.bmP[:, c * 128:(c + 1) * 128], psq[:, 0:128])),
             waits=[(sem.mpe, P.val(sem.mpe))], inc=sem.mevac)


def emit_ln_tile(P, t, sem, xin_tile, par, xnt_slot, first_waits=()):
    L = sem.lnch
    ls = t.lnsc[par]
    waits = list(first_waits)
    if P.val(sem.xn) >= 2:
        waits.append((sem.xn, P.val(sem.xn) - 1))
    P.op("sync",
         (lambda sy, par=par, xin_tile=xin_tile: sy.dma_start(
             out=t.xstage[par][:], in_=xin_tile)),
         waits=waits, inc=sem.xdma[par], inc_n=16)
    ld = (sem.xdma[par], P.val(sem.xdma[par]))
    sq_waits = [ld]
    if P.val(sem.tevac) >= 2 * KC:
        sq_waits.append((sem.tevac, P.val(sem.tevac) - KC))
    P.op("scalar",
         (lambda s, par=par, ls=ls: s.activation(
             out=t.xnbuf[par][:], in_=t.xstage[par][:], func=AF.Square,
             accum_out=ls[:, 1:2])),
         waits=sq_waits, inc=sem.sq)
    P.op("vector",
         (lambda v, par=par, ls=ls: v.reduce_sum(
             ls[:, 0:1], t.xstage[par][:], axis=AX.X)),
         waits=[ld, (L, P.val(L))], inc=L)
    P.chain("vector",
            (lambda v, ls=ls: v.tensor_scalar(
                out=ls[:, 2:3], in0=ls[:, 0:1], scalar1=1.0 / D,
                scalar2=None, op0=ALU.mult)), L)
    P.op("vector",
         (lambda v, ls=ls: v.tensor_scalar(
             out=ls[:, 3:4], in0=ls[:, 1:2], scalar1=1.0 / D,
             scalar2=None, op0=ALU.mult)),
         waits=[(L, P.val(L)), (sem.sq, P.val(sem.sq))], inc=L)
    P.chain("vector",
            (lambda v, ls=ls: v.tensor_scalar(
                out=ls[:, 4:5], in0=ls[:, 2:3], scalar1=ls[:, 2:3],
                scalar2=None, op0=ALU.mult)), L)
    P.chain("vector",
            (lambda v, ls=ls: v.tensor_scalar(
                out=ls[:, 5:6], in0=ls[:, 3:4], scalar1=ls[:, 4:5],
                scalar2=LN_EPS, op0=ALU.subtract, op1=ALU.add)), L)
    P.chain("scalar",
            (lambda s, ls=ls: s.activation(out=ls[:, 6:7], in_=ls[:, 5:6],
                                           func=AF.Sqrt)), L)
    P.chain("vector",
            (lambda v, ls=ls: v.reciprocal(ls[:, 7:8], ls[:, 6:7])), L)
    P.chain("vector",
            (lambda v, ls=ls: v.tensor_scalar(
                out=ls[:, 8:9], in0=ls[:, 6:7], scalar1=ls[:, 7:8],
                scalar2=None, op0=ALU.mult)), L)
    P.chain("vector",
            (lambda v, ls=ls: v.tensor_scalar(
                out=ls[:, 9:10], in0=ls[:, 8:9], scalar1=-1.0,
                scalar2=2.0, op0=ALU.mult, op1=ALU.add)), L)
    P.chain("vector",
            (lambda v, ls=ls: v.tensor_scalar(
                out=ls[:, 10:11], in0=ls[:, 7:8], scalar1=ls[:, 9:10],
                scalar2=None, op0=ALU.mult)), L)
    P.chain("vector",
            (lambda v, ls=ls: v.tensor_scalar(
                out=ls[:, 11:12], in0=ls[:, 2:3], scalar1=ls[:, 10:11],
                scalar2=-1.0, op0=ALU.mult, op1=ALU.mult)), L)
    P.op("scalar",
         (lambda s, par=par, ls=ls: s.activation(
             out=t.xnbuf[par][:], in_=t.xstage[par][:], func=AF.Identity,
             scale=ls[:, 10:11], bias=ls[:, 11:12])),
         waits=[(L, P.val(L))], inc=sem.xn)
    xnv = P.val(sem.xn)
    for c in range(KC):
        psl = t.pmain[c % 2][:, 0:128].bitcast(BF16)
        waits = [(sem.xn, xnv)]
        if P.val(sem.tevac) > 0:
            waits.append((sem.tevac, P.val(sem.tevac) - 1))
        if P.val(sem.tevac) < 2:
            waits.append((sem.consts, P.val(sem.consts)))
        P.op("tensor",
             (lambda te, c=c, par=par, psl=psl: te.transpose(
                 psl[:, 0:128],
                 t.xnbuf[par][:, c * 128:(c + 1) * 128], t.ident_bf[:])),
             waits=waits, inc=sem.tpe)
        P.op("scalar",
             (lambda s, psl=psl, slot=xnt_slot(c): s.copy(slot, psl[:, 0:128])),
             waits=[(sem.tpe, P.val(sem.tpe))], inc=sem.tevac)


def emit_recip_refine(P, t, sem, gs, src_col, dst_base):
    C = sem.gch
    P.chain("vector",
            (lambda v: v.reciprocal(gs[:, dst_base:dst_base + 1],
                                    gs[:, src_col:src_col + 1])), C)
    P.chain("vector",
            (lambda v: v.tensor_scalar(
                out=gs[:, dst_base + 1:dst_base + 2],
                in0=gs[:, src_col:src_col + 1],
                scalar1=gs[:, dst_base:dst_base + 1], scalar2=None,
                op0=ALU.mult)), C)
    P.chain("vector",
            (lambda v: v.tensor_scalar(
                out=gs[:, dst_base + 2:dst_base + 3],
                in0=gs[:, dst_base + 1:dst_base + 2],
                scalar1=-1.0, scalar2=2.0, op0=ALU.mult, op1=ALU.add)), C)
    P.chain("vector",
            (lambda v: v.tensor_scalar(
                out=gs[:, dst_base + 3:dst_base + 4],
                in0=gs[:, dst_base:dst_base + 1],
                scalar1=gs[:, dst_base + 2:dst_base + 3], scalar2=None,
                op0=ALU.mult)), C)


def emit_qk_tile(P, t, sem, tt):
    C = sem.gch
    gs = t.gsm[0]

    def _qk(te, tt=tt):
        last = None
        for c in range(KC):
            w = t.wring_a if c < 16 else t.wring_f
            cc = c % 16
            last = te.matmul(
                t.pmain[2][:, 0:512],
                t.xnT[:, c * 1024 + tt * 128: c * 1024 + (tt + 1) * 128],
                w[:, cc * 512:(cc + 1) * 512],
                start=(c == 0), stop=(c == KC - 1))
        return last
    P.chain("tensor", _qk, C,
            waits=[(sem.gqw, P.val(sem.gqw)), (sem.tevac, P.val(sem.tevac)),
                   (sem.consts, P.val(sem.consts))])
    P.chain("scalar",
            lambda s: s.activation(out=t.gq[:], in_=t.pmain[2][:, 0:256],
                                   func=AF.Identity, scale=t.cst[:, 0:1]), C)
    P.chain("vector",
            lambda v: v.tensor_tensor(out=t.gqk[:], in0=t.pmain[2][:, 256:512],
                                      in1=t.gq[:], op=ALU.mult), C)
    P.chain("vector",
            (lambda v, gs=gs: v.tensor_reduce(
                out=gs[:, 0:8][:, :, None],
                in_=t.gqk[:].rearrange("p (a b) -> p a b", a=8),
                axis=AX.X, op=ALU.add)), C)
    P.chain("vector",
            (lambda v, gs=gs: v.reduce_max(gs[:, 8:9], gs[:, 0:8],
                                           axis=AX.X, negate=True)), C)
    P.chain("scalar",
            (lambda s, gs=gs: s.activation(
                out=gs[:, 16:24], in_=gs[:, 0:8], func=AF.Exp,
                bias=gs[:, 8:9], scale=1.0)), C)
    P.chain("vector",
            (lambda v, gs=gs: v.reduce_sum(gs[:, 9:10], gs[:, 16:24],
                                           axis=AX.X)), C)
    emit_recip_refine(P, t, sem, gs, 9, 10)
    P.chain("vector",
            (lambda v, gs=gs, tt=tt: v.tensor_scalar(
                out=t.gsm[1][:, 8 * tt:8 * (tt + 1)], in0=gs[:, 16:24],
                scalar1=gs[:, 13:14], scalar2=None, op0=ALU.mult)), C)


def emit_hw_phase(P, t, sem, a2t_dram, bm2_dram):
    C = sem.gch
    P.op("gpsimd",
         lambda g: g.dma_start(
             out=t.a2t_bf[:, 0:D].rearrange("p (c n) -> p c n", c=KC),
             in_=a2t_dram[:, :].rearrange("(c p) n -> p c n", c=KC)),
         waits=[(sem.mpe, P.val(sem.mpe))], inc=sem.gw, inc_n=16)
    gwv = P.val(sem.gw)
    for tt in range(TT):
        def _h(te, tt=tt):
            last = None
            for c in range(KC):
                last = te.matmul(
                    t.pmain[2][:, 0:128],
                    t.xnT[:, c * 1024 + tt * 128: c * 1024 + (tt + 1) * 128],
                    t.a2t_bf[:, c * 128:(c + 1) * 128],
                    start=(c == 0), stop=(c == KC - 1))
            return last
        P.chain("tensor", _h, C, waits=[(sem.gw, gwv)])
        P.chain("vector",
                (lambda v, tt=tt: v.tensor_tensor(
                    out=t.hw_tok[:].rearrange("p (a b) -> p a b", a=8),
                    in0=t.pmain[2][:, 0:128].rearrange("p (a b) -> p a b", a=8),
                    in1=t.gsm[1][:, 8 * tt:8 * (tt + 1)][:, :, None]
                        .broadcast_to([128, 8, 16]),
                    op=ALU.mult)), C)
        P.chain("tensor",
                lambda te: te.transpose(
                    t.pmain[3][:, 0:128].bitcast(BF16)[:, 0:128],
                    t.hw_tok[:], t.ident_bf[:]), C)
        P.chain("scalar",
                (lambda s, tt=tt: s.copy(
                    t.hwT[:, tt * 128:(tt + 1) * 128],
                    t.pmain[3][:, 0:128].bitcast(BF16)[:, 0:128])), C)
    P.op("gpsimd",
         lambda g: g.dma_start(
             out=t.bm2_bf[:].rearrange("p (c n) -> p c n", c=1),
             in_=bm2_dram[:, :].rearrange("(c p) n -> p c n", c=1)),
         waits=[(sem.gch, P.val(sem.gch))], inc=sem.gw, inc_n=16)


def emit_zpre_tile(P, t, sem, tt, zT_bf, zT_f32, h8T):
    C = sem.gch

    def _zmm(te, tt=tt):
        last = None
        for c in range(KC):
            last = te.matmul(
                t.pmain[2][:, 0:40],
                t.zsrc(c, tt),
                t.ah8_bf[:, c * 40:(c + 1) * 40],
                start=(c == 0), stop=(c == KC - 1))
        return last
    P.chain("tensor", _zmm, C,
            waits=[(sem.gqw, P.val(sem.gqw)), (sem.tevac, P.val(sem.tevac)),
                   (sem.sld, P.val(sem.sld))])
    P.chain("scalar", lambda s: s.copy(t.zh_tok[:], t.pmain[2][:, 0:40]), C)
    P.chain("tensor",
            lambda te: te.transpose(t.pmain[3][0:32, 0:128], t.zh_tok[:, 0:32],
                                    t.ident_f32[:]), C)

    def _zev(s, tt=tt):
        last = s.copy(zT_bf[:, tt * 128:(tt + 1) * 128],
                      t.pmain[3][0:32, 0:128])
        if zT_f32 is not None:
            last = s.copy(zT_f32[:, tt * 128:(tt + 1) * 128],
                          t.pmain[3][0:32, 0:128])
        return last
    P.chain("scalar", _zev, C)
    P.chain("tensor",
            lambda te: te.transpose(t.pmain[3][0:8, 128:256],
                                    t.zh_tok[:, 32:40], t.ident_f32[:]), C)
    P.chain("scalar",
            (lambda s, tt=tt: s.activation(
                out=h8T[:, tt * 128:(tt + 1) * 128],
                in_=t.pmain[3][0:8, 128:256], func=AF.Identity,
                bias=t.b8col[:, 0:1], scale=1.0)), C)


def emit_anchor_gate(P, t, sem, zT_bf_anch, h8T_anch, limt_view):
    C = sem.gch
    for aj in range(4):
        P.chain("tensor",
                (lambda te, aj=aj: te.matmul(
                    t.pmain[2][:, 64 + aj * 8:64 + (aj + 1) * 8],
                    zT_bf_anch[:, aj * 128:(aj + 1) * 128],
                    limt_view, start=True, stop=True)), C)
        P.chain("tensor",
                (lambda te, aj=aj: te.transpose(
                    t.pmain[3][:, 256 + aj * 8:256 + (aj + 1) * 8],
                    h8T_anch[:, aj * 128:(aj + 1) * 128],
                    t.ident_f32[0:8, 0:8])), C)
    P.chain("scalar", lambda s: s.copy(t.dsbuf[:], t.pmain[2][:, 64:96]), C)
    P.chain("scalar", lambda s: s.copy(t.hsbuf[:], t.pmain[3][:, 256:288]), C)


def emit_locmax(P, t, sem):
    C = sem.gch
    P.chain("vector",
            lambda v: v.reduce_max(t.gsm[0][:, 0:1], t.hsbuf[:], axis=AX.X,
                                   apply_absolute_value=True), C)
    P.chain("vector",
            lambda v: v.reduce_max(t.gsm[0][:, 1:2], t.dsbuf[:], axis=AX.X,
                                   apply_absolute_value=True), C)
    P.chain("tensor",
            lambda te: te.transpose(t.pmain[3][0:2, 0:128], t.gsm[0][:, 0:2],
                                    t.ident_f32[:]), C)
    P.chain("scalar",
            lambda s: s.copy(t.max2x128[:], t.pmain[3][0:2, 0:128]), C)
    P.chain("vector",
            lambda v: v.reduce_max(t.locmax[:], t.max2x128[:], axis=AX.X), C)


def emit_l1_post(P, t, sem, gmax_dram, gsc_dram, bmat_dram, lim_dram):
    C = sem.gch
    g0 = t.gsm[0]
    P.op("sync", lambda sy: sy.dma_start(out=t.gmax8[:], in_=gmax_dram[:, :]),
         waits=[(sem.xn, P.val(sem.xn))], inc=sem.sld, inc_n=16)
    P.op("gpsimd",
         lambda g: g.dma_start(
             out=t.bmat_bf[:].rearrange("p (c n) -> p c n", c=1),
             in_=bmat_dram[:, :].rearrange("(c p) n -> p c n", c=1)),
         waits=[(sem.mpe, P.val(sem.mpe))], inc=sem.gld, inc_n=16)
    P.op("sync", lambda sy: sy.dma_start(out=t.lim_f32[:], in_=lim_dram[:, :]),
         waits=[(sem.xn, P.val(sem.xn))], inc=sem.sld, inc_n=16)
    P.op("vector", None,
         waits=[(sem.gld, P.val(sem.gld)), (sem.sld, P.val(sem.sld)),
                (sem.consts, P.val(sem.consts)), (sem.mevac, P.val(sem.mevac)),
                (sem.gch, P.val(sem.gch))],
         inc=C)
    P.chain("scalar", lambda s: s.activation(out=t.lim_bf[:], in_=t.lim_f32[:],
                                             func=AF.Identity,
                                             scale=t.cst[0:8, 2:3]), C)
    P.chain("tensor",
            lambda te: te.transpose(t.pgB[0:2, 0:8], t.gmax8[:],
                                    t.ident_f32[0:8, 0:8]), C)
    P.chain("scalar", lambda s: s.copy(t.max2x128[:, 0:8], t.pgB[0:2, 0:8]), C)
    P.chain("vector",
            lambda v: v.reduce_max(t.gsc2[:, 0:1], t.max2x128[:, 0:8],
                                   axis=AX.X), C)
    P.chain("vector",
            lambda v: v.tensor_scalar(out=t.gsc2[:, 1:2], in0=t.gsc2[:, 0:1],
                                      scalar1=1e-6, scalar2=None,
                                      op0=ALU.max), C)
    P.chain("sync",
            lambda sy: sy.dma_start(out=gsc_dram[:, :], in_=t.gsc2[:, 1:2]),
            C, inc_n=16)
    P.chain("sync",
            lambda sy: sy.dma_start(
                out=t.gscH[:], in_=gsc_dram[0:1, 0:1].to_broadcast((128, 1))),
            C, inc_n=16)
    P.chain("sync",
            lambda sy: sy.dma_start(
                out=t.gscD[:], in_=gsc_dram[1:2, 0:1].to_broadcast((128, 1))),
            C, inc_n=16)
    P.chain("vector", lambda v: v.tensor_copy(g0[:, 0:1], t.gscH[:]), C)
    emit_recip_refine(P, t, sem, g0, 0, 1)
    P.chain("vector", lambda v: v.tensor_copy(g0[:, 8:9], t.gscD[:]), C)
    emit_recip_refine(P, t, sem, g0, 8, 9)
    P.chain("vector",
            lambda v: v.tensor_scalar(out=g0[:, 16:17], in0=g0[:, 4:5],
                                      scalar1=0.5, scalar2=None,
                                      op0=ALU.mult), C)
    P.chain("vector",
            lambda v: v.tensor_scalar(out=g0[:, 17:18], in0=g0[:, 12:13],
                                      scalar1=0.5, scalar2=None,
                                      op0=ALU.mult), C)

    for aj in range(4):
        gs = t.gsm[1]
        sl = slice(aj * 8, (aj + 1) * 8)
        P.chain("vector",
                (lambda v, gs=gs, sl=sl: v.tensor_scalar(
                    out=gs[:, 16:24], in0=t.dsbuf[:, sl],
                    scalar1=g0[:, 17:18], scalar2=None, op0=ALU.mult)), C)
        P.chain("vector",
                (lambda v, gs=gs, sl=sl: v.scalar_tensor_tensor(
                    out=gs[:, 16:24], in0=t.hsbuf[:, sl], scalar=g0[:, 16:17],
                    in1=gs[:, 16:24], op0=ALU.mult, op1=ALU.add)), C)
        P.chain("vector",
                (lambda v, gs=gs: v.reduce_max(gs[:, 24:25], gs[:, 16:24],
                                               axis=AX.X, negate=True)), C)
        P.chain("scalar",
                (lambda s, gs=gs: s.activation(
                    out=gs[:, 32:40], in_=gs[:, 16:24], func=AF.Exp,
                    bias=gs[:, 24:25], scale=1.0)), C)
        P.chain("vector",
                (lambda v, gs=gs: v.reduce_sum(gs[:, 25:26], gs[:, 32:40],
                                               axis=AX.X)), C)
        emit_recip_refine(P, t, sem, gs, 25, 26)
        P.chain("vector",
                (lambda v, gs=gs: v.tensor_scalar(
                    out=gs[:, 40:48], in0=gs[:, 32:40],
                    scalar1=gs[:, 29:30], scalar2=None, op0=ALU.mult)), C)
        P.chain("vector",
                (lambda v, gs=gs: v.reduce_max(gs[:, 48:49], gs[:, 40:48],
                                               axis=AX.X)), C)
        P.chain("vector",
                (lambda v, gs=gs: v.tensor_scalar(
                    out=gs[:, 56:64], in0=gs[:, 40:48],
                    scalar1=gs[:, 48:49], scalar2=None, op0=ALU.is_ge)), C)
        P.chain("vector",
                (lambda v, gs=gs: v.reduce_sum(gs[:, 49:50], gs[:, 56:64],
                                               axis=AX.X)), C)
        P.chain("vector",
                (lambda v, gs=gs: v.scalar_tensor_tensor(
                    out=gs[:, 56:64], in0=gs[:, 56:64], scalar=-1e30,
                    in1=gs[:, 40:48], op0=ALU.mult, op1=ALU.add)), C)
        P.chain("vector",
                (lambda v, gs=gs: v.reduce_max(gs[:, 50:51], gs[:, 56:64],
                                               axis=AX.X)), C)
        P.chain("vector",
                (lambda v, gs=gs: v.tensor_scalar(
                    out=gs[:, 51:52], in0=gs[:, 49:50], scalar1=2.0,
                    scalar2=None, op0=ALU.is_ge)), C)
        P.chain("vector",
                (lambda v, gs=gs: v.tensor_scalar(
                    out=gs[:, 52:53], in0=gs[:, 48:49],
                    scalar1=gs[:, 50:51], scalar2=None, op0=ALU.subtract)), C)
        P.chain("vector",
                (lambda v, gs=gs: v.tensor_scalar(
                    out=gs[:, 53:54], in0=gs[:, 52:53],
                    scalar1=gs[:, 51:52], scalar2=None, op0=ALU.mult)), C)
        P.chain("vector",
                (lambda v, gs=gs: v.tensor_scalar(
                    out=gs[:, 54:55], in0=gs[:, 53:54],
                    scalar1=gs[:, 50:51], scalar2=None, op0=ALU.add)), C)
        P.chain("vector",
                (lambda v, gs=gs: v.tensor_scalar(
                    out=gs[:, 55:56], in0=gs[:, 54:55], scalar1=-2.0,
                    scalar2=None, op0=ALU.mult)), C)
        P.chain("scalar",
                (lambda s, gs=gs: s.activation(
                    out=gs[:, 56:64], in_=gs[:, 40:48], func=AF.Sigmoid,
                    bias=gs[:, 55:56], scale=t.cst[:, 1:2])), C)
        P.chain("vector",
                (lambda v, gs=gs: v.tensor_tensor(
                    out=gs[:, 40:48], in0=gs[:, 40:48], in1=gs[:, 56:64],
                    op=ALU.mult)), C)
        P.chain("vector",
                (lambda v, gs=gs: v.reduce_sum(gs[:, 25:26], gs[:, 40:48],
                                               axis=AX.X)), C)
        P.chain("vector",
                (lambda v, gs=gs: v.tensor_scalar(
                    out=gs[:, 26:27], in0=gs[:, 25:26], scalar1=1e-9,
                    scalar2=None, op0=ALU.add)), C)
        emit_recip_refine(P, t, sem, gs, 26, 27)
        P.chain("vector",
                (lambda v, gs=gs: v.tensor_scalar(
                    out=gs[:, 40:48], in0=gs[:, 40:48],
                    scalar1=gs[:, 30:31], scalar2=None, op0=ALU.mult)), C)
        P.chain("tensor",
                (lambda te, gs=gs: te.transpose(
                    t.pgB[0:8, 0:128], gs[:, 40:48], t.ident_f32[:])), C)
        P.chain("scalar",
                (lambda s, aj=aj: s.copy(
                    t.wfullT[:, aj * 256:(aj + 1) * 256]
                        .rearrange("p (a two) -> p a two", two=2),
                    t.pgB[0:8, 0:128][:, :, None].broadcast_to([8, 128, 2]))),
                C)
    P.chain("tensor",
            lambda te: te.matmul(t.pgA[0:32, 0:512], t.lim_bf[:],
                                 t.wfullT[:, 0:512], start=True, stop=True), C)
    P.chain("tensor",
            lambda te: te.matmul(t.pgC[0:32, 0:512], t.lim_bf[:],
                                 t.wfullT[:, 512:1024], start=True, stop=True),
            C)
    P.chain("vector",
            lambda v: v.tensor_tensor(out=t.zp_bf[:, 0:512],
                                      in0=t.zT_f32[:, 0:512],
                                      in1=t.pgA[0:32, 0:512], op=ALU.mult), C)
    P.chain("vector",
            lambda v: v.tensor_tensor(out=t.zp_bf[:, 512:1024],
                                      in0=t.zT_f32[:, 512:1024],
                                      in1=t.pgC[0:32, 0:512], op=ALU.mult), C)


def emit_main(P, t, sem, wa_dram, wf_dram, xin, xout, layer, extra_wait=(),
              ring_free_wait=(), x0_first_wait=()):
    for s in range(NSUP):
        for g in range(NG):
            for (ring, wd, wsems, cons_base) in (
                    (t.wring_a, wa_dram, sem.wda, 0),
                    (t.wring_f, wf_dram, sem.wdf, KC)):
                wsem = wsems[g]
                waits = []
                if s == 0:
                    waits.extend(ring_free_wait)
                if s > 0:
                    need = (s - 1) * 4 * KC + 2 * (cons_base + (g + 1) * GRP)
                    waits.append((sem.mmc, need))
                P.op("gpsimd",
                     (lambda gg, ring=ring, wd=wd, s=s, g=g: gg.dma_start(
                         out=ring[:, g * GRP * NW:(g + 1) * GRP * NW]
                             .rearrange("p (c n) -> p c n", c=GRP),
                         in_=wd[g * GRP * 128:(g + 1) * GRP * 128,
                                s * NW:(s + 1) * NW]
                             .rearrange("(c p) n -> p c n", c=GRP))),
                     waits=waits, inc=wsem, inc_n=16)
        for g in range(NG):
            P.op("vector",
                 (lambda v, s=s, g=g: v.tensor_tensor(
                     out=t.wring_a[:, g * GRP * NW:(g + 1) * GRP * NW]
                         .rearrange("p (c a b) -> p c a b", c=GRP, a=8),
                     in0=t.wring_a[:, g * GRP * NW:(g + 1) * GRP * NW]
                         .rearrange("p (c a b) -> p c a b", c=GRP, a=8),
                     in1=t.bmP[:].rearrange("p (c sp a) -> p c sp a", c=KC, sp=16)
                         [:, g * GRP:(g + 1) * GRP, s, :][:, :, :, None]
                         .broadcast_to([128, GRP, 8, 32]),
                     op=ALU.mult)),
                 waits=[(sem.wda[g], P.val(sem.wda[g])),
                        (sem.mevac, P.val(sem.mevac))],
                 inc=sem.wmask)
        P.op("sync",
             (lambda sy, s=s: sy.dma_start(
                 out=t.x0stage[s % 2][:].rearrange("p (tt n) -> p tt n", tt=TT),
                 in_=xin[:, s * NW:(s + 1) * NW]
                     .rearrange("(tt p) n -> p tt n", tt=TT))),
             waits=([(sem.evac, P.val(sem.evac) - TT)] if s >= 2
                    else list(x0_first_wait)),
             inc=sem.x0d, inc_n=16)
        x0v = P.val(sem.x0d)

        for ci in range(2 * KC):
            is_attn = ci < KC
            c = ci if is_attn else ci - KC
            ring = t.wring_a if is_attn else t.wring_f
            g = c // GRP
            waits = []
            if is_attn:
                waits.append((sem.wmask, P.val(sem.wmask) - (NG - 1 - g)))
            else:
                waits.append((sem.wdf[g], P.val(sem.wdf[g])))
            if ci == 0:
                if s == 0:
                    waits.append((sem.tevac, P.val(sem.tevac)))
                    waits.append((sem.gch, P.val(sem.gch)))
                    waits.extend(extra_wait)
                ev0 = P.val(sem.evac)
                for tt in range(TT):
                    w2 = list(waits)
                    if s > 0:
                        w2.append((sem.evac, ev0 - (TT - 1 - tt)))
                    P.op("tensor",
                         (lambda te, c=c, ring=ring, tt=tt: te.matmul(
                             t.pmain[tt][:, 0:NW],
                             t.xnT[:, c * 1024 + tt * 128:
                                   c * 1024 + (tt + 1) * 128],
                             ring[:, c * NW:(c + 1) * NW],
                             start=True, stop=False)),
                         waits=w2,
                         inc=(sem.mmc if tt == TT - 1 else None), inc_n=2)
                continue

            def _mms(te, c=c, ring=ring):
                last = None
                for tt in range(TT):
                    last = te.matmul(
                        t.pmain[tt][:, 0:NW],
                        t.xnT[:, c * 1024 + tt * 128: c * 1024 + (tt + 1) * 128],
                        ring[:, c * NW:(c + 1) * NW],
                        start=False, stop=False)
                return last
            P.op("tensor", _mms, waits=waits, inc=sem.mmc, inc_n=2)
        for tt in range(TT):
            def _fin(te, tt=tt, s=s, layer=layer):
                pm = t.pmain[tt][:, 0:NW]
                if layer == 0:
                    te.matmul(pm, t.hwT[:, tt * 128:(tt + 1) * 128],
                              t.bm2_bf[:, s * NW:(s + 1) * NW],
                              start=False, stop=False)
                else:
                    te.matmul(pm, t.zp_bf[:, tt * 128:(tt + 1) * 128],
                              t.bmat_bf[:, s * NW:(s + 1) * NW],
                              start=False, stop=False)
                te.matmul(pm, t.ones_f32[0:1, 0:128],
                          t.brows0[0:1, s * NW:(s + 1) * NW],
                          start=False, stop=False)
                return te.matmul(pm, t.zrow_bf[0:1, 0:128],
                                 t.zrow_bf[0:1, 0:NW],
                                 start=False, stop=True)
            P.op("tensor", _fin, waits=[], inc=sem.mmf)
            P.op("vector",
                 (lambda v, tt=tt, s=s: v.tensor_tensor(
                     out=t.ostage[s % 2][:, tt * NW:(tt + 1) * NW],
                     in0=t.pmain[tt][:, 0:NW],
                     in1=t.x0stage[s % 2][:, tt * NW:(tt + 1) * NW],
                     op=ALU.add)),
                 waits=[(sem.mmf, P.val(sem.mmf)), (sem.x0d, x0v)]
                 + ([(sem.ost, P.val(sem.ost))] if s >= 2 else []),
                 inc=sem.evac)
        P.op("sync",
             (lambda sy, s=s: sy.dma_start(
                 out=xout[:, s * NW:(s + 1) * NW]
                     .rearrange("(tt p) n -> p tt n", tt=TT),
                 in_=t.ostage[s % 2][:].rearrange("p (tt n) -> p tt n", tt=TT))),
             waits=[(sem.evac, P.val(sem.evac))], inc=sem.ost, inc_n=16)



def alloc_common(nc, es, t):
    sb = nc.sbuf_tensor
    t.xnT = es.enter_context(sb("xnT", [128, KC * 1024], BF16)).ap()
    t.wring_a = es.enter_context(sb("wring_a", [128, 8192], BF16)).ap()
    t.wring_f = es.enter_context(sb("wring_f", [128, 8192], BF16)).ap()
    xbig = [es.enter_context(sb(f"xbig{i}", [128, D], F32)).ap()
            for i in range(2)]
    t.xstage = xbig
    t.ostage = [xb[:, 0:TT * NW] for xb in xbig]
    t.x0stage = [xb[:, TT * NW:2 * TT * NW] for xb in xbig]
    mbig = es.enter_context(sb("mbig", [128, 2048], F32)).ap()
    t.mbig = mbig
    t.bmQ = es.enter_context(sb("bmQ", [128, 128], F32)).ap()
    t.Qbf = es.enter_context(sb("Qbf", [128, 128], BF16)).ap()
    t.bmP = es.enter_context(sb("bmP", [128, D], BF16)).ap()
    scratch16 = es.enter_context(sb("scratch16", [128, D], F32)).ap()
    t.E32 = scratch16.bitcast(BF16)[:, 0:D]
    t.brows0 = scratch16[0:1, :]
    t.xnbuf = [es.enter_context(sb(f"xnbuf{i}", [128, D], BF16)).ap()
               for i in range(2)]
    t.lnsc = [es.enter_context(sb(f"lnsc{i}", [128, 16], F32)).ap()
              for i in range(2)]
    t.ones_f32 = es.enter_context(sb("ones_f32", [1, 128], F32)).ap()
    t.ident_bf = es.enter_context(sb("ident_bf", [128, 128], BF16)).ap()
    t.ident_f32 = es.enter_context(sb("ident_f32", [128, 128], F32)).ap()
    t.gsm = [es.enter_context(sb(f"gsm{i}", [128, 64], F32)).ap()
             for i in range(2)]
    t.cst = es.enter_context(sb("cst", [128, 4], F32)).ap()
    t.zrow_bf = es.enter_context(sb("zrow_bf", [1, 256], BF16)).ap()
    t.ones_bf = es.enter_context(sb("ones_bf", [128, 1], BF16)).ap()
    t.mscat = es.enter_context(sb("mscat", [1, 1024], F32)).ap()
    t.bm2_bf = t.mbig.bitcast(BF16)[:, 0:D]
    t.bmat_bf = t.mbig.bitcast(BF16)[0:32, 0:D]
    t.ah8_bf = es.enter_context(sb("ah8_bf", [128, KC * 40], BF16)).ap()
    t.limt_f32 = es.enter_context(sb("limt_f32", [32, 8], F32)).ap()
    t.limt_bf = es.enter_context(sb("limt_bf", [32, 8], BF16)).ap()
    t.b8col = es.enter_context(sb("b8col", [8, 1], F32)).ap()


def alloc_psums(nc, es, t):
    ps = nc.psum_tensor
    banks = [es.enter_context(ps(f"bank{i}", [128, 512], F32)).ap()
             for i in range(8)]
    t.pmain = banks
    t.pgA, t.pgB, t.pgC, t.pgD = banks[4], banks[5], banks[6], banks[7]


class SemNames:
    names = ("consts", "mred", "mpe", "mevac", "sq", "xn", "tpe", "tevac",
             "gw", "gqw", "gch", "gld", "bld", "wmask", "mmc", "mmf", "x0d", "evac",
             "ost", "fin", "lnch", "sld", "bsum")
    arrays = {"mdma": 4, "xdma": 2, "wda": NG, "wdf": NG, "msc": 2}


def make_sems(nc, es):
    sem = NS()
    for n in SemNames.names:
        setattr(sem, n, es.enter_context(nc.semaphore(n)))
    for n, k in SemNames.arrays.items():
        setattr(sem, n, [es.enter_context(nc.semaphore(f"{n}{i}"))
                         for i in range(k)])
    return sem



def build_launch_a():
    nc = bass.Bass("TRN2", target_bir_lowering=False, debug=False)
    dt = nc.dram_tensor
    x0 = dt("x0", [TOK, D], F32, kind="ExternalInput").ap()
    wt_l0a = dt("wt_l0a", [D, D], F32, kind="ExternalInput").ap()
    wt_l0f = dt("wt_l0f", [D, D], F32, kind="ExternalInput").ap()
    maskt0 = dt("maskt0", [1024, 16384], I32, kind="ExternalInput").ap()
    wqk = dt("wqk", [D, 512], F32, kind="ExternalInput").ap()
    a2t = dt("a2t", [D, 128], F32, kind="ExternalInput").ap()
    bm2 = dt("bm2", [128, D], F32, kind="ExternalInput").ap()
    b_l0a = dt("b_l0a", [1, D], F32, kind="ExternalInput").ap()
    b_l0f = dt("b_l0f", [1, D], F32, kind="ExternalInput").ap()
    ah8 = dt("ah8", [D, 40], F32, kind="ExternalInput").ap()
    b_l1f = dt("b_l1f", [1, D], F32, kind="ExternalInput").ap()
    limt = dt("limt", [32, 8], F32, kind="ExternalInput").ap()

    x1 = dt("x1", [TOK, D], F32, kind="ExternalOutput").ap()
    locmax = dt("locmax", [2, 1], F32, kind="ExternalOutput").ap()
    bmq_d = dt("bmq_d", [128, 128], F32).ap()

    es = contextlib.ExitStack()
    t = NS()
    alloc_common(nc, es, t)
    t.bmq_d = bmq_d
    sb = nc.sbuf_tensor
    t.a2t_bf = t.mbig.bitcast(BF16)[:, 0:D]
    t.brows1 = es.enter_context(sb("brows1", [1, D], F32)).ap()
    t.gq = es.enter_context(sb("gq", [128, 256], F32)).ap()
    t.gqk = es.enter_context(sb("gqk", [128, 256], F32)).ap()
    t.hw_tok = es.enter_context(sb("hw_tok", [128, 128], BF16)).ap()
    t.hwT = es.enter_context(sb("hwT", [128, 1024], BF16)).ap()
    t.locmax = es.enter_context(sb("locmax_sb", [2, 1], F32)).ap()
    bmPf = t.bmP.bitcast(F32)
    t.zh_tok = bmPf[:, 0:40]
    t.zTa_bf = t.bmP[0:32, 2048:2560]
    t.h8Ta = bmPf[0:8, 64:576]
    t.dsbuf = bmPf[:, 1024:1056]
    t.hsbuf = bmPf[:, 1056:1088]
    t.max2x128 = bmPf[0:2, 1100:1228]
    alloc_psums(nc, es, t)
    sem = make_sems(nc, es)
    block = es.enter_context(nc.Block())

    with es:
        P = Prog()
        emit_consts(P, t, sem)
        P.op("gpsimd",
             lambda g: g.dma_start(
                 out=t.ah8_bf[:].rearrange("p (c n) -> p c n", c=KC),
                 in_=ah8[:, :].rearrange("(c p) n -> p c n", c=KC)),
             inc=sem.gqw, inc_n=16)
        P.op("sync", lambda sy: sy.dma_start(out=t.limt_f32[:], in_=limt[:, :]),
             inc=sem.sld, inc_n=16)
        P.op("sync",
             lambda sy: sy.dma_start(
                 out=t.b8col[:], in_=b_l1f[0:1, 0:8]
                     .rearrange("x (p y) -> p (x y)", p=8)),
             inc=sem.sld, inc_n=16)
        P.op("gpsimd",
             lambda g: g.dma_start(
                 out=t.wring_a[:].rearrange("p (c n) -> p c n", c=16),
                 in_=wqk[0:2048, :].rearrange("(c p) n -> p c n", c=16)),
             inc=sem.gqw, inc_n=16)
        P.op("gpsimd",
             lambda g: g.dma_start(
                 out=t.wring_f[:].rearrange("p (c n) -> p c n", c=16),
                 in_=wqk[2048:4096, :].rearrange("(c p) n -> p c n", c=16)),
             inc=sem.gqw, inc_n=16)
        P.chain("scalar", lambda s: s.copy(t.limt_bf[:], t.limt_f32[:]),
                sem.gch, waits=[(sem.sld, P.val(sem.sld))])
        for tt in range(TT):
            emit_mask2_group(P, t, sem, maskt0, 2 * tt)
            emit_mask2_group(P, t, sem, maskt0, 2 * tt + 1)
            emit_ln_tile(
                P, t, sem, x0[tt * 128:(tt + 1) * 128, :], tt % 2,
                (lambda c, tt=tt: t.xnT[:, c * 1024 + tt * 128:
                                        c * 1024 + (tt + 1) * 128]))
            emit_qk_tile(P, t, sem, tt)
        emit_bmp(P, t, sem)
        ring_free = P.val(sem.gch)
        emit_hw_phase(P, t, sem, a2t, bm2)
        P.op("sync", lambda sy: sy.dma_start(out=t.brows0[0:1, :], in_=b_l0a[:, :]),
             waits=[(sem.mpe, P.val(sem.mpe))], inc=sem.bld, inc_n=16)
        P.op("sync", lambda sy: sy.dma_start(out=t.brows1[0:1, :], in_=b_l0f[:, :]),
             inc=sem.bld, inc_n=16)
        P.op("vector",
             lambda v: v.tensor_tensor(out=t.brows0[0:1, :], in0=t.brows0[0:1, :],
                                       in1=t.brows1[0:1, :], op=ALU.add),
             waits=[(sem.bld, P.val(sem.bld))], inc=sem.bsum)
        emit_main(P, t, sem, wt_l0a, wt_l0f, x0, x1, layer=0,
                  extra_wait=[(sem.bld, P.val(sem.bld)), (sem.bsum, 1),
                              (sem.gw, P.val(sem.gw))],
                  ring_free_wait=[(sem.gch, ring_free)],
                  x0_first_wait=[(sem.xn, P.val(sem.xn))])
        P.op("sync", None, waits=[(sem.ost, P.val(sem.ost))], inc=sem.fin)
        finv = P.val(sem.fin)
        for e in ("scalar", "vector", "tensor", "gpsimd"):
            P.op(e, None, waits=[(sem.fin, finv)])
        x1odd = x1[1::2, :]

        def slot_a(c, at):
            if c < 16:
                return t.wring_a[:, c * 512 + at * 128: c * 512 + (at + 1) * 128]
            return t.wring_f[:, (c - 16) * 512 + at * 128:
                             (c - 16) * 512 + (at + 1) * 128]
        t.zsrc = slot_a
        for at in range(4):
            emit_ln_tile(P, t, sem, x1odd[at * 128:(at + 1) * 128, :], at % 2,
                         (lambda c, at=at: slot_a(c, at)),
                         first_waits=[(sem.fin, finv)])
            emit_zpre_tile(P, t, sem, at, t.zTa_bf, None, t.h8Ta)
        emit_anchor_gate(P, t, sem, t.zTa_bf, t.h8Ta, t.limt_bf[:])
        emit_locmax(P, t, sem)
        P.op("sync", lambda sy: sy.dma_start(out=locmax[:, :], in_=t.locmax[:]),
             waits=[(sem.gch, P.val(sem.gch))], inc=sem.ost, inc_n=16)
        P.op("sync", None, waits=[(sem.ost, P.val(sem.ost))])

        for name in ENGS:
            getattr(block, name)(lambda eng, name=name: P.replay(name, eng))
    return nc


def build_launch_b():
    nc = bass.Bass("TRN2", target_bir_lowering=False, debug=False)
    dt = nc.dram_tensor
    x1 = dt("x1", [TOK, D], F32, kind="ExternalInput").ap()
    gmax8 = dt("gmax8", [8, 2], F32, kind="ExternalInput").ap()
    wt_l1a = dt("wt_l1a", [D, D], F32, kind="ExternalInput").ap()
    wt_l1f = dt("wt_l1f", [D, D], F32, kind="ExternalInput").ap()
    maskt1 = dt("maskt1", [1024, 16384], I32, kind="ExternalInput").ap()
    ah8 = dt("ah8", [D, 40], F32, kind="ExternalInput").ap()
    limt = dt("limt", [32, 8], F32, kind="ExternalInput").ap()
    lim = dt("lim", [8, 32], F32, kind="ExternalInput").ap()
    bmat = dt("bmat", [32, D], F32, kind="ExternalInput").ap()
    b_l1f = dt("b_l1f", [1, D], F32, kind="ExternalInput").ap()

    y = dt("y", [TOK, D], F32, kind="ExternalOutput").ap()
    gsc_dram = dt("gsc_scratch", [2, 1], F32).ap()
    bmq_d = dt("bmq_d", [128, 128], F32).ap()

    es = contextlib.ExitStack()
    t = NS()
    alloc_common(nc, es, t)
    t.bmq_d = bmq_d
    sb = nc.sbuf_tensor
    t.zp_bf = es.enter_context(sb("zp_bf", [32, 1024], BF16)).ap()
    t.zT_f32 = es.enter_context(sb("zT_f32", [32, 1024], F32)).ap()
    t.zT_bf = es.enter_context(sb("zT_bf", [32, 1024], BF16)).ap()
    t.h8T = es.enter_context(sb("h8T", [8, 1024], F32)).ap()
    t.zh_tok = es.enter_context(sb("zh_tok", [128, 40], F32)).ap()
    t.dsbuf = es.enter_context(sb("dsbuf", [128, 32], F32)).ap()
    t.hsbuf = es.enter_context(sb("hsbuf", [128, 32], F32)).ap()
    xb0, xb1 = t.xstage[0], t.xstage[1]
    t.wfullT = xb0.bitcast(BF16)[0:8, 5120:6144]
    t.max2x128 = xb1[0:2, 64:192]
    t.gmax8 = xb1[0:8, 192:194]
    t.gsc2 = xb1[0:2, 200:202]
    t.gscH = xb1[:, 208:209]
    t.gscD = xb1[:, 209:210]
    t.lim_f32 = xb1[0:8, 216:248]
    t.lim_bf = xb1.bitcast(BF16)[0:8, 496:528]
    alloc_psums(nc, es, t)
    sem = make_sems(nc, es)
    block = es.enter_context(nc.Block())

    with es:
        P = Prog()
        emit_consts(P, t, sem)
        P.op("gpsimd",
             lambda g: g.dma_start(
                 out=t.ah8_bf[:].rearrange("p (c n) -> p c n", c=KC),
                 in_=ah8[:, :].rearrange("(c p) n -> p c n", c=KC)),
             inc=sem.gqw, inc_n=16)
        P.op("sync", lambda sy: sy.dma_start(out=t.limt_f32[:], in_=limt[:, :]),
             inc=sem.sld, inc_n=16)
        P.op("sync",
             lambda sy: sy.dma_start(
                 out=t.b8col[:], in_=b_l1f[0:1, 0:8]
                     .rearrange("x (p y) -> p (x y)", p=8)),
             inc=sem.sld, inc_n=16)
        P.chain("scalar", lambda s: s.copy(t.limt_bf[:], t.limt_f32[:]),
                sem.gch, waits=[(sem.sld, P.val(sem.sld))])
        t.zsrc = (lambda c, tt: t.xnT[:, c * 1024 + tt * 128:
                                      c * 1024 + (tt + 1) * 128])
        for tt in range(TT):
            emit_mask2_group(P, t, sem, maskt1, 2 * tt)
            emit_mask2_group(P, t, sem, maskt1, 2 * tt + 1)
            emit_ln_tile(
                P, t, sem, x1[tt * 128:(tt + 1) * 128, :], tt % 2,
                (lambda c, tt=tt: t.xnT[:, c * 1024 + tt * 128:
                                        c * 1024 + (tt + 1) * 128]))
            emit_zpre_tile(P, t, sem, tt, t.zT_bf, t.zT_f32, t.h8T)
        emit_bmp(P, t, sem)
        emit_anchor_gate(P, t, sem, t.zT_bf[:, 1::2], t.h8T[:, 1::2],
                         t.limt_bf[:])
        emit_l1_post(P, t, sem, gmax8, gsc_dram, bmat, lim)
        P.op("sync", lambda sy: sy.dma_start(out=t.brows0[0:1, :], in_=b_l1f[:, :]),
             waits=[(sem.mpe, P.val(sem.mpe))], inc=sem.bld, inc_n=16)
        emit_main(P, t, sem, wt_l1a, wt_l1f, x1, y, layer=1,
                  extra_wait=[(sem.bld, P.val(sem.bld))],
                  x0_first_wait=[(sem.gch, P.val(sem.gch))])
        P.op("sync", None, waits=[(sem.ost, P.val(sem.ost))])

        for name in ENGS:
            getattr(block, name)(lambda eng, name=name: P.replay(name, eng))
    return nc



_NCA = None
_NCB = None


def kernel(L0_attn_data, L0_attn_mask, L0_attn_weight, L0_attn_bias,
           L0_ffn_W, L0_ffn_b, L0_ffn_Wq, L0_ffn_Wk, L0_ffn_A, L0_ffn_Bm,
           L1_attn_weight, L1_attn_mask, L1_ffn_W, L1_ffn_b, L1_ffn_A,
           L1_ffn_Bmat, L1_ffn_LiMEs):
    global _NCA, _NCB
    f32 = np.float32
    cat = np.ascontiguousarray

    x0 = np.asarray(L0_attn_data, f32).reshape(NCORES * TOK, D)
    wt_l0a = cat(np.asarray(L0_attn_weight, f32).T)
    wt_l0f = cat(np.asarray(L0_ffn_W, f32).T)
    wt_l1a = cat(np.asarray(L1_attn_weight, f32).T)
    wt_l1f = cat(np.asarray(L1_ffn_W, f32).T)

    def _maskT2(m):
        return cat(np.asarray(m, np.int32).reshape(128, 32, 128, 32)
                   .transpose(1, 3, 2, 0).reshape(1024, 16384))
    maskt0 = _maskT2(L0_attn_mask)
    maskt1 = _maskT2(L1_attn_mask)
    wqk = cat(np.concatenate([np.asarray(L0_ffn_Wq, f32).T,
                              np.asarray(L0_ffn_Wk, f32).T], axis=1))
    a2t = cat(np.asarray(L0_ffn_A, f32).reshape(128, D).T)
    bm2 = cat(np.asarray(L0_ffn_Bm, f32).transpose(0, 2, 1).reshape(128, D))
    b_l0a = cat(np.asarray(L0_attn_bias, f32)[None, :])
    b_l0f = cat(np.asarray(L0_ffn_b, f32)[None, :])
    ah8 = cat(np.concatenate([np.asarray(L1_ffn_A, f32),
                              np.asarray(L1_ffn_W, f32)[0:8, :].T], axis=1))
    b_l1f = cat(np.asarray(L1_ffn_b, f32)[None, :])
    lim = cat(np.asarray(L1_ffn_LiMEs, f32))
    limt = cat(lim.T)
    bmat = cat(np.asarray(L1_ffn_Bmat, f32))

    if _NCA is None:
        _NCA = build_launch_a()
    in_maps_a = []
    for c in range(NCORES):
        in_maps_a.append(dict(
            x0=cat(x0[c * TOK:(c + 1) * TOK]), wt_l0a=wt_l0a, wt_l0f=wt_l0f,
            maskt0=maskt0, wqk=wqk, a2t=a2t, bm2=bm2, b_l0a=b_l0a,
            b_l0f=b_l0f, ah8=ah8, b_l1f=b_l1f, limt=limt))
    res_a = run_bass_kernel_spmd(_NCA, in_maps_a, core_ids=list(range(NCORES)))

    gmax8 = np.concatenate([r["locmax"].reshape(1, 2) for r in res_a.results],
                           axis=0)

    if _NCB is None:
        _NCB = build_launch_b()
    in_maps_b = []
    for c in range(NCORES):
        in_maps_b.append(dict(
            x1=res_a.results[c]["x1"], gmax8=gmax8, wt_l1a=wt_l1a,
            wt_l1f=wt_l1f, maskt1=maskt1, ah8=ah8, limt=limt, lim=lim,
            bmat=bmat, b_l1f=b_l1f))
    res_b = run_bass_kernel_spmd(_NCB, in_maps_b, core_ids=list(range(NCORES)))

    y = np.concatenate([r["y"] for r in res_b.results], axis=0)
    return y.reshape(2, NCORES * TOK // 2, D)


# version 12
# speedup vs baseline: 1.1763x; 1.1763x over previous
import contextlib
import numpy as np
import ml_dtypes
import concourse.bass as bass
import concourse.mybir as mybir
from concourse.bass_utils import run_bass_kernel_spmd

F32 = mybir.dt.float32
BF16 = mybir.dt.bfloat16
I32 = mybir.dt.int32
AF = mybir.ActivationFunctionType
ALU = mybir.AluOpType
AX = mybir.AxisListType

NCORES = 8
TOK = 1024
D = 4096
KC = 32
TT = 8
NSUP = 16
NW = 256
GRP = 8
NG = KC // GRP
INV_SQRT_DK = 0.17677669529663687
LN_EPS = 1e-5

ENGS = ("sync", "gpsimd", "scalar", "vector", "tensor")


class Prog:

    def __init__(self):
        self.ops = {e: [] for e in ENGS}
        self.cnt = {}

    def val(self, sem):
        return self.cnt.get(sem, 0)

    def op(self, eng, build, waits=(), inc=None, inc_n=1):
        tgt = None
        if inc is not None:
            self.cnt[inc] = self.cnt.get(inc, 0) + inc_n
            tgt = self.cnt[inc]
        self.ops[eng].append((tuple(waits), build, inc, inc_n))
        return tgt

    def chain(self, eng, build, sem, inc_n=1, waits=()):
        return self.op(eng, build, waits=[(sem, self.val(sem))] + list(waits),
                       inc=sem, inc_n=inc_n)

    def replay(self, eng_name, eng):
        for waits, build, inc, inc_n in self.ops[eng_name]:
            for sem, v in waits:
                if v > 0:
                    eng.wait_ge(sem, v)
            inst = build(eng) if build is not None else None
            if inc is not None:
                if inst is None:
                    inst = eng.nop()
                inst.then_inc(inc, inc_n)


class NS:
    pass



def emit_consts(P, t, sem):
    C = sem.consts

    def _ident_ops(ap, n):
        yield lambda g: g.memset(ap, 0.0)
        yield lambda g: g.affine_select(out=ap, in_=ap,
                                        compare_op=ALU.not_equal, fill=1.0,
                                        base=0, pattern=[[-1, n]],
                                        channel_multiplier=1)

    ops = []
    ops += list(_ident_ops(t.ident_bf[:], 128))
    ops += list(_ident_ops(t.ident_f32[:], 128))
    ops.append(lambda g: g.memset(t.ones_f32[:], 1.0))
    ops.append(lambda g: g.memset(t.zrow_bf[:], 0.0))
    ops.append(lambda g: g.memset(t.ones_bf[:], 1.0))
    ops.append(lambda g: g.memset(t.cst[:, 0:1], INV_SQRT_DK))
    ops.append(lambda g: g.memset(t.cst[:, 1:2], 2.0))
    ops.append(lambda g: g.memset(t.cst[:, 2:3], 0.5))
    ops.append(lambda g: g.memset(t.E32[:], 1.0))
    ops.append(lambda g: g.affine_select(out=t.E32[:], in_=t.E32[:],
                                         compare_op=ALU.is_ge, fill=0.0,
                                         base=0, pattern=[[1, D]],
                                         channel_multiplier=-32))
    ops.append(lambda g: g.affine_select(out=t.E32[:], in_=t.E32[:],
                                         compare_op=ALU.is_ge, fill=0.0,
                                         base=31, pattern=[[-1, D]],
                                         channel_multiplier=32))
    for f in ops:
        P.chain("gpsimd", f, C)


def emit_mask2_group(P, t, sem, maskT2, bt):
    mt = [t.mbig.bitcast(BF16)[:, i * 1024:(i + 1) * 1024] for i in range(4)]
    strip_banks = [t.pgA, t.pgB, t.pgC, t.pgD]
    for qc in range(8):
        i = bt * 8 + qc
        par = i % 4
        waits = []
        if i >= 4:
            waits.append((sem.mpe, P.val(sem.mpe) - 3))
        P.op("gpsimd",
             (lambda g, bt=bt, qc=qc, par=par: g.dma_start(
                 out=mt[par][:],
                 in_=maskT2[qc * 128:(qc + 1) * 128,
                            bt * 1024:(bt + 1) * 1024])),
             waits=waits, inc=sem.mdma[par], inc_n=16)
        dmav = P.val(sem.mdma[par])

        def _mm(te, bt=bt, qc=qc, par=par):
            last = None
            for j in range(2):
                last = te.matmul(
                    strip_banks[(2 * bt + j) % 4][0:1, 0:512],
                    t.ones_bf[:, 0:1],
                    mt[par][:, j * 512:(j + 1) * 512],
                    start=(qc == 0), stop=(qc == 7))
            return last
        mmw = [(sem.mdma[par], dmav)]
        if qc == 0:
            mmw.append((sem.consts, P.val(sem.consts)))
            if bt >= 2:
                mmw.append((sem.mevac, P.val(sem.mevac)))
        P.op("tensor", _mm, waits=mmw, inc=sem.mpe)
    for j in range(2):
        sidx = 2 * bt + j
        par = sidx % 2
        waits = [(sem.mpe, P.val(sem.mpe))]
        if sidx >= 2:
            waits.append((sem.msc[par], P.val(sem.msc[par])))
        P.op("scalar",
             (lambda s, bt=bt, j=j, par=par: s.copy(
                 t.mscat[0:1, par * 512:(par + 1) * 512],
                 strip_banks[(2 * bt + j) % 4][0:1, 0:512])),
             waits=waits, inc=sem.mevac)
        P.op("gpsimd",
             (lambda g, sidx=sidx, par=par: g.dma_start(
                 out=t.bmq_d[4 * sidx:4 * (sidx + 1), :]
                     .rearrange("(x a) n -> x a n", x=1),
                 in_=t.mscat[0:1, par * 512:(par + 1) * 512]
                     .rearrange("x (a n) -> x a n", a=4))),
             waits=[(sem.mevac, P.val(sem.mevac))],
             inc=sem.msc[par], inc_n=16)


def emit_bmp(P, t, sem):
    P.op("gpsimd", lambda g: g.dma_start(out=t.bmQ[:], in_=t.bmq_d[:, :]),
         waits=[(sem.msc[0], P.val(sem.msc[0])),
                (sem.msc[1], P.val(sem.msc[1]))],
         inc=sem.msc[0], inc_n=16)
    P.op("vector",
         lambda v: v.tensor_scalar(out=t.Qbf[:], in0=t.bmQ[:], scalar1=1.0,
                                   scalar2=None, op0=ALU.is_ge),
         waits=[(sem.msc[0], P.val(sem.msc[0]))],
         inc=sem.mred)
    qready = P.val(sem.mred)
    for c in range(KC):
        waits = [(sem.mred, qready)]
        if c >= 2:
            waits.append((sem.mevac, P.val(sem.mevac) - 1))
        psq = t.pgA if c % 2 == 0 else t.pgC
        P.op("tensor",
             (lambda te, c=c, psq=psq: te.matmul(
                 psq[:, 0:128], t.E32[:, c * 128:(c + 1) * 128],
                 t.Qbf[:], start=True, stop=True)),
             waits=waits, inc=sem.mpe)
        P.op("scalar",
             (lambda s, c=c, psq=psq: s.copy(
                 t.bmP[:, c * 128:(c + 1) * 128], psq[:, 0:128])),
             waits=[(sem.mpe, P.val(sem.mpe))], inc=sem.mevac)


def emit_ln_tile(P, t, sem, xin_tile, par, xnt_slot, first_waits=()):
    L = sem.lnch
    ls = t.lnsc[par]
    waits = list(first_waits)
    if P.val(sem.xn) >= 2:
        waits.append((sem.xn, P.val(sem.xn) - 1))
    P.op("sync",
         (lambda sy, par=par, xin_tile=xin_tile: sy.dma_start(
             out=t.xstage[par][:], in_=xin_tile)),
         waits=waits, inc=sem.xdma[par], inc_n=16)
    ld = (sem.xdma[par], P.val(sem.xdma[par]))
    sq_waits = [ld]
    if P.val(sem.tevac) >= 2 * KC:
        sq_waits.append((sem.tevac, P.val(sem.tevac) - KC))
    P.op("scalar",
         (lambda s, par=par, ls=ls: s.activation(
             out=t.xnbuf[par][:], in_=t.xstage[par][:], func=AF.Square,
             accum_out=ls[:, 1:2])),
         waits=sq_waits, inc=sem.sq)
    P.op("vector",
         (lambda v, par=par, ls=ls: v.reduce_sum(
             ls[:, 0:1], t.xstage[par][:], axis=AX.X)),
         waits=[ld, (L, P.val(L))], inc=L)
    P.op("vector",
         (lambda v, ls=ls: v.tensor_scalar(
             out=ls[:, 2:4], in0=ls[:, 0:2], scalar1=1.0 / D,
             scalar2=None, op0=ALU.mult)),
         waits=[(L, P.val(L)), (sem.sq, P.val(sem.sq))], inc=L)
    P.chain("vector",
            (lambda v, ls=ls: v.tensor_scalar(
                out=ls[:, 4:5], in0=ls[:, 2:3], scalar1=ls[:, 2:3],
                scalar2=None, op0=ALU.mult)), L)
    P.chain("vector",
            (lambda v, ls=ls: v.tensor_scalar(
                out=ls[:, 5:6], in0=ls[:, 3:4], scalar1=ls[:, 4:5],
                scalar2=LN_EPS, op0=ALU.subtract, op1=ALU.add)), L)
    P.chain("scalar",
            (lambda s, ls=ls: s.activation(out=ls[:, 6:7], in_=ls[:, 5:6],
                                           func=AF.Sqrt)), L)
    P.chain("vector",
            (lambda v, ls=ls: v.reciprocal(ls[:, 10:11], ls[:, 6:7])), L)
    P.chain("vector",
            (lambda v, ls=ls: v.tensor_scalar(
                out=ls[:, 11:12], in0=ls[:, 2:3], scalar1=ls[:, 10:11],
                scalar2=-1.0, op0=ALU.mult, op1=ALU.mult)), L)
    P.op("scalar",
         (lambda s, par=par, ls=ls: s.activation(
             out=t.xnbuf[par][:], in_=t.xstage[par][:], func=AF.Identity,
             scale=ls[:, 10:11], bias=ls[:, 11:12])),
         waits=[(L, P.val(L))], inc=sem.xn)
    xnv = P.val(sem.xn)
    for c in range(KC):
        psl = t.pmain[c % 2][:, 0:128].bitcast(BF16)
        waits = [(sem.xn, xnv)]
        if P.val(sem.tevac) > 0:
            waits.append((sem.tevac, P.val(sem.tevac) - 1))
        if P.val(sem.tevac) < 2:
            waits.append((sem.consts, P.val(sem.consts)))
        P.op("tensor",
             (lambda te, c=c, par=par, psl=psl: te.transpose(
                 psl[:, 0:128],
                 t.xnbuf[par][:, c * 128:(c + 1) * 128], t.ident_bf[:])),
             waits=waits, inc=sem.tpe)
        P.op("scalar",
             (lambda s, psl=psl, slot=xnt_slot(c): s.copy(slot, psl[:, 0:128])),
             waits=[(sem.tpe, P.val(sem.tpe))], inc=sem.tevac)


def emit_recip_refine(P, t, sem, gs, src_col, dst_base):
    C = sem.gch
    P.chain("vector",
            (lambda v: v.reciprocal(gs[:, dst_base:dst_base + 1],
                                    gs[:, src_col:src_col + 1])), C)
    P.chain("vector",
            (lambda v: v.tensor_scalar(
                out=gs[:, dst_base + 1:dst_base + 2],
                in0=gs[:, src_col:src_col + 1],
                scalar1=gs[:, dst_base:dst_base + 1], scalar2=None,
                op0=ALU.mult)), C)
    P.chain("vector",
            (lambda v: v.tensor_scalar(
                out=gs[:, dst_base + 2:dst_base + 3],
                in0=gs[:, dst_base + 1:dst_base + 2],
                scalar1=-1.0, scalar2=2.0, op0=ALU.mult, op1=ALU.add)), C)
    P.chain("vector",
            (lambda v: v.tensor_scalar(
                out=gs[:, dst_base + 3:dst_base + 4],
                in0=gs[:, dst_base:dst_base + 1],
                scalar1=gs[:, dst_base + 2:dst_base + 3], scalar2=None,
                op0=ALU.mult)), C)


def emit_qk_tile(P, t, sem, tt):
    C = sem.gch
    gs = t.gsm[0]

    def _qk(te, tt=tt):
        last = None
        for c in range(KC):
            w = t.wring_a if c < 16 else t.wring_f
            cc = c % 16
            last = te.matmul(
                t.pmain[2][:, 0:512],
                t.xnT[:, c * 1024 + tt * 128: c * 1024 + (tt + 1) * 128],
                w[:, cc * 512:(cc + 1) * 512],
                start=(c == 0), stop=(c == KC - 1))
        return last
    P.chain("tensor", _qk, C,
            waits=[(sem.gqw, P.val(sem.gqw)), (sem.tevac, P.val(sem.tevac)),
                   (sem.consts, P.val(sem.consts))])
    P.chain("scalar",
            lambda s: s.activation(out=t.gq[:], in_=t.pmain[2][:, 0:256],
                                   func=AF.Identity, scale=t.cst[:, 0:1]), C)
    P.chain("vector",
            lambda v: v.tensor_tensor(out=t.gqk[:], in0=t.pmain[2][:, 256:512],
                                      in1=t.gq[:], op=ALU.mult), C)
    P.chain("vector",
            (lambda v, gs=gs: v.tensor_reduce(
                out=gs[:, 0:8][:, :, None],
                in_=t.gqk[:].rearrange("p (a b) -> p a b", a=8),
                axis=AX.X, op=ALU.add)), C)
    P.chain("vector",
            (lambda v, gs=gs: v.reduce_max(gs[:, 8:9], gs[:, 0:8],
                                           axis=AX.X, negate=True)), C)
    P.chain("scalar",
            (lambda s, gs=gs: s.activation(
                out=gs[:, 16:24], in_=gs[:, 0:8], func=AF.Exp,
                bias=gs[:, 8:9], scale=1.0)), C)
    P.chain("vector",
            (lambda v, gs=gs: v.reduce_sum(gs[:, 9:10], gs[:, 16:24],
                                           axis=AX.X)), C)
    P.chain("vector",
            (lambda v, gs=gs: v.reciprocal(gs[:, 13:14], gs[:, 9:10])), C)
    P.chain("vector",
            (lambda v, gs=gs, tt=tt: v.tensor_scalar(
                out=t.gsm[1][:, 8 * tt:8 * (tt + 1)], in0=gs[:, 16:24],
                scalar1=gs[:, 13:14], scalar2=None, op0=ALU.mult)), C)


def emit_hw_phase(P, t, sem, a2t_dram, bm2_dram):
    C = sem.gch
    P.op("gpsimd",
         lambda g: g.dma_start(
             out=t.a2t_bf[:, 0:D].rearrange("p (c n) -> p c n", c=KC),
             in_=a2t_dram[:, :].rearrange("(c p) n -> p c n", c=KC)),
         waits=[(sem.mpe, P.val(sem.mpe))], inc=sem.gw, inc_n=16)
    gwv = P.val(sem.gw)
    for tt in range(TT):
        def _h(te, tt=tt):
            last = None
            for c in range(KC):
                last = te.matmul(
                    t.pmain[2][:, 0:128],
                    t.xnT[:, c * 1024 + tt * 128: c * 1024 + (tt + 1) * 128],
                    t.a2t_bf[:, c * 128:(c + 1) * 128],
                    start=(c == 0), stop=(c == KC - 1))
            return last
        P.chain("tensor", _h, C, waits=[(sem.gw, gwv)])
        P.chain("vector",
                (lambda v, tt=tt: v.tensor_tensor(
                    out=t.hw_tok[:].rearrange("p (a b) -> p a b", a=8),
                    in0=t.pmain[2][:, 0:128].rearrange("p (a b) -> p a b", a=8),
                    in1=t.gsm[1][:, 8 * tt:8 * (tt + 1)][:, :, None]
                        .broadcast_to([128, 8, 16]),
                    op=ALU.mult)), C)
        P.chain("tensor",
                lambda te: te.transpose(
                    t.pmain[3][:, 0:128].bitcast(BF16)[:, 0:128],
                    t.hw_tok[:], t.ident_bf[:]), C)
        P.chain("scalar",
                (lambda s, tt=tt: s.copy(
                    t.hwT[:, tt * 128:(tt + 1) * 128],
                    t.pmain[3][:, 0:128].bitcast(BF16)[:, 0:128])), C)
    P.op("gpsimd",
         lambda g: g.dma_start(
             out=t.bm2_bf[:].rearrange("p (c n) -> p c n", c=1),
             in_=bm2_dram[:, :].rearrange("(c p) n -> p c n", c=1)),
         waits=[(sem.gch, P.val(sem.gch))], inc=sem.gw, inc_n=16)


def emit_zpre_tile(P, t, sem, tt, zT_bf, zT_f32, h8T):
    C = sem.gch

    def _zmm(te, tt=tt):
        last = None
        for c in range(KC):
            last = te.matmul(
                t.pmain[2][:, 0:40],
                t.zsrc(c, tt),
                t.ah8_bf[:, c * 40:(c + 1) * 40],
                start=(c == 0), stop=(c == KC - 1))
        return last
    P.chain("tensor", _zmm, C,
            waits=[(sem.gqw, P.val(sem.gqw)), (sem.tevac, P.val(sem.tevac)),
                   (sem.sld, P.val(sem.sld))])
    P.chain("scalar", lambda s: s.copy(t.zh_tok[:], t.pmain[2][:, 0:40]), C)
    P.chain("tensor",
            lambda te: te.transpose(t.pmain[3][0:32, 0:128], t.zh_tok[:, 0:32],
                                    t.ident_f32[:]), C)

    def _zev(s, tt=tt):
        last = s.copy(zT_bf[:, tt * 128:(tt + 1) * 128],
                      t.pmain[3][0:32, 0:128])
        if zT_f32 is not None:
            last = s.copy(zT_f32[:, tt * 128:(tt + 1) * 128],
                          t.pmain[3][0:32, 0:128])
        return last
    P.chain("scalar", _zev, C)
    P.chain("tensor",
            lambda te: te.transpose(t.pmain[3][0:8, 128:256],
                                    t.zh_tok[:, 32:40], t.ident_f32[:]), C)
    P.chain("scalar",
            (lambda s, tt=tt: s.activation(
                out=h8T[:, tt * 128:(tt + 1) * 128],
                in_=t.pmain[3][0:8, 128:256], func=AF.Identity,
                bias=t.b8col[:, 0:1], scale=1.0)), C)


def emit_anchor_gate(P, t, sem, zT_bf_anch, h8T_anch, limt_view):
    C = sem.gch
    for aj in range(4):
        P.chain("tensor",
                (lambda te, aj=aj: te.matmul(
                    t.pmain[2][:, 64 + aj * 8:64 + (aj + 1) * 8],
                    zT_bf_anch[:, aj * 128:(aj + 1) * 128],
                    limt_view, start=True, stop=True)), C)
        P.chain("tensor",
                (lambda te, aj=aj: te.transpose(
                    t.pmain[3][:, 256 + aj * 8:256 + (aj + 1) * 8],
                    h8T_anch[:, aj * 128:(aj + 1) * 128],
                    t.ident_f32[0:8, 0:8])), C)
    P.chain("scalar", lambda s: s.copy(t.dsbuf[:], t.pmain[2][:, 64:96]), C)
    P.chain("scalar", lambda s: s.copy(t.hsbuf[:], t.pmain[3][:, 256:288]), C)


def emit_locmax(P, t, sem):
    C = sem.gch
    P.chain("vector",
            lambda v: v.reduce_max(t.gsm[0][:, 0:1], t.hsbuf[:], axis=AX.X,
                                   apply_absolute_value=True), C)
    P.chain("vector",
            lambda v: v.reduce_max(t.gsm[0][:, 1:2], t.dsbuf[:], axis=AX.X,
                                   apply_absolute_value=True), C)
    P.chain("tensor",
            lambda te: te.transpose(t.pmain[3][0:2, 0:128], t.gsm[0][:, 0:2],
                                    t.ident_f32[:]), C)
    P.chain("scalar",
            lambda s: s.copy(t.max2x128[:], t.pmain[3][0:2, 0:128]), C)
    P.chain("vector",
            lambda v: v.reduce_max(t.locmax[:], t.max2x128[:], axis=AX.X), C)


def emit_l1_post(P, t, sem, gmax_dram, gsc_dram, bmat_dram, lim_dram):
    C = sem.gch
    g0 = t.gsm[0]
    P.op("sync", lambda sy: sy.dma_start(out=t.gmax8[:], in_=gmax_dram[:, :]),
         waits=[(sem.xn, P.val(sem.xn))], inc=sem.sld, inc_n=16)
    P.op("gpsimd",
         lambda g: g.dma_start(
             out=t.bmat_bf[:].rearrange("p (c n) -> p c n", c=1),
             in_=bmat_dram[:, :].rearrange("(c p) n -> p c n", c=1)),
         waits=[(sem.mpe, P.val(sem.mpe))], inc=sem.gld, inc_n=16)
    P.op("sync", lambda sy: sy.dma_start(out=t.lim_f32[:], in_=lim_dram[:, :]),
         waits=[(sem.xn, P.val(sem.xn))], inc=sem.sld, inc_n=16)
    P.op("vector", None,
         waits=[(sem.gld, P.val(sem.gld)), (sem.sld, P.val(sem.sld)),
                (sem.consts, P.val(sem.consts)), (sem.mevac, P.val(sem.mevac)),
                (sem.gch, P.val(sem.gch))],
         inc=C)
    P.chain("scalar", lambda s: s.activation(out=t.lim_bf[:], in_=t.lim_f32[:],
                                             func=AF.Identity,
                                             scale=t.cst[0:8, 2:3]), C)
    P.chain("tensor",
            lambda te: te.transpose(t.pgB[0:2, 0:8], t.gmax8[:],
                                    t.ident_f32[0:8, 0:8]), C)
    P.chain("scalar", lambda s: s.copy(t.max2x128[:, 0:8], t.pgB[0:2, 0:8]), C)
    P.chain("vector",
            lambda v: v.reduce_max(t.gsc2[:, 0:1], t.max2x128[:, 0:8],
                                   axis=AX.X), C)
    P.chain("vector",
            lambda v: v.tensor_scalar(out=t.gsc2[:, 1:2], in0=t.gsc2[:, 0:1],
                                      scalar1=1e-6, scalar2=None,
                                      op0=ALU.max), C)
    P.chain("sync",
            lambda sy: sy.dma_start(out=gsc_dram[:, :], in_=t.gsc2[:, 1:2]),
            C, inc_n=16)
    P.chain("sync",
            lambda sy: sy.dma_start(
                out=t.gscH[:], in_=gsc_dram[0:1, 0:1].to_broadcast((128, 1))),
            C, inc_n=16)
    P.chain("sync",
            lambda sy: sy.dma_start(
                out=t.gscD[:], in_=gsc_dram[1:2, 0:1].to_broadcast((128, 1))),
            C, inc_n=16)
    P.chain("vector", lambda v: v.tensor_copy(g0[:, 0:1], t.gscH[:]), C)
    emit_recip_refine(P, t, sem, g0, 0, 1)
    P.chain("vector", lambda v: v.tensor_copy(g0[:, 8:9], t.gscD[:]), C)
    emit_recip_refine(P, t, sem, g0, 8, 9)
    P.chain("vector",
            lambda v: v.tensor_scalar(out=g0[:, 16:17], in0=g0[:, 4:5],
                                      scalar1=0.5, scalar2=None,
                                      op0=ALU.mult), C)
    P.chain("vector",
            lambda v: v.tensor_scalar(out=g0[:, 17:18], in0=g0[:, 12:13],
                                      scalar1=0.5, scalar2=None,
                                      op0=ALU.mult), C)

    for aj in range(4):
        gs = t.gsm[1]
        sl = slice(aj * 8, (aj + 1) * 8)
        P.chain("vector",
                (lambda v, gs=gs, sl=sl: v.tensor_scalar(
                    out=gs[:, 16:24], in0=t.dsbuf[:, sl],
                    scalar1=g0[:, 17:18], scalar2=None, op0=ALU.mult)), C)
        P.chain("vector",
                (lambda v, gs=gs, sl=sl: v.scalar_tensor_tensor(
                    out=gs[:, 16:24], in0=t.hsbuf[:, sl], scalar=g0[:, 16:17],
                    in1=gs[:, 16:24], op0=ALU.mult, op1=ALU.add)), C)
        P.chain("vector",
                (lambda v, gs=gs: v.reduce_max(gs[:, 24:25], gs[:, 16:24],
                                               axis=AX.X, negate=True)), C)
        P.chain("scalar",
                (lambda s, gs=gs: s.activation(
                    out=gs[:, 32:40], in_=gs[:, 16:24], func=AF.Exp,
                    bias=gs[:, 24:25], scale=1.0)), C)
        P.chain("vector",
                (lambda v, gs=gs: v.reduce_sum(gs[:, 25:26], gs[:, 32:40],
                                               axis=AX.X)), C)
        emit_recip_refine(P, t, sem, gs, 25, 26)
        P.chain("vector",
                (lambda v, gs=gs: v.tensor_scalar(
                    out=gs[:, 40:48], in0=gs[:, 32:40],
                    scalar1=gs[:, 29:30], scalar2=None, op0=ALU.mult)), C)
        P.chain("vector",
                (lambda v, gs=gs: v.reduce_max(gs[:, 48:49], gs[:, 40:48],
                                               axis=AX.X)), C)
        P.chain("vector",
                (lambda v, gs=gs: v.tensor_scalar(
                    out=gs[:, 56:64], in0=gs[:, 40:48],
                    scalar1=gs[:, 48:49], scalar2=None, op0=ALU.is_ge)), C)
        P.chain("vector",
                (lambda v, gs=gs: v.reduce_sum(gs[:, 49:50], gs[:, 56:64],
                                               axis=AX.X)), C)
        P.chain("vector",
                (lambda v, gs=gs: v.scalar_tensor_tensor(
                    out=gs[:, 56:64], in0=gs[:, 56:64], scalar=-1e30,
                    in1=gs[:, 40:48], op0=ALU.mult, op1=ALU.add)), C)
        P.chain("vector",
                (lambda v, gs=gs: v.reduce_max(gs[:, 50:51], gs[:, 56:64],
                                               axis=AX.X)), C)
        P.chain("vector",
                (lambda v, gs=gs: v.tensor_scalar(
                    out=gs[:, 51:52], in0=gs[:, 49:50], scalar1=2.0,
                    scalar2=None, op0=ALU.is_ge)), C)
        P.chain("vector",
                (lambda v, gs=gs: v.tensor_scalar(
                    out=gs[:, 52:53], in0=gs[:, 48:49],
                    scalar1=gs[:, 50:51], scalar2=None, op0=ALU.subtract)), C)
        P.chain("vector",
                (lambda v, gs=gs: v.tensor_scalar(
                    out=gs[:, 53:54], in0=gs[:, 52:53],
                    scalar1=gs[:, 51:52], scalar2=None, op0=ALU.mult)), C)
        P.chain("vector",
                (lambda v, gs=gs: v.tensor_scalar(
                    out=gs[:, 54:55], in0=gs[:, 53:54],
                    scalar1=gs[:, 50:51], scalar2=None, op0=ALU.add)), C)
        P.chain("vector",
                (lambda v, gs=gs: v.tensor_scalar(
                    out=gs[:, 55:56], in0=gs[:, 54:55], scalar1=-2.0,
                    scalar2=None, op0=ALU.mult)), C)
        P.chain("scalar",
                (lambda s, gs=gs: s.activation(
                    out=gs[:, 56:64], in_=gs[:, 40:48], func=AF.Sigmoid,
                    bias=gs[:, 55:56], scale=t.cst[:, 1:2])), C)
        P.chain("vector",
                (lambda v, gs=gs: v.tensor_tensor(
                    out=gs[:, 40:48], in0=gs[:, 40:48], in1=gs[:, 56:64],
                    op=ALU.mult)), C)
        P.chain("vector",
                (lambda v, gs=gs: v.reduce_sum(gs[:, 25:26], gs[:, 40:48],
                                               axis=AX.X)), C)
        P.chain("vector",
                (lambda v, gs=gs: v.tensor_scalar(
                    out=gs[:, 26:27], in0=gs[:, 25:26], scalar1=1e-9,
                    scalar2=None, op0=ALU.add)), C)
        emit_recip_refine(P, t, sem, gs, 26, 27)
        P.chain("vector",
                (lambda v, gs=gs: v.tensor_scalar(
                    out=gs[:, 40:48], in0=gs[:, 40:48],
                    scalar1=gs[:, 30:31], scalar2=None, op0=ALU.mult)), C)
        P.chain("tensor",
                (lambda te, gs=gs: te.transpose(
                    t.pgB[0:8, 0:128], gs[:, 40:48], t.ident_f32[:])), C)
        P.chain("scalar",
                (lambda s, aj=aj: s.copy(
                    t.wfullT[:, aj * 256:(aj + 1) * 256]
                        .rearrange("p (a two) -> p a two", two=2),
                    t.pgB[0:8, 0:128][:, :, None].broadcast_to([8, 128, 2]))),
                C)
    P.chain("tensor",
            lambda te: te.matmul(t.pgA[0:32, 0:512], t.lim_bf[:],
                                 t.wfullT[:, 0:512], start=True, stop=True), C)
    P.chain("tensor",
            lambda te: te.matmul(t.pgC[0:32, 0:512], t.lim_bf[:],
                                 t.wfullT[:, 512:1024], start=True, stop=True),
            C)
    P.chain("vector",
            lambda v: v.tensor_tensor(out=t.zp_bf[:, 0:512],
                                      in0=t.zT_f32[:, 0:512],
                                      in1=t.pgA[0:32, 0:512], op=ALU.mult), C)
    P.chain("vector",
            lambda v: v.tensor_tensor(out=t.zp_bf[:, 512:1024],
                                      in0=t.zT_f32[:, 512:1024],
                                      in1=t.pgC[0:32, 0:512], op=ALU.mult), C)


def emit_main(P, t, sem, wa_dram, wf_dram, xin, xout, layer, extra_wait=(),
              ring_free_wait=(), x0_first_wait=()):
    for s in range(NSUP):
        for g in range(NG):
            for (ring, wd, wsems, cons_base) in (
                    (t.wring_a, wa_dram, sem.wda, 0),
                    (t.wring_f, wf_dram, sem.wdf, KC)):
                wsem = wsems[g]
                waits = []
                if s == 0:
                    waits.extend(ring_free_wait)
                if s > 0:
                    need = (s - 1) * 4 * KC + 2 * (cons_base + (g + 1) * GRP)
                    waits.append((sem.mmc, need))
                P.op("gpsimd",
                     (lambda gg, ring=ring, wd=wd, s=s, g=g: gg.dma_start(
                         out=ring[:, g * GRP * NW:(g + 1) * GRP * NW]
                             .rearrange("p (c n) -> p c n", c=GRP),
                         in_=wd[g * GRP * 128:(g + 1) * GRP * 128,
                                s * NW:(s + 1) * NW]
                             .rearrange("(c p) n -> p c n", c=GRP))),
                     waits=waits, inc=wsem, inc_n=16)
        for g in range(NG):
            P.op("vector",
                 (lambda v, s=s, g=g: v.tensor_tensor(
                     out=t.wring_a[:, g * GRP * NW:(g + 1) * GRP * NW]
                         .rearrange("p (c a b) -> p c a b", c=GRP, a=8),
                     in0=t.wring_a[:, g * GRP * NW:(g + 1) * GRP * NW]
                         .rearrange("p (c a b) -> p c a b", c=GRP, a=8),
                     in1=t.bmP[:].rearrange("p (c sp a) -> p c sp a", c=KC, sp=16)
                         [:, g * GRP:(g + 1) * GRP, s, :][:, :, :, None]
                         .broadcast_to([128, GRP, 8, 32]),
                     op=ALU.mult)),
                 waits=[(sem.wda[g], P.val(sem.wda[g])),
                        (sem.mevac, P.val(sem.mevac))],
                 inc=sem.wmask)
        P.op("sync",
             (lambda sy, s=s: sy.dma_start(
                 out=t.x0stage[s % 2][:].rearrange("p (tt n) -> p tt n", tt=TT),
                 in_=xin[:, s * NW:(s + 1) * NW]
                     .rearrange("(tt p) n -> p tt n", tt=TT))),
             waits=([(sem.evac, P.val(sem.evac) - TT)] if s >= 2
                    else list(x0_first_wait)),
             inc=sem.x0d, inc_n=16)
        x0v = P.val(sem.x0d)

        for ci in range(2 * KC):
            is_attn = ci < KC
            c = ci if is_attn else ci - KC
            ring = t.wring_a if is_attn else t.wring_f
            g = c // GRP
            waits = []
            if is_attn:
                waits.append((sem.wmask, P.val(sem.wmask) - (NG - 1 - g)))
            else:
                waits.append((sem.wdf[g], P.val(sem.wdf[g])))
            if ci == 0:
                if s == 0:
                    waits.append((sem.tevac, P.val(sem.tevac)))
                    waits.append((sem.gch, P.val(sem.gch)))
                    waits.extend(extra_wait)
                ev0 = P.val(sem.evac)
                for tt in range(TT):
                    w2 = list(waits)
                    if s > 0:
                        w2.append((sem.evac, ev0 - (TT - 1 - tt)))
                    P.op("tensor",
                         (lambda te, c=c, ring=ring, tt=tt: te.matmul(
                             t.pmain[tt][:, 0:NW],
                             t.xnT[:, c * 1024 + tt * 128:
                                   c * 1024 + (tt + 1) * 128],
                             ring[:, c * NW:(c + 1) * NW],
                             start=True, stop=False)),
                         waits=w2,
                         inc=(sem.mmc if tt == TT - 1 else None), inc_n=2)
                continue

            def _mms(te, c=c, ring=ring):
                last = None
                for tt in range(TT):
                    last = te.matmul(
                        t.pmain[tt][:, 0:NW],
                        t.xnT[:, c * 1024 + tt * 128: c * 1024 + (tt + 1) * 128],
                        ring[:, c * NW:(c + 1) * NW],
                        start=False, stop=False)
                return last
            P.op("tensor", _mms, waits=waits, inc=sem.mmc, inc_n=2)
        for tt in range(TT):
            def _fin(te, tt=tt, s=s, layer=layer):
                pm = t.pmain[tt][:, 0:NW]
                if layer == 0:
                    te.matmul(pm, t.hwT[:, tt * 128:(tt + 1) * 128],
                              t.bm2_bf[:, s * NW:(s + 1) * NW],
                              start=False, stop=False)
                else:
                    te.matmul(pm, t.zp_bf[:, tt * 128:(tt + 1) * 128],
                              t.bmat_bf[:, s * NW:(s + 1) * NW],
                              start=False, stop=False)
                te.matmul(pm, t.ones_f32[0:1, 0:128],
                          t.brows0[0:1, s * NW:(s + 1) * NW],
                          start=False, stop=False)
                return te.matmul(pm, t.zrow_bf[0:1, 0:128],
                                 t.zrow_bf[0:1, 0:NW],
                                 start=False, stop=True)
            P.op("tensor", _fin, waits=[], inc=sem.mmf)
            P.op("vector",
                 (lambda v, tt=tt, s=s: v.tensor_tensor(
                     out=t.ostage[s % 2][:, tt * NW:(tt + 1) * NW],
                     in0=t.pmain[tt][:, 0:NW],
                     in1=t.x0stage[s % 2][:, tt * NW:(tt + 1) * NW],
                     op=ALU.add)),
                 waits=[(sem.mmf, P.val(sem.mmf)), (sem.x0d, x0v)]
                 + ([(sem.ost, P.val(sem.ost))] if s >= 2 else []),
                 inc=sem.evac)
        P.op("sync",
             (lambda sy, s=s: sy.dma_start(
                 out=xout[:, s * NW:(s + 1) * NW]
                     .rearrange("(tt p) n -> p tt n", tt=TT),
                 in_=t.ostage[s % 2][:].rearrange("p (tt n) -> p tt n", tt=TT))),
             waits=[(sem.evac, P.val(sem.evac))], inc=sem.ost, inc_n=16)



def alloc_common(nc, es, t):
    sb = nc.sbuf_tensor
    t.xnT = es.enter_context(sb("xnT", [128, KC * 1024], BF16)).ap()
    t.wring_a = es.enter_context(sb("wring_a", [128, 8192], BF16)).ap()
    t.wring_f = es.enter_context(sb("wring_f", [128, 8192], BF16)).ap()
    xbig = [es.enter_context(sb(f"xbig{i}", [128, D], F32)).ap()
            for i in range(2)]
    t.xstage = xbig
    t.ostage = [xb[:, 0:TT * NW] for xb in xbig]
    t.x0stage = [xb[:, TT * NW:2 * TT * NW] for xb in xbig]
    mbig = es.enter_context(sb("mbig", [128, 2048], F32)).ap()
    t.mbig = mbig
    t.bmQ = es.enter_context(sb("bmQ", [128, 128], F32)).ap()
    t.Qbf = es.enter_context(sb("Qbf", [128, 128], BF16)).ap()
    t.bmP = es.enter_context(sb("bmP", [128, D], BF16)).ap()
    scratch16 = es.enter_context(sb("scratch16", [128, D], F32)).ap()
    t.E32 = scratch16.bitcast(BF16)[:, 0:D]
    t.brows0 = scratch16[0:1, :]
    t.xnbuf = [es.enter_context(sb(f"xnbuf{i}", [128, D], BF16)).ap()
               for i in range(2)]
    t.lnsc = [es.enter_context(sb(f"lnsc{i}", [128, 16], F32)).ap()
              for i in range(2)]
    t.ones_f32 = es.enter_context(sb("ones_f32", [1, 128], F32)).ap()
    t.ident_bf = es.enter_context(sb("ident_bf", [128, 128], BF16)).ap()
    t.ident_f32 = es.enter_context(sb("ident_f32", [128, 128], F32)).ap()
    t.gsm = [es.enter_context(sb(f"gsm{i}", [128, 64], F32)).ap()
             for i in range(2)]
    t.cst = es.enter_context(sb("cst", [128, 4], F32)).ap()
    t.zrow_bf = es.enter_context(sb("zrow_bf", [1, 256], BF16)).ap()
    t.ones_bf = es.enter_context(sb("ones_bf", [128, 1], BF16)).ap()
    t.mscat = es.enter_context(sb("mscat", [1, 1024], F32)).ap()
    t.bm2_bf = t.mbig.bitcast(BF16)[:, 0:D]
    t.bmat_bf = t.mbig.bitcast(BF16)[0:32, 0:D]
    t.ah8_bf = es.enter_context(sb("ah8_bf", [128, KC * 40], BF16)).ap()
    t.limt_f32 = es.enter_context(sb("limt_f32", [32, 8], F32)).ap()
    t.limt_bf = es.enter_context(sb("limt_bf", [32, 8], BF16)).ap()
    t.b8col = es.enter_context(sb("b8col", [8, 1], F32)).ap()


def alloc_psums(nc, es, t):
    ps = nc.psum_tensor
    banks = [es.enter_context(ps(f"bank{i}", [128, 512], F32)).ap()
             for i in range(8)]
    t.pmain = banks
    t.pgA, t.pgB, t.pgC, t.pgD = banks[4], banks[5], banks[6], banks[7]


class SemNames:
    names = ("consts", "mred", "mpe", "mevac", "sq", "xn", "tpe", "tevac",
             "gw", "gqw", "gch", "gld", "bld", "wmask", "mmc", "mmf", "x0d", "evac",
             "ost", "fin", "lnch", "sld", "bsum")
    arrays = {"mdma": 4, "xdma": 2, "wda": NG, "wdf": NG, "msc": 2}


def make_sems(nc, es):
    sem = NS()
    for n in SemNames.names:
        setattr(sem, n, es.enter_context(nc.semaphore(n)))
    for n, k in SemNames.arrays.items():
        setattr(sem, n, [es.enter_context(nc.semaphore(f"{n}{i}"))
                         for i in range(k)])
    return sem



def build_launch_a():
    nc = bass.Bass("TRN2", target_bir_lowering=False, debug=False)
    dt = nc.dram_tensor
    x0 = dt("x0", [TOK, D], F32, kind="ExternalInput").ap()
    wt_l0a = dt("wt_l0a", [D, D], F32, kind="ExternalInput").ap()
    wt_l0f = dt("wt_l0f", [D, D], F32, kind="ExternalInput").ap()
    maskt0 = dt("maskt0", [1024, 16384], I32, kind="ExternalInput").ap()
    wqk = dt("wqk", [D, 512], F32, kind="ExternalInput").ap()
    a2t = dt("a2t", [D, 128], F32, kind="ExternalInput").ap()
    bm2 = dt("bm2", [128, D], F32, kind="ExternalInput").ap()
    b_l0a = dt("b_l0a", [1, D], F32, kind="ExternalInput").ap()
    b_l0f = dt("b_l0f", [1, D], F32, kind="ExternalInput").ap()
    ah8 = dt("ah8", [D, 40], F32, kind="ExternalInput").ap()
    b_l1f = dt("b_l1f", [1, D], F32, kind="ExternalInput").ap()
    limt = dt("limt", [32, 8], F32, kind="ExternalInput").ap()

    x1 = dt("x1", [TOK, D], F32, kind="ExternalOutput").ap()
    locmax = dt("locmax", [2, 1], F32, kind="ExternalOutput").ap()
    bmq_d = dt("bmq_d", [128, 128], F32).ap()

    es = contextlib.ExitStack()
    t = NS()
    alloc_common(nc, es, t)
    t.bmq_d = bmq_d
    sb = nc.sbuf_tensor
    t.a2t_bf = t.mbig.bitcast(BF16)[:, 0:D]
    t.brows1 = es.enter_context(sb("brows1", [1, D], F32)).ap()
    t.gq = es.enter_context(sb("gq", [128, 256], F32)).ap()
    t.gqk = es.enter_context(sb("gqk", [128, 256], F32)).ap()
    t.hw_tok = es.enter_context(sb("hw_tok", [128, 128], BF16)).ap()
    t.hwT = es.enter_context(sb("hwT", [128, 1024], BF16)).ap()
    t.locmax = es.enter_context(sb("locmax_sb", [2, 1], F32)).ap()
    bmPf = t.bmP.bitcast(F32)
    t.zh_tok = bmPf[:, 0:40]
    t.zTa_bf = t.bmP[0:32, 2048:2560]
    t.h8Ta = bmPf[0:8, 64:576]
    t.dsbuf = bmPf[:, 1024:1056]
    t.hsbuf = bmPf[:, 1056:1088]
    t.max2x128 = bmPf[0:2, 1100:1228]
    alloc_psums(nc, es, t)
    sem = make_sems(nc, es)
    block = es.enter_context(nc.Block())

    with es:
        P = Prog()
        emit_consts(P, t, sem)
        P.op("gpsimd",
             lambda g: g.dma_start(
                 out=t.ah8_bf[:].rearrange("p (c n) -> p c n", c=KC),
                 in_=ah8[:, :].rearrange("(c p) n -> p c n", c=KC)),
             inc=sem.gqw, inc_n=16)
        P.op("sync", lambda sy: sy.dma_start(out=t.limt_f32[:], in_=limt[:, :]),
             inc=sem.sld, inc_n=16)
        P.op("sync",
             lambda sy: sy.dma_start(
                 out=t.b8col[:], in_=b_l1f[0:1, 0:8]
                     .rearrange("x (p y) -> p (x y)", p=8)),
             inc=sem.sld, inc_n=16)
        P.op("gpsimd",
             lambda g: g.dma_start(
                 out=t.wring_a[:].rearrange("p (c n) -> p c n", c=16),
                 in_=wqk[0:2048, :].rearrange("(c p) n -> p c n", c=16)),
             inc=sem.gqw, inc_n=16)
        P.op("gpsimd",
             lambda g: g.dma_start(
                 out=t.wring_f[:].rearrange("p (c n) -> p c n", c=16),
                 in_=wqk[2048:4096, :].rearrange("(c p) n -> p c n", c=16)),
             inc=sem.gqw, inc_n=16)
        P.chain("scalar", lambda s: s.copy(t.limt_bf[:], t.limt_f32[:]),
                sem.gch, waits=[(sem.sld, P.val(sem.sld))])
        for tt in range(TT):
            emit_mask2_group(P, t, sem, maskt0, 2 * tt)
            emit_mask2_group(P, t, sem, maskt0, 2 * tt + 1)
            emit_ln_tile(
                P, t, sem, x0[tt * 128:(tt + 1) * 128, :], tt % 2,
                (lambda c, tt=tt: t.xnT[:, c * 1024 + tt * 128:
                                        c * 1024 + (tt + 1) * 128]))
            emit_qk_tile(P, t, sem, tt)
        emit_bmp(P, t, sem)
        ring_free = P.val(sem.gch)
        emit_hw_phase(P, t, sem, a2t, bm2)
        P.op("sync", lambda sy: sy.dma_start(out=t.brows0[0:1, :], in_=b_l0a[:, :]),
             waits=[(sem.mpe, P.val(sem.mpe))], inc=sem.bld, inc_n=16)
        P.op("sync", lambda sy: sy.dma_start(out=t.brows1[0:1, :], in_=b_l0f[:, :]),
             inc=sem.bld, inc_n=16)
        P.op("vector",
             lambda v: v.tensor_tensor(out=t.brows0[0:1, :], in0=t.brows0[0:1, :],
                                       in1=t.brows1[0:1, :], op=ALU.add),
             waits=[(sem.bld, P.val(sem.bld))], inc=sem.bsum)
        emit_main(P, t, sem, wt_l0a, wt_l0f, x0, x1, layer=0,
                  extra_wait=[(sem.bld, P.val(sem.bld)), (sem.bsum, 1),
                              (sem.gw, P.val(sem.gw))],
                  ring_free_wait=[(sem.gch, ring_free)],
                  x0_first_wait=[(sem.xn, P.val(sem.xn))])
        P.op("sync", None, waits=[(sem.ost, P.val(sem.ost))], inc=sem.fin)
        finv = P.val(sem.fin)
        for e in ("scalar", "vector", "tensor", "gpsimd"):
            P.op(e, None, waits=[(sem.fin, finv)])
        x1odd = x1[1::2, :]

        def slot_a(c, at):
            if c < 16:
                return t.wring_a[:, c * 512 + at * 128: c * 512 + (at + 1) * 128]
            return t.wring_f[:, (c - 16) * 512 + at * 128:
                             (c - 16) * 512 + (at + 1) * 128]
        t.zsrc = slot_a
        for at in range(4):
            emit_ln_tile(P, t, sem, x1odd[at * 128:(at + 1) * 128, :], at % 2,
                         (lambda c, at=at: slot_a(c, at)),
                         first_waits=[(sem.fin, finv)])
            emit_zpre_tile(P, t, sem, at, t.zTa_bf, None, t.h8Ta)
        emit_anchor_gate(P, t, sem, t.zTa_bf, t.h8Ta, t.limt_bf[:])
        emit_locmax(P, t, sem)
        P.op("sync", lambda sy: sy.dma_start(out=locmax[:, :], in_=t.locmax[:]),
             waits=[(sem.gch, P.val(sem.gch))], inc=sem.ost, inc_n=16)
        P.op("sync", None, waits=[(sem.ost, P.val(sem.ost))])

        for name in ENGS:
            getattr(block, name)(lambda eng, name=name: P.replay(name, eng))
    return nc


def build_launch_b():
    nc = bass.Bass("TRN2", target_bir_lowering=False, debug=False)
    dt = nc.dram_tensor
    x1 = dt("x1", [TOK, D], F32, kind="ExternalInput").ap()
    gmax8 = dt("gmax8", [8, 2], F32, kind="ExternalInput").ap()
    wt_l1a = dt("wt_l1a", [D, D], F32, kind="ExternalInput").ap()
    wt_l1f = dt("wt_l1f", [D, D], F32, kind="ExternalInput").ap()
    maskt1 = dt("maskt1", [1024, 16384], I32, kind="ExternalInput").ap()
    ah8 = dt("ah8", [D, 40], F32, kind="ExternalInput").ap()
    limt = dt("limt", [32, 8], F32, kind="ExternalInput").ap()
    lim = dt("lim", [8, 32], F32, kind="ExternalInput").ap()
    bmat = dt("bmat", [32, D], F32, kind="ExternalInput").ap()
    b_l1f = dt("b_l1f", [1, D], F32, kind="ExternalInput").ap()

    y = dt("y", [TOK, D], F32, kind="ExternalOutput").ap()
    gsc_dram = dt("gsc_scratch", [2, 1], F32).ap()
    bmq_d = dt("bmq_d", [128, 128], F32).ap()

    es = contextlib.ExitStack()
    t = NS()
    alloc_common(nc, es, t)
    t.bmq_d = bmq_d
    sb = nc.sbuf_tensor
    t.zp_bf = es.enter_context(sb("zp_bf", [32, 1024], BF16)).ap()
    t.zT_f32 = es.enter_context(sb("zT_f32", [32, 1024], F32)).ap()
    t.zT_bf = es.enter_context(sb("zT_bf", [32, 1024], BF16)).ap()
    t.h8T = es.enter_context(sb("h8T", [8, 1024], F32)).ap()
    t.zh_tok = es.enter_context(sb("zh_tok", [128, 40], F32)).ap()
    t.dsbuf = es.enter_context(sb("dsbuf", [128, 32], F32)).ap()
    t.hsbuf = es.enter_context(sb("hsbuf", [128, 32], F32)).ap()
    xb0, xb1 = t.xstage[0], t.xstage[1]
    t.wfullT = xb0.bitcast(BF16)[0:8, 5120:6144]
    t.max2x128 = xb1[0:2, 64:192]
    t.gmax8 = xb1[0:8, 192:194]
    t.gsc2 = xb1[0:2, 200:202]
    t.gscH = xb1[:, 208:209]
    t.gscD = xb1[:, 209:210]
    t.lim_f32 = xb1[0:8, 216:248]
    t.lim_bf = xb1.bitcast(BF16)[0:8, 496:528]
    alloc_psums(nc, es, t)
    sem = make_sems(nc, es)
    block = es.enter_context(nc.Block())

    with es:
        P = Prog()
        emit_consts(P, t, sem)
        P.op("gpsimd",
             lambda g: g.dma_start(
                 out=t.ah8_bf[:].rearrange("p (c n) -> p c n", c=KC),
                 in_=ah8[:, :].rearrange("(c p) n -> p c n", c=KC)),
             inc=sem.gqw, inc_n=16)
        P.op("sync", lambda sy: sy.dma_start(out=t.limt_f32[:], in_=limt[:, :]),
             inc=sem.sld, inc_n=16)
        P.op("sync",
             lambda sy: sy.dma_start(
                 out=t.b8col[:], in_=b_l1f[0:1, 0:8]
                     .rearrange("x (p y) -> p (x y)", p=8)),
             inc=sem.sld, inc_n=16)
        P.chain("scalar", lambda s: s.copy(t.limt_bf[:], t.limt_f32[:]),
                sem.gch, waits=[(sem.sld, P.val(sem.sld))])
        t.zsrc = (lambda c, tt: t.xnT[:, c * 1024 + tt * 128:
                                      c * 1024 + (tt + 1) * 128])
        for tt in range(TT):
            emit_mask2_group(P, t, sem, maskt1, 2 * tt)
            emit_mask2_group(P, t, sem, maskt1, 2 * tt + 1)
            emit_ln_tile(
                P, t, sem, x1[tt * 128:(tt + 1) * 128, :], tt % 2,
                (lambda c, tt=tt: t.xnT[:, c * 1024 + tt * 128:
                                        c * 1024 + (tt + 1) * 128]))
            emit_zpre_tile(P, t, sem, tt, t.zT_bf, t.zT_f32, t.h8T)
        emit_bmp(P, t, sem)
        emit_anchor_gate(P, t, sem, t.zT_bf[:, 1::2], t.h8T[:, 1::2],
                         t.limt_bf[:])
        emit_l1_post(P, t, sem, gmax8, gsc_dram, bmat, lim)
        P.op("sync", lambda sy: sy.dma_start(out=t.brows0[0:1, :], in_=b_l1f[:, :]),
             waits=[(sem.mpe, P.val(sem.mpe))], inc=sem.bld, inc_n=16)
        emit_main(P, t, sem, wt_l1a, wt_l1f, x1, y, layer=1,
                  extra_wait=[(sem.bld, P.val(sem.bld))],
                  x0_first_wait=[(sem.gch, P.val(sem.gch))])
        P.op("sync", None, waits=[(sem.ost, P.val(sem.ost))])

        for name in ENGS:
            getattr(block, name)(lambda eng, name=name: P.replay(name, eng))
    return nc



_NCA = None
_NCB = None


def kernel(L0_attn_data, L0_attn_mask, L0_attn_weight, L0_attn_bias,
           L0_ffn_W, L0_ffn_b, L0_ffn_Wq, L0_ffn_Wk, L0_ffn_A, L0_ffn_Bm,
           L1_attn_weight, L1_attn_mask, L1_ffn_W, L1_ffn_b, L1_ffn_A,
           L1_ffn_Bmat, L1_ffn_LiMEs):
    global _NCA, _NCB
    f32 = np.float32
    cat = np.ascontiguousarray

    x0 = np.asarray(L0_attn_data, f32).reshape(NCORES * TOK, D)
    wt_l0a = cat(np.asarray(L0_attn_weight, f32).T)
    wt_l0f = cat(np.asarray(L0_ffn_W, f32).T)
    wt_l1a = cat(np.asarray(L1_attn_weight, f32).T)
    wt_l1f = cat(np.asarray(L1_ffn_W, f32).T)

    def _maskT2(m):
        return cat(np.asarray(m, np.int32).reshape(128, 32, 128, 32)
                   .transpose(1, 3, 2, 0).reshape(1024, 16384))
    maskt0 = _maskT2(L0_attn_mask)
    maskt1 = _maskT2(L1_attn_mask)
    wqk = cat(np.concatenate([np.asarray(L0_ffn_Wq, f32).T,
                              np.asarray(L0_ffn_Wk, f32).T], axis=1))
    a2t = cat(np.asarray(L0_ffn_A, f32).reshape(128, D).T)
    bm2 = cat(np.asarray(L0_ffn_Bm, f32).transpose(0, 2, 1).reshape(128, D))
    b_l0a = cat(np.asarray(L0_attn_bias, f32)[None, :])
    b_l0f = cat(np.asarray(L0_ffn_b, f32)[None, :])
    ah8 = cat(np.concatenate([np.asarray(L1_ffn_A, f32),
                              np.asarray(L1_ffn_W, f32)[0:8, :].T], axis=1))
    b_l1f = cat(np.asarray(L1_ffn_b, f32)[None, :])
    lim = cat(np.asarray(L1_ffn_LiMEs, f32))
    limt = cat(lim.T)
    bmat = cat(np.asarray(L1_ffn_Bmat, f32))

    if _NCA is None:
        _NCA = build_launch_a()
    in_maps_a = []
    for c in range(NCORES):
        in_maps_a.append(dict(
            x0=cat(x0[c * TOK:(c + 1) * TOK]), wt_l0a=wt_l0a, wt_l0f=wt_l0f,
            maskt0=maskt0, wqk=wqk, a2t=a2t, bm2=bm2, b_l0a=b_l0a,
            b_l0f=b_l0f, ah8=ah8, b_l1f=b_l1f, limt=limt))
    res_a = run_bass_kernel_spmd(_NCA, in_maps_a, core_ids=list(range(NCORES)))

    gmax8 = np.concatenate([r["locmax"].reshape(1, 2) for r in res_a.results],
                           axis=0)

    if _NCB is None:
        _NCB = build_launch_b()
    in_maps_b = []
    for c in range(NCORES):
        in_maps_b.append(dict(
            x1=res_a.results[c]["x1"], gmax8=gmax8, wt_l1a=wt_l1a,
            wt_l1f=wt_l1f, maskt1=maskt1, ah8=ah8, limt=limt, lim=lim,
            bmat=bmat, b_l1f=b_l1f))
    res_b = run_bass_kernel_spmd(_NCB, in_maps_b, core_ids=list(range(NCORES)))

    y = np.concatenate([r["y"] for r in res_b.results], axis=0)
    return y.reshape(2, NCORES * TOK // 2, D)


# version 13
# speedup vs baseline: 1.2369x; 1.0515x over previous
import contextlib
import numpy as np
import ml_dtypes
import concourse.bass as bass
import concourse.mybir as mybir
from concourse.bass_utils import run_bass_kernel_spmd

F32 = mybir.dt.float32
BF16 = mybir.dt.bfloat16
I32 = mybir.dt.int32
AF = mybir.ActivationFunctionType
ALU = mybir.AluOpType
AX = mybir.AxisListType

NCORES = 8
TOK = 1024
D = 4096
KC = 32
TT = 8
NSUP = 16
NW = 256
GRP = 8
NG = KC // GRP
INV_SQRT_DK = 0.17677669529663687
LN_EPS = 1e-5

ENGS = ("sync", "gpsimd", "scalar", "vector", "tensor")


class Prog:

    def __init__(self):
        self.ops = {e: [] for e in ENGS}
        self.cnt = {}

    def val(self, sem):
        return self.cnt.get(sem, 0)

    def op(self, eng, build, waits=(), inc=None, inc_n=1):
        tgt = None
        if inc is not None:
            self.cnt[inc] = self.cnt.get(inc, 0) + inc_n
            tgt = self.cnt[inc]
        self.ops[eng].append((tuple(waits), build, inc, inc_n))
        return tgt

    def chain(self, eng, build, sem, inc_n=1, waits=()):
        return self.op(eng, build, waits=[(sem, self.val(sem))] + list(waits),
                       inc=sem, inc_n=inc_n)

    def replay(self, eng_name, eng):
        for waits, build, inc, inc_n in self.ops[eng_name]:
            for sem, v in waits:
                if v > 0:
                    eng.wait_ge(sem, v)
            inst = build(eng) if build is not None else None
            if inc is not None:
                if inst is None:
                    inst = eng.nop()
                inst.then_inc(inc, inc_n)


class NS:
    pass



def emit_consts(P, t, sem):
    C = sem.consts

    def _ident_ops(ap, n):
        yield lambda g: g.memset(ap, 0.0)
        yield lambda g: g.affine_select(out=ap, in_=ap,
                                        compare_op=ALU.not_equal, fill=1.0,
                                        base=0, pattern=[[-1, n]],
                                        channel_multiplier=1)

    ops = []
    ops += list(_ident_ops(t.ident_bf[:], 128))
    ops += list(_ident_ops(t.ident_f32[:], 128))
    ops.append(lambda g: g.memset(t.ones_f32[:], 1.0))
    ops.append(lambda g: g.memset(t.zrow_bf[:], 0.0))
    ops.append(lambda g: g.memset(t.ones_bf[:], 1.0))
    ops.append(lambda g: g.memset(t.cst[:, 0:1], INV_SQRT_DK))
    ops.append(lambda g: g.memset(t.cst[:, 1:2], 2.0))
    ops.append(lambda g: g.memset(t.cst[:, 2:3], 0.5))
    for f in ops:
        P.chain("gpsimd", f, C)


def emit_mask2_group(P, t, sem, maskT2, bt):
    mt = [t.mbig.bitcast(BF16)[:, i * 1024:(i + 1) * 1024] for i in range(4)]
    strip_banks = [t.pgA, t.pgB]
    for qc in range(8):
        i = bt * 8 + qc
        par = i % 4
        waits = []
        if i >= 4:
            waits.append((sem.mpe, P.val(sem.mpe) - 3))
        P.op("gpsimd",
             (lambda g, bt=bt, qc=qc, par=par: g.dma_start(
                 out=mt[par][:],
                 in_=maskT2[qc * 128:(qc + 1) * 128,
                            bt * 1024:(bt + 1) * 1024])),
             waits=waits, inc=sem.mdma[par], inc_n=16)
        dmav = P.val(sem.mdma[par])

        def _mm(te, bt=bt, qc=qc, par=par):
            last = None
            for j in range(2):
                last = te.matmul(
                    strip_banks[j][0:1, 0:512],
                    t.ones_bf[:, 0:1],
                    mt[par][:, j * 512:(j + 1) * 512],
                    start=(qc == 0), stop=(qc == 7))
            return last
        mmw = [(sem.mdma[par], dmav)]
        if qc == 0:
            mmw.append((sem.consts, P.val(sem.consts)))
            if bt >= 1:
                mmw.append((sem.mevac, P.val(sem.mevac)))
        P.op("tensor", _mm, waits=mmw, inc=sem.mpe)
    for j in range(2):
        sidx = 2 * bt + j
        par = sidx % 2
        waits = [(sem.mpe, P.val(sem.mpe))]
        if sidx >= 2:
            waits.append((sem.msc[par], P.val(sem.msc[par])))
        P.op("scalar",
             (lambda s, j=j, par=par: s.copy(
                 t.mscat[0:1, par * 512:(par + 1) * 512],
                 strip_banks[j][0:1, 0:512])),
             waits=waits, inc=sem.mevac)
        P.op("gpsimd",
             (lambda g, sidx=sidx, par=par: g.dma_start(
                 out=t.bmq_d[4 * sidx:4 * (sidx + 1), :]
                     .rearrange("(x a) n -> x a n", x=1),
                 in_=t.mscat[0:1, par * 512:(par + 1) * 512]
                     .rearrange("x (a n) -> x a n", a=4))),
             waits=[(sem.mevac, P.val(sem.mevac))],
             inc=sem.msc[par], inc_n=16)


def emit_bmp(P, t, sem):
    P.op("gpsimd", lambda g: g.dma_start(out=t.bmQ[:], in_=t.bmq_d[:, :]),
         waits=[(sem.msc[0], P.val(sem.msc[0])),
                (sem.msc[1], P.val(sem.msc[1]))],
         inc=sem.msc[0], inc_n=16)
    P.op("vector",
         lambda v: v.tensor_scalar(out=t.Qbf[:], in0=t.bmQ[:], scalar1=1.0,
                                   scalar2=None, op0=ALU.is_ge),
         waits=[(sem.msc[0], P.val(sem.msc[0]))],
         inc=sem.mred)
    qready = P.val(sem.mred)
    for c in range(KC):
        waits = [(sem.mred, qready)]
        if c < 2:
            waits.append((sem.gqw, P.val(sem.gqw)))
        if c >= 2:
            waits.append((sem.mevac, P.val(sem.mevac) - 1))
        psq = t.pgC if c % 2 == 0 else t.pgD
        P.op("tensor",
             (lambda te, c=c, psq=psq: te.matmul(
                 psq[:, 0:128], t.E32[:, c * 128:(c + 1) * 128],
                 t.Qbf[:], start=True, stop=True)),
             waits=waits, inc=sem.mpe)
        P.op("scalar",
             (lambda s, c=c, psq=psq: s.copy(
                 t.bmP[:, c * 128:(c + 1) * 128], psq[:, 0:128])),
             waits=[(sem.mpe, P.val(sem.mpe))], inc=sem.mevac)


def emit_ln_stats(P, t, sem, xin_tile, tt, st):
    L = sem.lnch
    par = tt % 2
    ls = t.lnsc8[:, tt * 16:(tt + 1) * 16]
    P.op("sync",
         (lambda sy, par=par, xin_tile=xin_tile: sy.dma_start(
             out=t.xstage[par][:], in_=xin_tile)),
         waits=st.rel.get(par, []), inc=sem.xdma[par], inc_n=16)
    ld = (sem.xdma[par], P.val(sem.xdma[par]))
    sqv = P.op("scalar",
               (lambda s, par=par, ls=ls: s.activation(
                   out=t.xnbuf[par][:], in_=t.xstage[par][:], func=AF.Square,
                   accum_out=ls[:, 1:2])),
               waits=[ld], inc=sem.sq)
    P.op("vector",
         (lambda v, par=par, ls=ls: v.reduce_sum(
             ls[:, 0:1], t.xstage[par][:], axis=AX.X)),
         waits=[ld, (L, P.val(L))], inc=L)
    st.rel[par] = [(sem.sq, sqv), (L, P.val(L))]
    P.chain("vector",
            (lambda v, ls=ls: v.tensor_scalar(
                out=ls[:, 2:4], in0=ls[:, 0:2], scalar1=1.0 / D,
                scalar2=None, op0=ALU.mult)), L,
            waits=[(sem.sq, sqv)])
    P.chain("vector",
            (lambda v, ls=ls: v.tensor_scalar(
                out=ls[:, 4:5], in0=ls[:, 2:3], scalar1=ls[:, 2:3],
                scalar2=None, op0=ALU.mult)), L)
    P.chain("vector",
            (lambda v, ls=ls: v.tensor_scalar(
                out=ls[:, 5:6], in0=ls[:, 3:4], scalar1=ls[:, 4:5],
                scalar2=LN_EPS, op0=ALU.subtract, op1=ALU.add)), L)
    P.chain("scalar",
            (lambda s, ls=ls: s.activation(out=ls[:, 6:7], in_=ls[:, 5:6],
                                           func=AF.Sqrt)), L)
    P.chain("vector",
            (lambda v, ls=ls: v.reciprocal(ls[:, 10:11], ls[:, 6:7])), L)
    P.chain("vector",
            (lambda v, ls=ls: v.tensor_scalar(
                out=ls[:, 11:12], in0=ls[:, 2:3], scalar1=ls[:, 10:11],
                scalar2=-1.0, op0=ALU.mult, op1=ALU.mult)), L)


def emit_ln_chunk(P, t, sem, xin, c, st):
    par = c % 2
    waits = []
    if c < 2:
        waits += st.rel[par]
    else:
        waits.append((sem.xn, st.xnv[c - 2]))
        waits.append((sem.xns, st.xnsv[c - 2]))
    P.op("sync",
         (lambda sy, par=par, c=c, xin=xin: sy.dma_start(
             out=t.xck[par][:].rearrange("p (tt n) -> p tt n", tt=TT),
             in_=xin[:, c * 128:(c + 1) * 128]
                 .rearrange("(tt p) n -> p tt n", tt=TT))),
         waits=waits, inc=sem.xdma[par], inc_n=16)
    dv = (sem.xdma[par], P.val(sem.xdma[par]))
    tvals = []
    for tt in range(TT):
        ls = t.lnsc8[:, tt * 16:(tt + 1) * 16]
        w = [dv]
        if c < 2:
            w.append((sem.lnch, st.statsL))
        else:
            w.append((sem.tpe, st.tpev[c - 2]))
        if tt < 4:
            v = P.op("vector",
                     (lambda v, par=par, tt=tt, ls=ls: v.tensor_scalar(
                         out=t.xnbuf[par][:, tt * 128:(tt + 1) * 128],
                         in0=t.xck[par][:, tt * 128:(tt + 1) * 128],
                         scalar1=ls[:, 10:11], scalar2=ls[:, 11:12],
                         op0=ALU.mult, op1=ALU.add)),
                     waits=w, inc=sem.xn)
            tvals.append((sem.xn, v))
        else:
            v = P.op("scalar",
                     (lambda s, par=par, tt=tt, ls=ls: s.activation(
                         out=t.xnbuf[par][:, tt * 128:(tt + 1) * 128],
                         in_=t.xck[par][:, tt * 128:(tt + 1) * 128],
                         func=AF.Identity, scale=ls[:, 10:11],
                         bias=ls[:, 11:12])),
                     waits=w, inc=sem.xns)
            tvals.append((sem.xns, v))
    st.xnv[c] = P.val(sem.xn)
    st.xnsv[c] = P.val(sem.xns)
    bank = t.pmain[par].bitcast(BF16)
    for tt in range(TT):
        w = [tvals[tt]]
        if c < 2:
            w.append((sem.consts, P.val(sem.consts)))
        else:
            w.append((sem.tevac, st.tev[c - 2]))
        P.op("tensor",
             (lambda te, par=par, tt=tt, bank=bank: te.transpose(
                 bank[:, tt * 128:(tt + 1) * 128],
                 t.xnbuf[par][:, tt * 128:(tt + 1) * 128], t.ident_bf[:])),
             waits=w, inc=sem.tpe)
    st.tpev[c] = P.val(sem.tpe)
    P.op("scalar",
         (lambda s, c=c, bank=bank: s.copy(
             t.xnT[:, c * 1024:(c + 1) * 1024], bank[:, 0:1024])),
         waits=[(sem.tpe, P.val(sem.tpe))], inc=sem.tevac)
    st.tev[c] = P.val(sem.tevac)


def emit_qk_pair(P, t, sem, pr, free_prev):
    C = sem.gch
    gs = t.gsm[0]

    def _qk(te, pr=pr):
        last = None
        for c in range(KC):
            w = t.wring_a if c < 16 else t.wring_f
            cc = c % 16
            for i in range(2):
                tt = 2 * pr + i
                last = te.matmul(
                    t.pmain[2 + i][:, 0:512],
                    t.xnT[:, c * 1024 + tt * 128: c * 1024 + (tt + 1) * 128],
                    w[:, cc * 512:(cc + 1) * 512],
                    start=(c == 0), stop=(c == KC - 1))
        return last
    P.op("tensor", _qk,
         waits=[(C, free_prev), (sem.gqw, P.val(sem.gqw)),
                (sem.tevac, P.val(sem.tevac))],
         inc=C)
    bases = (0, 32)
    for i in range(2):
        bank, base = t.pmain[2 + i], bases[i]
        P.chain("scalar",
                (lambda s, bank=bank: s.activation(
                    out=t.gq[:], in_=bank[:, 0:256], func=AF.Identity,
                    scale=t.cst[:, 0:1])), C)
        P.chain("vector",
                (lambda v, bank=bank: v.tensor_tensor(
                    out=t.gqk[:], in0=bank[:, 256:512], in1=t.gq[:],
                    op=ALU.mult)), C)
        P.chain("vector",
                (lambda v, gs=gs, base=base: v.tensor_reduce(
                    out=gs[:, base:base + 8][:, :, None],
                    in_=t.gqk[:].rearrange("p (a b) -> p a b", a=8),
                    axis=AX.X, op=ALU.add)), C)
    free_val = P.val(C)
    for i in range(2):
        base, tt = bases[i], 2 * pr + i
        P.chain("vector",
                (lambda v, gs=gs, base=base: v.reduce_max(
                    gs[:, base + 8:base + 9], gs[:, base:base + 8],
                    axis=AX.X, negate=True)), C)
        P.chain("scalar",
                (lambda s, gs=gs, base=base: s.activation(
                    out=gs[:, base + 16:base + 24], in_=gs[:, base:base + 8],
                    func=AF.Exp, bias=gs[:, base + 8:base + 9], scale=1.0)), C)
        P.chain("vector",
                (lambda v, gs=gs, base=base: v.reduce_sum(
                    gs[:, base + 9:base + 10], gs[:, base + 16:base + 24],
                    axis=AX.X)), C)
        P.chain("vector",
                (lambda v, gs=gs, base=base: v.reciprocal(
                    gs[:, base + 13:base + 14], gs[:, base + 9:base + 10])), C)
        P.chain("vector",
                (lambda v, gs=gs, base=base, tt=tt: v.tensor_scalar(
                    out=t.gsm[1][:, 8 * tt:8 * (tt + 1)],
                    in0=gs[:, base + 16:base + 24],
                    scalar1=gs[:, base + 13:base + 14], scalar2=None,
                    op0=ALU.mult)), C)
    return free_val


def emit_recip_refine(P, t, sem, gs, src_col, dst_base):
    C = sem.gch
    P.chain("vector",
            (lambda v: v.reciprocal(gs[:, dst_base:dst_base + 1],
                                    gs[:, src_col:src_col + 1])), C)
    P.chain("vector",
            (lambda v: v.tensor_scalar(
                out=gs[:, dst_base + 1:dst_base + 2],
                in0=gs[:, src_col:src_col + 1],
                scalar1=gs[:, dst_base:dst_base + 1], scalar2=None,
                op0=ALU.mult)), C)
    P.chain("vector",
            (lambda v: v.tensor_scalar(
                out=gs[:, dst_base + 2:dst_base + 3],
                in0=gs[:, dst_base + 1:dst_base + 2],
                scalar1=-1.0, scalar2=2.0, op0=ALU.mult, op1=ALU.add)), C)
    P.chain("vector",
            (lambda v: v.tensor_scalar(
                out=gs[:, dst_base + 3:dst_base + 4],
                in0=gs[:, dst_base:dst_base + 1],
                scalar1=gs[:, dst_base + 2:dst_base + 3], scalar2=None,
                op0=ALU.mult)), C)


def emit_hw_phase(P, t, sem, hooks=None):
    C = sem.gch
    gwv = P.val(sem.gw)
    for tt in range(TT):
        def _h(te, tt=tt):
            last = None
            for c in range(KC):
                last = te.matmul(
                    t.pmain[2][:, 0:128],
                    t.xnT[:, c * 1024 + tt * 128: c * 1024 + (tt + 1) * 128],
                    t.a2t_bf[:, c * 128:(c + 1) * 128],
                    start=(c == 0), stop=(c == KC - 1))
            return last
        P.chain("tensor", _h, C, waits=[(sem.gw, gwv)])
        P.chain("vector",
                (lambda v, tt=tt: v.tensor_tensor(
                    out=t.hw_tok[:].rearrange("p (a b) -> p a b", a=8),
                    in0=t.pmain[2][:, 0:128].rearrange("p (a b) -> p a b", a=8),
                    in1=t.gsm[1][:, 8 * tt:8 * (tt + 1)][:, :, None]
                        .broadcast_to([128, 8, 16]),
                    op=ALU.mult)), C)
        P.chain("tensor",
                lambda te: te.transpose(
                    t.pmain[3][:, 0:128].bitcast(BF16)[:, 0:128],
                    t.hw_tok[:], t.ident_bf[:]), C)
        P.chain("scalar",
                (lambda s, tt=tt: s.copy(
                    t.hwT[:, tt * 128:(tt + 1) * 128],
                    t.pmain[3][:, 0:128].bitcast(BF16)[:, 0:128])), C)
        if hooks and tt in hooks:
            hooks[tt]()


def emit_z_chunk_mm(P, t, sem, c, st, first):
    def _z(te, c=c, first=first):
        last = None
        for tt in range(TT):
            last = te.matmul(
                t.pmain[2][:, tt * 40:(tt + 1) * 40],
                t.xnT[:, c * 1024 + tt * 128: c * 1024 + (tt + 1) * 128],
                t.ah8_bf[:, c * 40:(c + 1) * 40],
                start=(first and tt == 0), stop=False)
        return last
    waits = [(sem.tevac, st.tev[c])]
    if first:
        waits.append((sem.gqw, P.val(sem.gqw)))
    P.op("tensor", _z, waits=waits)


def emit_z_close(P, t, sem):
    P.op("tensor",
         lambda te: te.matmul(t.pmain[2][:, 0:320], t.zrow_bf[0:1, 0:128],
                              t.zrow_bf[0:1, 0:320], start=False, stop=True),
         waits=[], inc=sem.zdone)


def emit_z_evac(P, t, sem, tt, zT_bf, zT_f32, h8T, sldv):
    C = sem.gch
    P.chain("scalar",
            (lambda s, tt=tt: s.copy(t.zh_tok[:],
                                     t.pmain[2][:, tt * 40:(tt + 1) * 40])), C,
            waits=([(sem.zdone, 1)] if tt == 0 else []))
    P.chain("tensor",
            lambda te: te.transpose(t.pmain[3][0:32, 0:128], t.zh_tok[:, 0:32],
                                    t.ident_f32[:]), C)

    def _zev(s, tt=tt):
        last = s.copy(zT_bf[:, tt * 128:(tt + 1) * 128],
                      t.pmain[3][0:32, 0:128])
        if zT_f32 is not None:
            last = s.copy(zT_f32[:, tt * 128:(tt + 1) * 128],
                          t.pmain[3][0:32, 0:128])
        return last
    P.chain("scalar", _zev, C)
    P.chain("tensor",
            lambda te: te.transpose(t.pmain[3][0:8, 128:256],
                                    t.zh_tok[:, 32:40], t.ident_f32[:]), C)
    P.chain("scalar",
            (lambda s, tt=tt: s.activation(
                out=h8T[:, tt * 128:(tt + 1) * 128],
                in_=t.pmain[3][0:8, 128:256], func=AF.Identity,
                bias=t.b8col[:, 0:1], scale=1.0)), C,
            waits=([(sem.sld, sldv)] if tt == 0 else []))


def emit_zpre_tile(P, t, sem, tt, zT_bf, zT_f32, h8T):
    C = sem.gch

    def _zmm(te, tt=tt):
        last = None
        for c in range(KC):
            last = te.matmul(
                t.pmain[2][:, 0:40],
                t.zsrc(c, tt),
                t.ah8_bf[:, c * 40:(c + 1) * 40],
                start=(c == 0), stop=(c == KC - 1))
        return last
    P.chain("tensor", _zmm, C,
            waits=[(sem.gqw, P.val(sem.gqw)), (sem.tevac, P.val(sem.tevac)),
                   (sem.sld, P.val(sem.sld))])
    P.chain("scalar", lambda s: s.copy(t.zh_tok[:], t.pmain[2][:, 0:40]), C)
    P.chain("tensor",
            lambda te: te.transpose(t.pmain[3][0:32, 0:128], t.zh_tok[:, 0:32],
                                    t.ident_f32[:]), C)

    def _zev(s, tt=tt):
        last = s.copy(zT_bf[:, tt * 128:(tt + 1) * 128],
                      t.pmain[3][0:32, 0:128])
        if zT_f32 is not None:
            last = s.copy(zT_f32[:, tt * 128:(tt + 1) * 128],
                          t.pmain[3][0:32, 0:128])
        return last
    P.chain("scalar", _zev, C)
    P.chain("tensor",
            lambda te: te.transpose(t.pmain[3][0:8, 128:256],
                                    t.zh_tok[:, 32:40], t.ident_f32[:]), C)
    P.chain("scalar",
            (lambda s, tt=tt: s.activation(
                out=h8T[:, tt * 128:(tt + 1) * 128],
                in_=t.pmain[3][0:8, 128:256], func=AF.Identity,
                bias=t.b8col[:, 0:1], scale=1.0)), C)


def emit_ln_tile(P, t, sem, xin_tile, par, xnt_slot, first_waits=()):
    L = sem.lnch
    ls = t.lnsc[par]
    waits = list(first_waits)
    if P.val(sem.xn) >= 2:
        waits.append((sem.xn, P.val(sem.xn) - 1))
    P.op("sync",
         (lambda sy, par=par, xin_tile=xin_tile: sy.dma_start(
             out=t.xstage[par][:], in_=xin_tile)),
         waits=waits, inc=sem.xdma[par], inc_n=16)
    ld = (sem.xdma[par], P.val(sem.xdma[par]))
    sq_waits = [ld]
    if P.val(sem.tevac) >= 2 * KC:
        sq_waits.append((sem.tevac, P.val(sem.tevac) - KC))
    P.op("scalar",
         (lambda s, par=par, ls=ls: s.activation(
             out=t.xnbuf[par][:], in_=t.xstage[par][:], func=AF.Square,
             accum_out=ls[:, 1:2])),
         waits=sq_waits, inc=sem.sq)
    P.op("vector",
         (lambda v, par=par, ls=ls: v.reduce_sum(
             ls[:, 0:1], t.xstage[par][:], axis=AX.X)),
         waits=[ld, (L, P.val(L))], inc=L)
    P.op("vector",
         (lambda v, ls=ls: v.tensor_scalar(
             out=ls[:, 2:4], in0=ls[:, 0:2], scalar1=1.0 / D,
             scalar2=None, op0=ALU.mult)),
         waits=[(L, P.val(L)), (sem.sq, P.val(sem.sq))], inc=L)
    P.chain("vector",
            (lambda v, ls=ls: v.tensor_scalar(
                out=ls[:, 4:5], in0=ls[:, 2:3], scalar1=ls[:, 2:3],
                scalar2=None, op0=ALU.mult)), L)
    P.chain("vector",
            (lambda v, ls=ls: v.tensor_scalar(
                out=ls[:, 5:6], in0=ls[:, 3:4], scalar1=ls[:, 4:5],
                scalar2=LN_EPS, op0=ALU.subtract, op1=ALU.add)), L)
    P.chain("scalar",
            (lambda s, ls=ls: s.activation(out=ls[:, 6:7], in_=ls[:, 5:6],
                                           func=AF.Sqrt)), L)
    P.chain("vector",
            (lambda v, ls=ls: v.reciprocal(ls[:, 10:11], ls[:, 6:7])), L)
    P.chain("vector",
            (lambda v, ls=ls: v.tensor_scalar(
                out=ls[:, 11:12], in0=ls[:, 2:3], scalar1=ls[:, 10:11],
                scalar2=-1.0, op0=ALU.mult, op1=ALU.mult)), L)
    P.op("scalar",
         (lambda s, par=par, ls=ls: s.activation(
             out=t.xnbuf[par][:], in_=t.xstage[par][:], func=AF.Identity,
             scale=ls[:, 10:11], bias=ls[:, 11:12])),
         waits=[(L, P.val(L))], inc=sem.xn)
    xnv = P.val(sem.xn)
    for c in range(KC):
        psl = t.pmain[c % 2][:, 0:128].bitcast(BF16)
        waits = [(sem.xn, xnv)]
        if P.val(sem.tevac) > 0:
            waits.append((sem.tevac, P.val(sem.tevac) - 1))
        P.op("tensor",
             (lambda te, c=c, par=par, psl=psl: te.transpose(
                 psl[:, 0:128],
                 t.xnbuf[par][:, c * 128:(c + 1) * 128], t.ident_bf[:])),
             waits=waits, inc=sem.tpe)
        P.op("scalar",
             (lambda s, psl=psl, slot=xnt_slot(c): s.copy(slot, psl[:, 0:128])),
             waits=[(sem.tpe, P.val(sem.tpe))], inc=sem.tevac)


def emit_anchor_gate(P, t, sem, zT_bf_anch, h8T_anch, limt_view):
    C = sem.gch
    for aj in range(4):
        P.chain("tensor",
                (lambda te, aj=aj: te.matmul(
                    t.pmain[2][:, 64 + aj * 8:64 + (aj + 1) * 8],
                    zT_bf_anch[:, aj * 128:(aj + 1) * 128],
                    limt_view, start=True, stop=True)), C)
        P.chain("tensor",
                (lambda te, aj=aj: te.transpose(
                    t.pmain[3][:, 256 + aj * 8:256 + (aj + 1) * 8],
                    h8T_anch[:, aj * 128:(aj + 1) * 128],
                    t.ident_f32[0:8, 0:8])), C)
    P.chain("scalar", lambda s: s.copy(t.dsbuf[:], t.pmain[2][:, 64:96]), C)
    P.chain("scalar", lambda s: s.copy(t.hsbuf[:], t.pmain[3][:, 256:288]), C)


def emit_locmax(P, t, sem):
    C = sem.gch
    P.chain("vector",
            lambda v: v.reduce_max(t.gsm[0][:, 0:1], t.hsbuf[:], axis=AX.X,
                                   apply_absolute_value=True), C)
    P.chain("vector",
            lambda v: v.reduce_max(t.gsm[0][:, 1:2], t.dsbuf[:], axis=AX.X,
                                   apply_absolute_value=True), C)
    P.chain("tensor",
            lambda te: te.transpose(t.pmain[3][0:2, 0:128], t.gsm[0][:, 0:2],
                                    t.ident_f32[:]), C)
    P.chain("scalar",
            lambda s: s.copy(t.max2x128[:], t.pmain[3][0:2, 0:128]), C)
    P.chain("vector",
            lambda v: v.reduce_max(t.locmax[:], t.max2x128[:], axis=AX.X), C)


def emit_l1_post(P, t, sem, gmax_dram, gsc_dram, bmat_dram, lim_dram):
    C = sem.gch
    g0 = t.gsm[0]
    P.op("sync", lambda sy: sy.dma_start(out=t.gmax8[:], in_=gmax_dram[:, :]),
         waits=[(sem.xn, P.val(sem.xn)), (sem.xns, P.val(sem.xns))],
         inc=sem.sld, inc_n=16)
    P.op("gpsimd",
         lambda g: g.dma_start(
             out=t.bmat_bf[:].rearrange("p (c n) -> p c n", c=1),
             in_=bmat_dram[:, :].rearrange("(c p) n -> p c n", c=1)),
         waits=[(sem.mpe, P.val(sem.mpe))], inc=sem.gld, inc_n=16)
    P.op("sync", lambda sy: sy.dma_start(out=t.lim_f32[:], in_=lim_dram[:, :]),
         waits=[(sem.xn, P.val(sem.xn)), (sem.xns, P.val(sem.xns))],
         inc=sem.sld, inc_n=16)
    P.op("vector", None,
         waits=[(sem.gld, P.val(sem.gld)), (sem.sld, P.val(sem.sld)),
                (sem.consts, P.val(sem.consts)), (sem.mevac, P.val(sem.mevac)),
                (sem.gch, P.val(sem.gch))],
         inc=C)
    P.chain("scalar", lambda s: s.activation(out=t.lim_bf[:], in_=t.lim_f32[:],
                                             func=AF.Identity,
                                             scale=t.cst[0:8, 2:3]), C)
    P.chain("tensor",
            lambda te: te.transpose(t.pgB[0:2, 0:8], t.gmax8[:],
                                    t.ident_f32[0:8, 0:8]), C)
    P.chain("scalar", lambda s: s.copy(t.max2x128[:, 0:8], t.pgB[0:2, 0:8]), C)
    P.chain("vector",
            lambda v: v.reduce_max(t.gsc2[:, 0:1], t.max2x128[:, 0:8],
                                   axis=AX.X), C)
    P.chain("vector",
            lambda v: v.tensor_scalar(out=t.gsc2[:, 1:2], in0=t.gsc2[:, 0:1],
                                      scalar1=1e-6, scalar2=None,
                                      op0=ALU.max), C)
    P.chain("sync",
            lambda sy: sy.dma_start(out=gsc_dram[:, :], in_=t.gsc2[:, 1:2]),
            C, inc_n=16)
    P.chain("sync",
            lambda sy: sy.dma_start(
                out=t.gscH[:], in_=gsc_dram[0:1, 0:1].to_broadcast((128, 1))),
            C, inc_n=16)
    P.chain("sync",
            lambda sy: sy.dma_start(
                out=t.gscD[:], in_=gsc_dram[1:2, 0:1].to_broadcast((128, 1))),
            C, inc_n=16)
    P.chain("vector", lambda v: v.tensor_copy(g0[:, 0:1], t.gscH[:]), C)
    emit_recip_refine(P, t, sem, g0, 0, 1)
    P.chain("vector", lambda v: v.tensor_copy(g0[:, 8:9], t.gscD[:]), C)
    emit_recip_refine(P, t, sem, g0, 8, 9)
    P.chain("vector",
            lambda v: v.tensor_scalar(out=g0[:, 16:17], in0=g0[:, 4:5],
                                      scalar1=0.5, scalar2=None,
                                      op0=ALU.mult), C)
    P.chain("vector",
            lambda v: v.tensor_scalar(out=g0[:, 17:18], in0=g0[:, 12:13],
                                      scalar1=0.5, scalar2=None,
                                      op0=ALU.mult), C)

    for aj in range(4):
        gs = t.gsm[1]
        sl = slice(aj * 8, (aj + 1) * 8)
        P.chain("vector",
                (lambda v, gs=gs, sl=sl: v.tensor_scalar(
                    out=gs[:, 16:24], in0=t.dsbuf[:, sl],
                    scalar1=g0[:, 17:18], scalar2=None, op0=ALU.mult)), C)
        P.chain("vector",
                (lambda v, gs=gs, sl=sl: v.scalar_tensor_tensor(
                    out=gs[:, 16:24], in0=t.hsbuf[:, sl], scalar=g0[:, 16:17],
                    in1=gs[:, 16:24], op0=ALU.mult, op1=ALU.add)), C)
        P.chain("vector",
                (lambda v, gs=gs: v.reduce_max(gs[:, 24:25], gs[:, 16:24],
                                               axis=AX.X, negate=True)), C)
        P.chain("scalar",
                (lambda s, gs=gs: s.activation(
                    out=gs[:, 32:40], in_=gs[:, 16:24], func=AF.Exp,
                    bias=gs[:, 24:25], scale=1.0)), C)
        P.chain("vector",
                (lambda v, gs=gs: v.reduce_sum(gs[:, 25:26], gs[:, 32:40],
                                               axis=AX.X)), C)
        emit_recip_refine(P, t, sem, gs, 25, 26)
        P.chain("vector",
                (lambda v, gs=gs: v.tensor_scalar(
                    out=gs[:, 40:48], in0=gs[:, 32:40],
                    scalar1=gs[:, 29:30], scalar2=None, op0=ALU.mult)), C)
        P.chain("vector",
                (lambda v, gs=gs: v.reduce_max(gs[:, 48:49], gs[:, 40:48],
                                               axis=AX.X)), C)
        P.chain("vector",
                (lambda v, gs=gs: v.tensor_scalar(
                    out=gs[:, 56:64], in0=gs[:, 40:48],
                    scalar1=gs[:, 48:49], scalar2=None, op0=ALU.is_ge)), C)
        P.chain("vector",
                (lambda v, gs=gs: v.reduce_sum(gs[:, 49:50], gs[:, 56:64],
                                               axis=AX.X)), C)
        P.chain("vector",
                (lambda v, gs=gs: v.scalar_tensor_tensor(
                    out=gs[:, 56:64], in0=gs[:, 56:64], scalar=-1e30,
                    in1=gs[:, 40:48], op0=ALU.mult, op1=ALU.add)), C)
        P.chain("vector",
                (lambda v, gs=gs: v.reduce_max(gs[:, 50:51], gs[:, 56:64],
                                               axis=AX.X)), C)
        P.chain("vector",
                (lambda v, gs=gs: v.tensor_scalar(
                    out=gs[:, 51:52], in0=gs[:, 49:50], scalar1=2.0,
                    scalar2=None, op0=ALU.is_ge)), C)
        P.chain("vector",
                (lambda v, gs=gs: v.tensor_scalar(
                    out=gs[:, 52:53], in0=gs[:, 48:49],
                    scalar1=gs[:, 50:51], scalar2=None, op0=ALU.subtract)), C)
        P.chain("vector",
                (lambda v, gs=gs: v.tensor_scalar(
                    out=gs[:, 53:54], in0=gs[:, 52:53],
                    scalar1=gs[:, 51:52], scalar2=None, op0=ALU.mult)), C)
        P.chain("vector",
                (lambda v, gs=gs: v.tensor_scalar(
                    out=gs[:, 54:55], in0=gs[:, 53:54],
                    scalar1=gs[:, 50:51], scalar2=None, op0=ALU.add)), C)
        P.chain("vector",
                (lambda v, gs=gs: v.tensor_scalar(
                    out=gs[:, 55:56], in0=gs[:, 54:55], scalar1=-2.0,
                    scalar2=None, op0=ALU.mult)), C)
        P.chain("scalar",
                (lambda s, gs=gs: s.activation(
                    out=gs[:, 56:64], in_=gs[:, 40:48], func=AF.Sigmoid,
                    bias=gs[:, 55:56], scale=t.cst[:, 1:2])), C)
        P.chain("vector",
                (lambda v, gs=gs: v.tensor_tensor(
                    out=gs[:, 40:48], in0=gs[:, 40:48], in1=gs[:, 56:64],
                    op=ALU.mult)), C)
        P.chain("vector",
                (lambda v, gs=gs: v.reduce_sum(gs[:, 25:26], gs[:, 40:48],
                                               axis=AX.X)), C)
        P.chain("vector",
                (lambda v, gs=gs: v.tensor_scalar(
                    out=gs[:, 26:27], in0=gs[:, 25:26], scalar1=1e-9,
                    scalar2=None, op0=ALU.add)), C)
        emit_recip_refine(P, t, sem, gs, 26, 27)
        P.chain("vector",
                (lambda v, gs=gs: v.tensor_scalar(
                    out=gs[:, 40:48], in0=gs[:, 40:48],
                    scalar1=gs[:, 30:31], scalar2=None, op0=ALU.mult)), C)
        P.chain("tensor",
                (lambda te, gs=gs: te.transpose(
                    t.pgB[0:8, 0:128], gs[:, 40:48], t.ident_f32[:])), C)
        P.chain("scalar",
                (lambda s, aj=aj: s.copy(
                    t.wfullT[:, aj * 256:(aj + 1) * 256]
                        .rearrange("p (a two) -> p a two", two=2),
                    t.pgB[0:8, 0:128][:, :, None].broadcast_to([8, 128, 2]))),
                C)
    P.chain("tensor",
            lambda te: te.matmul(t.pgA[0:32, 0:512], t.lim_bf[:],
                                 t.wfullT[:, 0:512], start=True, stop=True), C)
    P.chain("tensor",
            lambda te: te.matmul(t.pgC[0:32, 0:512], t.lim_bf[:],
                                 t.wfullT[:, 512:1024], start=True, stop=True),
            C)
    P.chain("vector",
            lambda v: v.tensor_tensor(out=t.zp_bf[:, 0:512],
                                      in0=t.zT_f32[:, 0:512],
                                      in1=t.pgA[0:32, 0:512], op=ALU.mult), C)
    P.chain("vector",
            lambda v: v.tensor_tensor(out=t.zp_bf[:, 512:1024],
                                      in0=t.zT_f32[:, 512:1024],
                                      in1=t.pgC[0:32, 0:512], op=ALU.mult), C)


def emit_main(P, t, sem, wa_dram, wf_dram, xin, xout, layer, extra_wait=(),
              ring_free_wait=(), x0_first_wait=()):
    for s in range(NSUP):
        for g in range(NG):
            for (ring, wd, wsems, cons_base) in (
                    (t.wring_f, wf_dram, sem.wdf, 0),
                    (t.wring_a, wa_dram, sem.wda, KC)):
                wsem = wsems[g]
                waits = []
                if s == 0:
                    waits.extend(ring_free_wait)
                if s > 0:
                    need = (s - 1) * 4 * KC + 2 * (cons_base + (g + 1) * GRP)
                    waits.append((sem.mmc, need))
                P.op("gpsimd",
                     (lambda gg, ring=ring, wd=wd, s=s, g=g: gg.dma_start(
                         out=ring[:, g * GRP * NW:(g + 1) * GRP * NW]
                             .rearrange("p (c n) -> p c n", c=GRP),
                         in_=wd[g * GRP * 128:(g + 1) * GRP * 128,
                                s * NW:(s + 1) * NW]
                             .rearrange("(c p) n -> p c n", c=GRP))),
                     waits=waits, inc=wsem, inc_n=16)
        for g in range(NG):
            P.op("vector",
                 (lambda v, s=s, g=g: v.tensor_tensor(
                     out=t.wring_a[:, g * GRP * NW:(g + 1) * GRP * NW]
                         .rearrange("p (c a b) -> p c a b", c=GRP, a=8),
                     in0=t.wring_a[:, g * GRP * NW:(g + 1) * GRP * NW]
                         .rearrange("p (c a b) -> p c a b", c=GRP, a=8),
                     in1=t.bmP[:].rearrange("p (c sp a) -> p c sp a", c=KC, sp=16)
                         [:, g * GRP:(g + 1) * GRP, s, :][:, :, :, None]
                         .broadcast_to([128, GRP, 8, 32]),
                     op=ALU.mult)),
                 waits=[(sem.wda[g], P.val(sem.wda[g])),
                        (sem.mevac, P.val(sem.mevac))],
                 inc=sem.wmask)
        P.op("sync",
             (lambda sy, s=s: sy.dma_start(
                 out=t.x0stage[s % 2][:].rearrange("p (tt n) -> p tt n", tt=TT),
                 in_=xin[:, s * NW:(s + 1) * NW]
                     .rearrange("(tt p) n -> p tt n", tt=TT))),
             waits=([(sem.evac, P.val(sem.evac) - TT)] if s >= 2
                    else list(x0_first_wait)),
             inc=sem.x0d, inc_n=16)
        x0v = P.val(sem.x0d)

        for ci in range(2 * KC):
            is_ffn = ci < KC
            c = ci if is_ffn else ci - KC
            ring = t.wring_f if is_ffn else t.wring_a
            g = c // GRP
            waits = []
            if is_ffn:
                waits.append((sem.wdf[g], P.val(sem.wdf[g])))
            else:
                waits.append((sem.wmask, P.val(sem.wmask) - (NG - 1 - g)))
            if ci == 0:
                if s == 0:
                    waits.append((sem.tevac, P.val(sem.tevac)))
                    waits.append((sem.gch, P.val(sem.gch)))
                    waits.extend(extra_wait)
                ev0 = P.val(sem.evac)
                for tt in range(TT):
                    w2 = list(waits)
                    if s > 0:
                        w2.append((sem.evac, ev0 - (TT - 1 - tt)))
                    P.op("tensor",
                         (lambda te, c=c, ring=ring, tt=tt: te.matmul(
                             t.pmain[tt][:, 0:NW],
                             t.xnT[:, c * 1024 + tt * 128:
                                   c * 1024 + (tt + 1) * 128],
                             ring[:, c * NW:(c + 1) * NW],
                             start=True, stop=False)),
                         waits=w2,
                         inc=(sem.mmc if tt == TT - 1 else None), inc_n=2)
                continue

            def _mms(te, c=c, ring=ring):
                last = None
                for tt in range(TT):
                    last = te.matmul(
                        t.pmain[tt][:, 0:NW],
                        t.xnT[:, c * 1024 + tt * 128: c * 1024 + (tt + 1) * 128],
                        ring[:, c * NW:(c + 1) * NW],
                        start=False, stop=False)
                return last
            P.op("tensor", _mms, waits=waits, inc=sem.mmc, inc_n=2)
        for tt in range(TT):
            def _fin(te, tt=tt, s=s, layer=layer):
                pm = t.pmain[tt][:, 0:NW]
                if layer == 0:
                    te.matmul(pm, t.hwT[:, tt * 128:(tt + 1) * 128],
                              t.bm2_bf[:, s * NW:(s + 1) * NW],
                              start=False, stop=False)
                else:
                    te.matmul(pm, t.zp_bf[:, tt * 128:(tt + 1) * 128],
                              t.bmat_bf[:, s * NW:(s + 1) * NW],
                              start=False, stop=False)
                te.matmul(pm, t.ones_f32[0:1, 0:128],
                          t.brows0[0:1, s * NW:(s + 1) * NW],
                          start=False, stop=False)
                return te.matmul(pm, t.zrow_bf[0:1, 0:128],
                                 t.zrow_bf[0:1, 0:NW],
                                 start=False, stop=True)
            P.op("tensor", _fin, waits=[], inc=sem.mmf)
            P.op("vector",
                 (lambda v, tt=tt, s=s: v.tensor_tensor(
                     out=t.ostage[s % 2][:, tt * NW:(tt + 1) * NW],
                     in0=t.pmain[tt][:, 0:NW],
                     in1=t.x0stage[s % 2][:, tt * NW:(tt + 1) * NW],
                     op=ALU.add)),
                 waits=[(sem.mmf, P.val(sem.mmf)), (sem.x0d, x0v)]
                 + ([(sem.ost, P.val(sem.ost))] if s >= 2 else []),
                 inc=sem.evac)
        P.op("sync",
             (lambda sy, s=s: sy.dma_start(
                 out=xout[:, s * NW:(s + 1) * NW]
                     .rearrange("(tt p) n -> p tt n", tt=TT),
                 in_=t.ostage[s % 2][:].rearrange("p (tt n) -> p tt n", tt=TT))),
             waits=[(sem.evac, P.val(sem.evac))], inc=sem.ost, inc_n=16)



def alloc_common(nc, es, t):
    sb = nc.sbuf_tensor
    t.xnT = es.enter_context(sb("xnT", [128, KC * 1024], BF16)).ap()
    t.wring_a = es.enter_context(sb("wring_a", [128, 8192], BF16)).ap()
    t.wring_f = es.enter_context(sb("wring_f", [128, 8192], BF16)).ap()
    xbig = [es.enter_context(sb(f"xbig{i}", [128, D], F32)).ap()
            for i in range(2)]
    t.xstage = xbig
    t.ostage = [xb[:, 0:TT * NW] for xb in xbig]
    t.x0stage = [xb[:, TT * NW:2 * TT * NW] for xb in xbig]
    t.xck = [xb[:, 0:1024] for xb in xbig]
    mbig = es.enter_context(sb("mbig", [128, 2048], F32)).ap()
    t.mbig = mbig
    t.bmQ = es.enter_context(sb("bmQ", [128, 128], F32)).ap()
    t.Qbf = es.enter_context(sb("Qbf", [128, 128], BF16)).ap()
    t.bmP = es.enter_context(sb("bmP", [128, D], BF16)).ap()
    scratch16 = es.enter_context(sb("scratch16", [128, D], F32)).ap()
    t.E32 = scratch16.bitcast(BF16)[:, 0:D]
    t.brows0 = scratch16[0:1, :]
    t.xnbuf = [es.enter_context(sb(f"xnbuf{i}", [128, D], BF16)).ap()
               for i in range(2)]
    t.lnsc = [es.enter_context(sb(f"lnsc{i}", [128, 16], F32)).ap()
              for i in range(2)]
    t.lnsc8 = es.enter_context(sb("lnsc8", [128, 128], F32)).ap()
    t.ones_f32 = es.enter_context(sb("ones_f32", [1, 128], F32)).ap()
    t.ident_bf = es.enter_context(sb("ident_bf", [128, 128], BF16)).ap()
    t.ident_f32 = es.enter_context(sb("ident_f32", [128, 128], F32)).ap()
    t.gsm = [es.enter_context(sb(f"gsm{i}", [128, 64], F32)).ap()
             for i in range(2)]
    t.cst = es.enter_context(sb("cst", [128, 4], F32)).ap()
    t.zrow_bf = es.enter_context(sb("zrow_bf", [1, 512], BF16)).ap()
    t.ones_bf = es.enter_context(sb("ones_bf", [128, 1], BF16)).ap()
    t.mscat = es.enter_context(sb("mscat", [1, 1024], F32)).ap()
    t.bm2_bf = t.mbig.bitcast(BF16)[:, 0:D]
    t.bmat_bf = t.mbig.bitcast(BF16)[0:32, 0:D]
    t.ah8_bf = es.enter_context(sb("ah8_bf", [128, KC * 40], BF16)).ap()
    t.limt_f32 = es.enter_context(sb("limt_f32", [32, 8], F32)).ap()
    t.limt_bf = es.enter_context(sb("limt_bf", [32, 8], BF16)).ap()
    t.b8col = es.enter_context(sb("b8col", [8, 1], F32)).ap()


def alloc_psums(nc, es, t):
    ps = nc.psum_tensor
    banks = [es.enter_context(ps(f"bank{i}", [128, 512], F32)).ap()
             for i in range(8)]
    t.pmain = banks
    t.pgA, t.pgB, t.pgC, t.pgD = banks[4], banks[5], banks[6], banks[7]


class SemNames:
    names = ("consts", "mred", "mpe", "mevac", "sq", "xn", "xns", "tpe",
             "tevac", "gw", "gqw", "gch", "gld", "bld", "wmask", "mmc", "mmf",
             "x0d", "evac", "ost", "fin", "lnch", "sld", "bsum", "zdone")
    arrays = {"mdma": 4, "xdma": 2, "wda": NG, "wdf": NG, "msc": 2}


def make_sems(nc, es):
    sem = NS()
    for n in SemNames.names:
        setattr(sem, n, es.enter_context(nc.semaphore(n)))
    for n, k in SemNames.arrays.items():
        setattr(sem, n, [es.enter_context(nc.semaphore(f"{n}{i}"))
                         for i in range(k)])
    return sem



def build_launch_a():
    nc = bass.Bass("TRN2", target_bir_lowering=False, debug=False)
    dt = nc.dram_tensor
    x0 = dt("x0", [TOK, D], F32, kind="ExternalInput").ap()
    wt_l0a = dt("wt_l0a", [D, D], F32, kind="ExternalInput").ap()
    wt_l0f = dt("wt_l0f", [D, D], F32, kind="ExternalInput").ap()
    maskt0 = dt("maskt0", [1024, 16384], I32, kind="ExternalInput").ap()
    wqk = dt("wqk", [D, 512], F32, kind="ExternalInput").ap()
    a2t = dt("a2t", [D, 128], F32, kind="ExternalInput").ap()
    bm2 = dt("bm2", [128, D], F32, kind="ExternalInput").ap()
    b_l0a = dt("b_l0a", [1, D], F32, kind="ExternalInput").ap()
    b_l0f = dt("b_l0f", [1, D], F32, kind="ExternalInput").ap()
    ah8 = dt("ah8", [D, 40], F32, kind="ExternalInput").ap()
    b_l1f = dt("b_l1f", [1, D], F32, kind="ExternalInput").ap()
    limt = dt("limt", [32, 8], F32, kind="ExternalInput").ap()
    e32h = dt("e32h", [128, D], F32, kind="ExternalInput").ap()

    x1 = dt("x1", [TOK, D], F32, kind="ExternalOutput").ap()
    locmax = dt("locmax", [2, 1], F32, kind="ExternalOutput").ap()
    bmq_d = dt("bmq_d", [128, 128], F32).ap()

    es = contextlib.ExitStack()
    t = NS()
    alloc_common(nc, es, t)
    t.bmq_d = bmq_d
    sb = nc.sbuf_tensor
    t.a2t_bf = es.enter_context(sb("a2t_bf", [128, D], BF16)).ap()
    t.brows1 = t.xstage[0][0:1, :]
    t.gq = es.enter_context(sb("gq", [128, 256], F32)).ap()
    t.gqk = es.enter_context(sb("gqk", [128, 256], F32)).ap()
    t.hw_tok = es.enter_context(sb("hw_tok", [128, 128], BF16)).ap()
    t.hwT = es.enter_context(sb("hwT", [128, 1024], BF16)).ap()
    t.locmax = es.enter_context(sb("locmax_sb", [2, 1], F32)).ap()
    bmPf = t.bmP.bitcast(F32)
    t.zh_tok = bmPf[:, 0:40]
    t.zTa_bf = t.bmP[0:32, 2048:2560]
    t.h8Ta = bmPf[0:8, 64:576]
    t.dsbuf = bmPf[:, 1024:1056]
    t.hsbuf = bmPf[:, 1056:1088]
    t.max2x128 = bmPf[0:2, 1100:1228]
    alloc_psums(nc, es, t)
    sem = make_sems(nc, es)
    block = es.enter_context(nc.Block())

    with es:
        P = Prog()
        emit_consts(P, t, sem)
        P.op("gpsimd",
             lambda g: g.dma_start(
                 out=t.ah8_bf[:].rearrange("p (c n) -> p c n", c=KC),
                 in_=ah8[:, :].rearrange("(c p) n -> p c n", c=KC)),
             inc=sem.gqw, inc_n=16)
        for bt in range(4):
            emit_mask2_group(P, t, sem, maskt0, bt)
        P.op("gpsimd",
             lambda g: g.dma_start(
                 out=t.wring_a[:].rearrange("p (c n) -> p c n", c=16),
                 in_=wqk[0:2048, :].rearrange("(c p) n -> p c n", c=16)),
             inc=sem.gqw, inc_n=16)
        P.op("gpsimd",
             lambda g: g.dma_start(
                 out=t.wring_f[:].rearrange("p (c n) -> p c n", c=16),
                 in_=wqk[2048:4096, :].rearrange("(c p) n -> p c n", c=16)),
             inc=sem.gqw, inc_n=16)
        P.op("gpsimd",
             lambda g: g.dma_start(
                 out=t.E32[:].rearrange("p (c n) -> p c n", c=1),
                 in_=e32h[:, :].rearrange("(c p) n -> p c n", c=1)),
             inc=sem.gqw, inc_n=16)
        P.op("gpsimd",
             lambda g: g.dma_start(
                 out=t.a2t_bf[:, 0:D].rearrange("p (c n) -> p c n", c=KC),
                 in_=a2t[:, :].rearrange("(c p) n -> p c n", c=KC)),
             inc=sem.gw, inc_n=16)
        P.op("sync", lambda sy: sy.dma_start(out=t.limt_f32[:], in_=limt[:, :]),
             inc=sem.sld, inc_n=16)
        P.op("sync",
             lambda sy: sy.dma_start(
                 out=t.b8col[:], in_=b_l1f[0:1, 0:8]
                     .rearrange("x (p y) -> p (x y)", p=8)),
             inc=sem.sld, inc_n=16)
        P.chain("scalar", lambda s: s.copy(t.limt_bf[:], t.limt_f32[:]),
                sem.gch, waits=[(sem.sld, P.val(sem.sld))])

        st = NS()
        st.rel, st.xnv, st.xnsv, st.tpev, st.tev = {}, {}, {}, {}, {}
        for tt in range(TT):
            emit_ln_stats(P, t, sem, x0[tt * 128:(tt + 1) * 128, :], tt, st)
        st.statsL = P.val(sem.lnch)
        nbt = 4
        for c in range(KC):
            emit_ln_chunk(P, t, sem, x0, c, st)
            if c in (10, 16, 22, 28):
                emit_mask2_group(P, t, sem, maskt0, nbt)
                nbt += 1
        fp = P.val(sem.gch)
        for pr in range(4):
            fp = emit_qk_pair(P, t, sem, pr, fp)
            emit_mask2_group(P, t, sem, maskt0, nbt)
            nbt += 1
        ring_free = P.val(sem.gch)

        def _grp():
            nonlocal nbt
            emit_mask2_group(P, t, sem, maskt0, nbt)
            nbt += 1
        emit_hw_phase(P, t, sem, hooks={1: _grp, 3: _grp, 5: _grp, 7: _grp})
        emit_bmp(P, t, sem)
        P.op("gpsimd",
             lambda g: g.dma_start(
                 out=t.bm2_bf[:].rearrange("p (c n) -> p c n", c=1),
                 in_=bm2[:, :].rearrange("(c p) n -> p c n", c=1)),
             waits=[(sem.mpe, P.val(sem.mpe))], inc=sem.gw, inc_n=16)
        P.op("sync", lambda sy: sy.dma_start(out=t.brows0[0:1, :], in_=b_l0a[:, :]),
             waits=[(sem.mpe, P.val(sem.mpe))], inc=sem.bld, inc_n=16)
        P.op("sync", lambda sy: sy.dma_start(out=t.brows1[0:1, :], in_=b_l0f[:, :]),
             inc=sem.bld, inc_n=16)
        P.op("vector",
             lambda v: v.tensor_tensor(out=t.brows0[0:1, :], in0=t.brows0[0:1, :],
                                       in1=t.brows1[0:1, :], op=ALU.add),
             waits=[(sem.bld, P.val(sem.bld))], inc=sem.bsum)
        emit_main(P, t, sem, wt_l0a, wt_l0f, x0, x1, layer=0,
                  extra_wait=[(sem.bld, P.val(sem.bld)), (sem.bsum, 1),
                              (sem.gw, P.val(sem.gw)),
                              (sem.mevac, P.val(sem.mevac))],
                  ring_free_wait=[(sem.gch, ring_free)],
                  x0_first_wait=[(sem.sq, 8), (sem.lnch, st.statsL),
                                 (sem.bsum, 1)])
        P.op("sync", None, waits=[(sem.ost, P.val(sem.ost))], inc=sem.fin)
        finv = P.val(sem.fin)
        for e in ("scalar", "vector", "tensor", "gpsimd"):
            P.op(e, None, waits=[(sem.fin, finv)])
        x1odd = x1[1::2, :]

        def slot_a(c, at):
            if c < 16:
                return t.wring_a[:, c * 512 + at * 128: c * 512 + (at + 1) * 128]
            return t.wring_f[:, (c - 16) * 512 + at * 128:
                             (c - 16) * 512 + (at + 1) * 128]
        t.zsrc = slot_a
        for at in range(4):
            emit_ln_tile(P, t, sem, x1odd[at * 128:(at + 1) * 128, :], at % 2,
                         (lambda c, at=at: slot_a(c, at)),
                         first_waits=[(sem.fin, finv)])
            emit_zpre_tile(P, t, sem, at, t.zTa_bf, None, t.h8Ta)
        emit_anchor_gate(P, t, sem, t.zTa_bf, t.h8Ta, t.limt_bf[:])
        emit_locmax(P, t, sem)
        P.op("sync", lambda sy: sy.dma_start(out=locmax[:, :], in_=t.locmax[:]),
             waits=[(sem.gch, P.val(sem.gch))], inc=sem.ost, inc_n=16)
        P.op("sync", None, waits=[(sem.ost, P.val(sem.ost))])

        for name in ENGS:
            getattr(block, name)(lambda eng, name=name: P.replay(name, eng))
    return nc


def build_launch_b():
    nc = bass.Bass("TRN2", target_bir_lowering=False, debug=False)
    dt = nc.dram_tensor
    x1 = dt("x1", [TOK, D], F32, kind="ExternalInput").ap()
    gmax8 = dt("gmax8", [8, 2], F32, kind="ExternalInput").ap()
    wt_l1a = dt("wt_l1a", [D, D], F32, kind="ExternalInput").ap()
    wt_l1f = dt("wt_l1f", [D, D], F32, kind="ExternalInput").ap()
    maskt1 = dt("maskt1", [1024, 16384], I32, kind="ExternalInput").ap()
    ah8 = dt("ah8", [D, 40], F32, kind="ExternalInput").ap()
    limt = dt("limt", [32, 8], F32, kind="ExternalInput").ap()
    lim = dt("lim", [8, 32], F32, kind="ExternalInput").ap()
    bmat = dt("bmat", [32, D], F32, kind="ExternalInput").ap()
    b_l1f = dt("b_l1f", [1, D], F32, kind="ExternalInput").ap()
    e32h = dt("e32h", [128, D], F32, kind="ExternalInput").ap()

    y = dt("y", [TOK, D], F32, kind="ExternalOutput").ap()
    gsc_dram = dt("gsc_scratch", [2, 1], F32).ap()
    bmq_d = dt("bmq_d", [128, 128], F32).ap()

    es = contextlib.ExitStack()
    t = NS()
    alloc_common(nc, es, t)
    t.bmq_d = bmq_d
    sb = nc.sbuf_tensor
    t.zp_bf = es.enter_context(sb("zp_bf", [32, 1024], BF16)).ap()
    t.zT_f32 = es.enter_context(sb("zT_f32", [32, 1024], F32)).ap()
    t.zT_bf = es.enter_context(sb("zT_bf", [32, 1024], BF16)).ap()
    t.h8T = es.enter_context(sb("h8T", [8, 1024], F32)).ap()
    t.zh_tok = es.enter_context(sb("zh_tok", [128, 40], F32)).ap()
    t.dsbuf = es.enter_context(sb("dsbuf", [128, 32], F32)).ap()
    t.hsbuf = es.enter_context(sb("hsbuf", [128, 32], F32)).ap()
    xb0, xb1 = t.xstage[0], t.xstage[1]
    t.wfullT = xb0.bitcast(BF16)[0:8, 5120:6144]
    t.max2x128 = xb1[0:2, 1064:1192]
    t.gmax8 = xb1[0:8, 1192:1194]
    t.gsc2 = xb1[0:2, 1200:1202]
    t.gscH = xb1[:, 1208:1209]
    t.gscD = xb1[:, 1209:1210]
    t.lim_f32 = xb1[0:8, 1216:1248]
    t.lim_bf = xb1.bitcast(BF16)[0:8, 2496:2528]
    alloc_psums(nc, es, t)
    sem = make_sems(nc, es)
    block = es.enter_context(nc.Block())

    with es:
        P = Prog()
        emit_consts(P, t, sem)
        P.op("gpsimd",
             lambda g: g.dma_start(
                 out=t.ah8_bf[:].rearrange("p (c n) -> p c n", c=KC),
                 in_=ah8[:, :].rearrange("(c p) n -> p c n", c=KC)),
             inc=sem.gqw, inc_n=16)
        for bt in range(4):
            emit_mask2_group(P, t, sem, maskt1, bt)
        P.op("gpsimd",
             lambda g: g.dma_start(
                 out=t.E32[:].rearrange("p (c n) -> p c n", c=1),
                 in_=e32h[:, :].rearrange("(c p) n -> p c n", c=1)),
             inc=sem.gqw, inc_n=16)
        P.op("sync", lambda sy: sy.dma_start(out=t.limt_f32[:], in_=limt[:, :]),
             inc=sem.sld, inc_n=16)
        sldv = P.val(sem.sld)
        P.op("sync",
             lambda sy: sy.dma_start(
                 out=t.b8col[:], in_=b_l1f[0:1, 0:8]
                     .rearrange("x (p y) -> p (x y)", p=8)),
             inc=sem.sld, inc_n=16)
        b8v = P.val(sem.sld)
        P.chain("scalar", lambda s: s.copy(t.limt_bf[:], t.limt_f32[:]),
                sem.gch, waits=[(sem.sld, b8v)])

        st = NS()
        st.rel, st.xnv, st.xnsv, st.tpev, st.tev = {}, {}, {}, {}, {}
        for tt in range(TT):
            emit_ln_stats(P, t, sem, x1[tt * 128:(tt + 1) * 128, :], tt, st)
        st.statsL = P.val(sem.lnch)
        nbt = 4
        for c in range(KC):
            emit_ln_chunk(P, t, sem, x1, c, st)
            if c >= 1:
                emit_z_chunk_mm(P, t, sem, c - 1, st, first=(c == 1))
            if c in (10, 16, 22, 28):
                emit_mask2_group(P, t, sem, maskt1, nbt)
                nbt += 1
        emit_z_chunk_mm(P, t, sem, KC - 1, st, first=False)
        emit_z_close(P, t, sem)
        for tt in range(TT):
            emit_z_evac(P, t, sem, tt, t.zT_bf, t.zT_f32, t.h8T, b8v)
        while nbt < 16:
            emit_mask2_group(P, t, sem, maskt1, nbt)
            nbt += 1
        emit_bmp(P, t, sem)
        emit_anchor_gate(P, t, sem, t.zT_bf[:, 1::2], t.h8T[:, 1::2],
                         t.limt_bf[:])
        emit_l1_post(P, t, sem, gmax8, gsc_dram, bmat, lim)
        P.op("sync", lambda sy: sy.dma_start(out=t.brows0[0:1, :], in_=b_l1f[:, :]),
             waits=[(sem.mpe, P.val(sem.mpe))], inc=sem.bld, inc_n=16)
        emit_main(P, t, sem, wt_l1a, wt_l1f, x1, y, layer=1,
                  extra_wait=[(sem.bld, P.val(sem.bld)),
                              (sem.mevac, P.val(sem.mevac))],
                  x0_first_wait=[(sem.gch, P.val(sem.gch)), (sem.sq, 8),
                                 (sem.lnch, st.statsL)])
        P.op("sync", None, waits=[(sem.ost, P.val(sem.ost))])

        for name in ENGS:
            getattr(block, name)(lambda eng, name=name: P.replay(name, eng))
    return nc



_NCA = None
_NCB = None


def _e32_host():
    e = np.zeros((128, D), np.float32)
    e[np.arange(D) // 32, np.arange(D)] = 1.0
    return e


def kernel(L0_attn_data, L0_attn_mask, L0_attn_weight, L0_attn_bias,
           L0_ffn_W, L0_ffn_b, L0_ffn_Wq, L0_ffn_Wk, L0_ffn_A, L0_ffn_Bm,
           L1_attn_weight, L1_attn_mask, L1_ffn_W, L1_ffn_b, L1_ffn_A,
           L1_ffn_Bmat, L1_ffn_LiMEs):
    global _NCA, _NCB
    f32 = np.float32
    cat = np.ascontiguousarray

    x0 = np.asarray(L0_attn_data, f32).reshape(NCORES * TOK, D)
    wt_l0a = cat(np.asarray(L0_attn_weight, f32).T)
    wt_l0f = cat(np.asarray(L0_ffn_W, f32).T)
    wt_l1a = cat(np.asarray(L1_attn_weight, f32).T)
    wt_l1f = cat(np.asarray(L1_ffn_W, f32).T)

    def _maskT2(m):
        return cat(np.asarray(m, np.int32).reshape(128, 32, 128, 32)
                   .transpose(1, 3, 2, 0).reshape(1024, 16384))
    maskt0 = _maskT2(L0_attn_mask)
    maskt1 = _maskT2(L1_attn_mask)
    wqk = cat(np.concatenate([np.asarray(L0_ffn_Wq, f32).T,
                              np.asarray(L0_ffn_Wk, f32).T], axis=1))
    a2t = cat(np.asarray(L0_ffn_A, f32).reshape(128, D).T)
    bm2 = cat(np.asarray(L0_ffn_Bm, f32).transpose(0, 2, 1).reshape(128, D))
    b_l0a = cat(np.asarray(L0_attn_bias, f32)[None, :])
    b_l0f = cat(np.asarray(L0_ffn_b, f32)[None, :])
    ah8 = cat(np.concatenate([np.asarray(L1_ffn_A, f32),
                              np.asarray(L1_ffn_W, f32)[0:8, :].T], axis=1))
    b_l1f = cat(np.asarray(L1_ffn_b, f32)[None, :])
    lim = cat(np.asarray(L1_ffn_LiMEs, f32))
    limt = cat(lim.T)
    bmat = cat(np.asarray(L1_ffn_Bmat, f32))
    e32h = _e32_host()

    if _NCA is None:
        _NCA = build_launch_a()
    in_maps_a = []
    for c in range(NCORES):
        in_maps_a.append(dict(
            x0=cat(x0[c * TOK:(c + 1) * TOK]), wt_l0a=wt_l0a, wt_l0f=wt_l0f,
            maskt0=maskt0, wqk=wqk, a2t=a2t, bm2=bm2, b_l0a=b_l0a,
            b_l0f=b_l0f, ah8=ah8, b_l1f=b_l1f, limt=limt, e32h=e32h))
    res_a = run_bass_kernel_spmd(_NCA, in_maps_a, core_ids=list(range(NCORES)))

    gmax8 = np.concatenate([r["locmax"].reshape(1, 2) for r in res_a.results],
                           axis=0)

    if _NCB is None:
        _NCB = build_launch_b()
    in_maps_b = []
    for c in range(NCORES):
        in_maps_b.append(dict(
            x1=res_a.results[c]["x1"], gmax8=gmax8, wt_l1a=wt_l1a,
            wt_l1f=wt_l1f, maskt1=maskt1, ah8=ah8, limt=limt, lim=lim,
            bmat=bmat, b_l1f=b_l1f, e32h=e32h))
    res_b = run_bass_kernel_spmd(_NCB, in_maps_b, core_ids=list(range(NCORES)))

    y = np.concatenate([r["y"] for r in res_b.results], axis=0)
    return y.reshape(2, NCORES * TOK // 2, D)


# version 14
# speedup vs baseline: 1.3026x; 1.0531x over previous
import contextlib
import numpy as np
import ml_dtypes
import concourse.bass as bass
import concourse.mybir as mybir
from concourse.bass_utils import run_bass_kernel_spmd

F32 = mybir.dt.float32
BF16 = mybir.dt.bfloat16
I32 = mybir.dt.int32
AF = mybir.ActivationFunctionType
ALU = mybir.AluOpType
AX = mybir.AxisListType

NCORES = 8
TOK = 1024
D = 4096
KC = 32
TT = 8
NSUP = 16
NW = 256
GRP = 8
NG = KC // GRP
INV_SQRT_DK = 0.17677669529663687
LN_EPS = 1e-5

ENGS = ("sync", "gpsimd", "scalar", "vector", "tensor")


class Prog:

    def __init__(self):
        self.ops = {e: [] for e in ENGS}
        self.cnt = {}

    def val(self, sem):
        return self.cnt.get(sem, 0)

    def op(self, eng, build, waits=(), inc=None, inc_n=1):
        tgt = None
        if inc is not None:
            self.cnt[inc] = self.cnt.get(inc, 0) + inc_n
            tgt = self.cnt[inc]
        self.ops[eng].append((tuple(waits), build, inc, inc_n))
        return tgt

    def chain(self, eng, build, sem, inc_n=1, waits=()):
        return self.op(eng, build, waits=[(sem, self.val(sem))] + list(waits),
                       inc=sem, inc_n=inc_n)

    def replay(self, eng_name, eng):
        for waits, build, inc, inc_n in self.ops[eng_name]:
            for sem, v in waits:
                if v > 0:
                    eng.wait_ge(sem, v)
            inst = build(eng) if build is not None else None
            if inc is not None:
                if inst is None:
                    inst = eng.nop()
                inst.then_inc(inc, inc_n)


class NS:
    pass



def emit_consts(P, t, sem):
    C = sem.consts

    def _ident_ops(ap, n):
        yield lambda g: g.memset(ap, 0.0)
        yield lambda g: g.affine_select(out=ap, in_=ap,
                                        compare_op=ALU.not_equal, fill=1.0,
                                        base=0, pattern=[[-1, n]],
                                        channel_multiplier=1)

    ops = []
    ops += list(_ident_ops(t.ident_bf[:], 128))
    ops += list(_ident_ops(t.ident_f32[:], 128))
    ops.append(lambda g: g.memset(t.ones_f32[:], 1.0))
    ops.append(lambda g: g.memset(t.zrow_bf[:], 0.0))
    ops.append(lambda g: g.memset(t.ones_bf[:], 1.0))
    ops.append(lambda g: g.memset(t.onesrow_bf[:], 1.0))
    ops.append(lambda g: g.memset(t.cst[:, 0:1], INV_SQRT_DK))
    ops.append(lambda g: g.memset(t.cst[:, 1:2], 2.0))
    ops.append(lambda g: g.memset(t.cst[:, 2:3], 0.5))
    for f in ops:
        P.chain("gpsimd", f, C)


def emit_mask2_group(P, t, sem, maskT2, bt):
    mt = [t.mbig.bitcast(BF16)[:, i * 1024:(i + 1) * 1024] for i in range(4)]
    strip_banks = [t.pgA, t.pgB]
    for qc in range(8):
        i = bt * 8 + qc
        par = i % 4
        waits = []
        if i >= 4:
            waits.append((sem.mpe, P.val(sem.mpe) - 3))
        P.op("gpsimd",
             (lambda g, bt=bt, qc=qc, par=par: g.dma_start(
                 out=mt[par][:],
                 in_=maskT2[qc * 128:(qc + 1) * 128,
                            bt * 1024:(bt + 1) * 1024])),
             waits=waits, inc=sem.mdma[par], inc_n=16)
        dmav = P.val(sem.mdma[par])

        def _mm(te, bt=bt, qc=qc, par=par):
            last = None
            for j in range(2):
                last = te.matmul(
                    strip_banks[j][0:1, 0:512],
                    t.ones_bf[:, 0:1],
                    mt[par][:, j * 512:(j + 1) * 512],
                    start=(qc == 0), stop=(qc == 7))
            return last
        mmw = [(sem.mdma[par], dmav)]
        if qc == 0:
            mmw.append((sem.consts, P.val(sem.consts)))
            if bt >= 1:
                mmw.append((sem.mevac, P.val(sem.mevac)))
        P.op("tensor", _mm, waits=mmw, inc=sem.mpe)
    for j in range(2):
        sidx = 2 * bt + j
        par = sidx % 2
        waits = [(sem.mpe, P.val(sem.mpe))]
        if sidx >= 2:
            waits.append((sem.msc[par], P.val(sem.msc[par])))
        P.op("scalar",
             (lambda s, j=j, par=par: s.copy(
                 t.mscat[0:1, par * 512:(par + 1) * 512],
                 strip_banks[j][0:1, 0:512])),
             waits=waits, inc=sem.mevac)
        P.op("gpsimd",
             (lambda g, sidx=sidx, par=par: g.dma_start(
                 out=t.bmq_d[4 * sidx:4 * (sidx + 1), :]
                     .rearrange("(x a) n -> x a n", x=1),
                 in_=t.mscat[0:1, par * 512:(par + 1) * 512]
                     .rearrange("x (a n) -> x a n", a=4))),
             waits=[(sem.mevac, P.val(sem.mevac))],
             inc=sem.msc[par], inc_n=16)


def emit_bmp(P, t, sem):
    P.op("gpsimd", lambda g: g.dma_start(out=t.bmQ[:], in_=t.bmq_d[:, :]),
         waits=[(sem.msc[0], P.val(sem.msc[0])),
                (sem.msc[1], P.val(sem.msc[1]))],
         inc=sem.msc[0], inc_n=16)
    P.op("vector",
         lambda v: v.tensor_scalar(out=t.Qbf[:], in0=t.bmQ[:], scalar1=1.0,
                                   scalar2=None, op0=ALU.is_ge),
         waits=[(sem.msc[0], P.val(sem.msc[0]))],
         inc=sem.mred)
    qready = P.val(sem.mred)
    for c in range(KC):
        waits = [(sem.mred, qready)]
        if c < 2:
            waits.append((sem.gqw, P.val(sem.gqw)))
        if c >= 2:
            waits.append((sem.mevac, P.val(sem.mevac) - 1))
        psq = t.pgC if c % 2 == 0 else t.pgD
        P.op("tensor",
             (lambda te, c=c, psq=psq: te.matmul(
                 psq[:, 0:128], t.E32[:, c * 128:(c + 1) * 128],
                 t.Qbf[:], start=True, stop=True)),
             waits=waits, inc=sem.mpe)
        P.op("scalar",
             (lambda s, c=c, psq=psq: s.copy(
                 t.bmP[:, c * 128:(c + 1) * 128], psq[:, 0:128])),
             waits=[(sem.mpe, P.val(sem.mpe))], inc=sem.mevac)


def emit_ln_stats(P, t, sem, xin_tile, tt, st):
    L = sem.lnch
    par = tt % 2
    ls = t.lnsc8[:, tt * 16:(tt + 1) * 16]
    P.op("sync",
         (lambda sy, par=par, xin_tile=xin_tile: sy.dma_start(
             out=t.xstage[par][:], in_=xin_tile)),
         waits=st.rel.get(par, []), inc=sem.xdma[par], inc_n=16)
    ld = (sem.xdma[par], P.val(sem.xdma[par]))
    sqv = P.op("scalar",
               (lambda s, par=par, ls=ls: s.activation(
                   out=t.xnbuf[par][:], in_=t.xstage[par][:], func=AF.Square,
                   accum_out=ls[:, 1:2])),
               waits=[ld], inc=sem.sq)
    P.op("vector",
         (lambda v, par=par, ls=ls: v.reduce_sum(
             ls[:, 0:1], t.xstage[par][:], axis=AX.X)),
         waits=[ld, (L, P.val(L))], inc=L)
    st.rel[par] = [(sem.sq, sqv), (L, P.val(L))]
    P.chain("vector",
            (lambda v, ls=ls: v.tensor_scalar(
                out=ls[:, 2:4], in0=ls[:, 0:2], scalar1=1.0 / D,
                scalar2=None, op0=ALU.mult)), L,
            waits=[(sem.sq, sqv)])
    P.chain("vector",
            (lambda v, ls=ls: v.tensor_scalar(
                out=ls[:, 4:5], in0=ls[:, 2:3], scalar1=ls[:, 2:3],
                scalar2=None, op0=ALU.mult)), L)
    P.chain("vector",
            (lambda v, ls=ls: v.tensor_scalar(
                out=ls[:, 5:6], in0=ls[:, 3:4], scalar1=ls[:, 4:5],
                scalar2=LN_EPS, op0=ALU.subtract, op1=ALU.add)), L)
    P.chain("scalar",
            (lambda s, ls=ls: s.activation(out=ls[:, 6:7], in_=ls[:, 5:6],
                                           func=AF.Sqrt)), L)
    P.chain("vector",
            (lambda v, ls=ls: v.reciprocal(ls[:, 10:11], ls[:, 6:7])), L)
    P.chain("vector",
            (lambda v, ls=ls: v.tensor_scalar(
                out=ls[:, 11:12], in0=ls[:, 2:3], scalar1=ls[:, 10:11],
                scalar2=-1.0, op0=ALU.mult, op1=ALU.mult)), L)


def emit_ln_chunk(P, t, sem, xin, c, st):
    par = c % 2
    waits = []
    if c < 2:
        waits += st.rel[par]
    else:
        waits.append((sem.xn, st.xnv[c - 2]))
        waits.append((sem.xns, st.xnsv[c - 2]))
    P.op("sync",
         (lambda sy, par=par, c=c, xin=xin: sy.dma_start(
             out=t.xck[par][:].rearrange("p (tt n) -> p tt n", tt=TT),
             in_=xin[:, c * 128:(c + 1) * 128]
                 .rearrange("(tt p) n -> p tt n", tt=TT))),
         waits=waits, inc=sem.xdma[par], inc_n=16)
    dv = (sem.xdma[par], P.val(sem.xdma[par]))
    tvals = []
    for tt in range(TT):
        ls = t.lnsc8[:, tt * 16:(tt + 1) * 16]
        w = [dv]
        if c < 2:
            w.append((sem.lnch, st.statsL))
        else:
            w.append((sem.tpe, st.tpev[c - 2]))
        if tt < 4:
            v = P.op("vector",
                     (lambda v, par=par, tt=tt, ls=ls: v.tensor_scalar(
                         out=t.xnbuf[par][:, tt * 128:(tt + 1) * 128],
                         in0=t.xck[par][:, tt * 128:(tt + 1) * 128],
                         scalar1=ls[:, 10:11], scalar2=ls[:, 11:12],
                         op0=ALU.mult, op1=ALU.add)),
                     waits=w, inc=sem.xn)
            tvals.append((sem.xn, v))
        else:
            v = P.op("scalar",
                     (lambda s, par=par, tt=tt, ls=ls: s.activation(
                         out=t.xnbuf[par][:, tt * 128:(tt + 1) * 128],
                         in_=t.xck[par][:, tt * 128:(tt + 1) * 128],
                         func=AF.Identity, scale=ls[:, 10:11],
                         bias=ls[:, 11:12])),
                     waits=w, inc=sem.xns)
            tvals.append((sem.xns, v))
    st.xnv[c] = P.val(sem.xn)
    st.xnsv[c] = P.val(sem.xns)
    bank = t.pmain[par].bitcast(BF16)
    for tt in range(TT):
        w = [tvals[tt]]
        if c < 2:
            w.append((sem.consts, P.val(sem.consts)))
        else:
            w.append((sem.tevac, st.tev[c - 2]))
        P.op("tensor",
             (lambda te, par=par, tt=tt, bank=bank: te.transpose(
                 bank[:, tt * 128:(tt + 1) * 128],
                 t.xnbuf[par][:, tt * 128:(tt + 1) * 128], t.ident_bf[:])),
             waits=w, inc=sem.tpe)
    st.tpev[c] = P.val(sem.tpe)
    P.op("scalar",
         (lambda s, c=c, bank=bank: s.copy(
             t.xnT[:, c * 1024:(c + 1) * 1024], bank[:, 0:1024])),
         waits=[(sem.tpe, P.val(sem.tpe))], inc=sem.tevac)
    st.tev[c] = P.val(sem.tevac)


def emit_qk_pair(P, t, sem, pr, free_prev):
    C = sem.gch
    gs = t.gsm[0]

    def _qk(te, pr=pr):
        last = None
        for c in range(KC):
            w = t.wring_a if c < 16 else t.wring_f
            cc = c % 16
            for i in range(2):
                tt = 2 * pr + i
                last = te.matmul(
                    t.pmain[2 + i][:, 0:512],
                    t.xnT[:, c * 1024 + tt * 128: c * 1024 + (tt + 1) * 128],
                    w[:, cc * 512:(cc + 1) * 512],
                    start=(c == 0), stop=(c == KC - 1))
        return last
    qv = P.op("tensor", _qk,
              waits=[(C, free_prev), (sem.gqw, P.val(sem.gqw)),
                     (sem.tevac, P.val(sem.tevac))],
              inc=sem.qkp)
    bases = (0, 32)
    for i in range(2):
        bank, base = t.pmain[2 + i], bases[i]
        P.chain("scalar",
                (lambda s, bank=bank: s.activation(
                    out=t.gq[:], in_=bank[:, 0:256], func=AF.Identity,
                    scale=t.cst[:, 0:1])), C,
                waits=([(sem.qkp, qv)] if i == 0 else []))
        P.chain("vector",
                (lambda v, bank=bank: v.tensor_tensor(
                    out=t.gqk[:], in0=bank[:, 256:512], in1=t.gq[:],
                    op=ALU.mult)), C)
        P.chain("vector",
                (lambda v, gs=gs, base=base: v.tensor_reduce(
                    out=gs[:, base:base + 8][:, :, None],
                    in_=t.gqk[:].rearrange("p (a b) -> p a b", a=8),
                    axis=AX.X, op=ALU.add)), C)
    free_val = P.val(C)
    for i in range(2):
        base, tt = bases[i], 2 * pr + i
        P.chain("vector",
                (lambda v, gs=gs, base=base: v.reduce_max(
                    gs[:, base + 8:base + 9], gs[:, base:base + 8],
                    axis=AX.X, negate=True)), C)
        P.chain("scalar",
                (lambda s, gs=gs, base=base: s.activation(
                    out=gs[:, base + 16:base + 24], in_=gs[:, base:base + 8],
                    func=AF.Exp, bias=gs[:, base + 8:base + 9], scale=1.0)), C)
        P.chain("vector",
                (lambda v, gs=gs, base=base: v.reduce_sum(
                    gs[:, base + 9:base + 10], gs[:, base + 16:base + 24],
                    axis=AX.X)), C)
        P.chain("vector",
                (lambda v, gs=gs, base=base: v.reciprocal(
                    gs[:, base + 13:base + 14], gs[:, base + 9:base + 10])), C)
        P.chain("vector",
                (lambda v, gs=gs, base=base, tt=tt: v.tensor_scalar(
                    out=t.gsm[1][:, 8 * tt:8 * (tt + 1)],
                    in0=gs[:, base + 16:base + 24],
                    scalar1=gs[:, base + 13:base + 14], scalar2=None,
                    op0=ALU.mult)), C)
    return free_val


def emit_recip_refine(P, t, sem, gs, src_col, dst_base):
    C = sem.gch
    P.chain("vector",
            (lambda v: v.reciprocal(gs[:, dst_base:dst_base + 1],
                                    gs[:, src_col:src_col + 1])), C)
    P.chain("vector",
            (lambda v: v.tensor_scalar(
                out=gs[:, dst_base + 1:dst_base + 2],
                in0=gs[:, src_col:src_col + 1],
                scalar1=gs[:, dst_base:dst_base + 1], scalar2=None,
                op0=ALU.mult)), C)
    P.chain("vector",
            (lambda v: v.tensor_scalar(
                out=gs[:, dst_base + 2:dst_base + 3],
                in0=gs[:, dst_base + 1:dst_base + 2],
                scalar1=-1.0, scalar2=2.0, op0=ALU.mult, op1=ALU.add)), C)
    P.chain("vector",
            (lambda v: v.tensor_scalar(
                out=gs[:, dst_base + 3:dst_base + 4],
                in0=gs[:, dst_base:dst_base + 1],
                scalar1=gs[:, dst_base + 2:dst_base + 3], scalar2=None,
                op0=ALU.mult)), C)


def emit_hw_phase(P, t, sem, hooks=None):
    C = sem.gch
    gwv = P.val(sem.gw)
    for tt in range(TT):
        def _h(te, tt=tt):
            last = None
            for c in range(KC):
                last = te.matmul(
                    t.pmain[2][:, 0:128],
                    t.xnT[:, c * 1024 + tt * 128: c * 1024 + (tt + 1) * 128],
                    t.a2t_bf[:, c * 128:(c + 1) * 128],
                    start=(c == 0), stop=(c == KC - 1))
            return last
        P.chain("tensor", _h, C, waits=[(sem.gw, gwv)])
        P.chain("vector",
                (lambda v, tt=tt: v.tensor_tensor(
                    out=t.hw_tok[:].rearrange("p (a b) -> p a b", a=8),
                    in0=t.pmain[2][:, 0:128].rearrange("p (a b) -> p a b", a=8),
                    in1=t.gsm[1][:, 8 * tt:8 * (tt + 1)][:, :, None]
                        .broadcast_to([128, 8, 16]),
                    op=ALU.mult)), C)
        P.chain("tensor",
                lambda te: te.transpose(
                    t.pmain[3][:, 0:128].bitcast(BF16)[:, 0:128],
                    t.hw_tok[:], t.ident_bf[:]), C)
        P.chain("scalar",
                (lambda s, tt=tt: s.copy(
                    t.hwT[:, tt * 128:(tt + 1) * 128],
                    t.pmain[3][:, 0:128].bitcast(BF16)[:, 0:128])), C)
        if hooks and tt in hooks:
            hooks[tt]()


def emit_z_chunk_mm(P, t, sem, c, st, first):
    def _z(te, c=c, first=first):
        last = None
        for tt in range(TT):
            last = te.matmul(
                t.pmain[2][:, tt * 40:(tt + 1) * 40],
                t.xnT[:, c * 1024 + tt * 128: c * 1024 + (tt + 1) * 128],
                t.ah8_bf[:, c * 40:(c + 1) * 40],
                start=(first and tt == 0), stop=False)
        return last
    waits = [(sem.tevac, st.tev[c])]
    if first:
        waits.append((sem.gqw, P.val(sem.gqw)))
    P.op("tensor", _z, waits=waits)


def emit_z_close(P, t, sem):
    P.op("tensor",
         lambda te: te.matmul(t.pmain[2][:, 0:320], t.zrow_bf[0:1, 0:128],
                              t.zrow_bf[0:1, 0:320], start=False, stop=True),
         waits=[], inc=sem.zdone)


def emit_z_evac(P, t, sem, tt, zT_bf, zT_f32, h8T, sldv):
    C = sem.gch
    P.chain("scalar",
            (lambda s, tt=tt: s.copy(t.zh_tok[:],
                                     t.pmain[2][:, tt * 40:(tt + 1) * 40])), C,
            waits=([(sem.zdone, 1)] if tt == 0 else []))
    P.chain("tensor",
            lambda te: te.transpose(t.pmain[3][0:32, 0:128], t.zh_tok[:, 0:32],
                                    t.ident_f32[:]), C)

    def _zev(s, tt=tt):
        last = s.copy(zT_bf[:, tt * 128:(tt + 1) * 128],
                      t.pmain[3][0:32, 0:128])
        if zT_f32 is not None:
            last = s.copy(zT_f32[:, tt * 128:(tt + 1) * 128],
                          t.pmain[3][0:32, 0:128])
        return last
    P.chain("scalar", _zev, C)
    P.chain("tensor",
            lambda te: te.transpose(t.pmain[3][0:8, 128:256],
                                    t.zh_tok[:, 32:40], t.ident_f32[:]), C)
    P.chain("scalar",
            (lambda s, tt=tt: s.activation(
                out=h8T[:, tt * 128:(tt + 1) * 128],
                in_=t.pmain[3][0:8, 128:256], func=AF.Identity,
                bias=t.b8col[:, 0:1], scale=1.0)), C,
            waits=([(sem.sld, sldv)] if tt == 0 else []))


def emit_zpre_tile(P, t, sem, tt, zT_bf, zT_f32, h8T):
    C = sem.gch

    def _zmm(te, tt=tt):
        last = None
        for c in range(KC):
            last = te.matmul(
                t.pmain[2][:, 0:40],
                t.zsrc(c, tt),
                t.ah8_bf[:, c * 40:(c + 1) * 40],
                start=(c == 0), stop=(c == KC - 1))
        return last
    P.chain("tensor", _zmm, C,
            waits=[(sem.gqw, P.val(sem.gqw)), (sem.tevac, P.val(sem.tevac)),
                   (sem.sld, P.val(sem.sld))])
    P.chain("scalar", lambda s: s.copy(t.zh_tok[:], t.pmain[2][:, 0:40]), C)
    P.chain("tensor",
            lambda te: te.transpose(t.pmain[3][0:32, 0:128], t.zh_tok[:, 0:32],
                                    t.ident_f32[:]), C)

    def _zev(s, tt=tt):
        last = s.copy(zT_bf[:, tt * 128:(tt + 1) * 128],
                      t.pmain[3][0:32, 0:128])
        if zT_f32 is not None:
            last = s.copy(zT_f32[:, tt * 128:(tt + 1) * 128],
                          t.pmain[3][0:32, 0:128])
        return last
    P.chain("scalar", _zev, C)
    P.chain("tensor",
            lambda te: te.transpose(t.pmain[3][0:8, 128:256],
                                    t.zh_tok[:, 32:40], t.ident_f32[:]), C)
    P.chain("scalar",
            (lambda s, tt=tt: s.activation(
                out=h8T[:, tt * 128:(tt + 1) * 128],
                in_=t.pmain[3][0:8, 128:256], func=AF.Identity,
                bias=t.b8col[:, 0:1], scale=1.0)), C)


def emit_ln_tile(P, t, sem, xin_tile, par, xnt_slot, first_waits=()):
    L = sem.lnch
    ls = t.lnsc[par]
    waits = list(first_waits)
    if P.val(sem.xn) >= 2:
        waits.append((sem.xn, P.val(sem.xn) - 1))
    P.op("sync",
         (lambda sy, par=par, xin_tile=xin_tile: sy.dma_start(
             out=t.xstage[par][:], in_=xin_tile)),
         waits=waits, inc=sem.xdma[par], inc_n=16)
    ld = (sem.xdma[par], P.val(sem.xdma[par]))
    sq_waits = [ld]
    if P.val(sem.tevac) >= 2 * KC:
        sq_waits.append((sem.tevac, P.val(sem.tevac) - KC))
    P.op("scalar",
         (lambda s, par=par, ls=ls: s.activation(
             out=t.xnbuf[par][:], in_=t.xstage[par][:], func=AF.Square,
             accum_out=ls[:, 1:2])),
         waits=sq_waits, inc=sem.sq)
    P.op("vector",
         (lambda v, par=par, ls=ls: v.reduce_sum(
             ls[:, 0:1], t.xstage[par][:], axis=AX.X)),
         waits=[ld, (L, P.val(L))], inc=L)
    P.op("vector",
         (lambda v, ls=ls: v.tensor_scalar(
             out=ls[:, 2:4], in0=ls[:, 0:2], scalar1=1.0 / D,
             scalar2=None, op0=ALU.mult)),
         waits=[(L, P.val(L)), (sem.sq, P.val(sem.sq))], inc=L)
    P.chain("vector",
            (lambda v, ls=ls: v.tensor_scalar(
                out=ls[:, 4:5], in0=ls[:, 2:3], scalar1=ls[:, 2:3],
                scalar2=None, op0=ALU.mult)), L)
    P.chain("vector",
            (lambda v, ls=ls: v.tensor_scalar(
                out=ls[:, 5:6], in0=ls[:, 3:4], scalar1=ls[:, 4:5],
                scalar2=LN_EPS, op0=ALU.subtract, op1=ALU.add)), L)
    P.chain("scalar",
            (lambda s, ls=ls: s.activation(out=ls[:, 6:7], in_=ls[:, 5:6],
                                           func=AF.Sqrt)), L)
    P.chain("vector",
            (lambda v, ls=ls: v.reciprocal(ls[:, 10:11], ls[:, 6:7])), L)
    P.chain("vector",
            (lambda v, ls=ls: v.tensor_scalar(
                out=ls[:, 11:12], in0=ls[:, 2:3], scalar1=ls[:, 10:11],
                scalar2=-1.0, op0=ALU.mult, op1=ALU.mult)), L)
    P.op("scalar",
         (lambda s, par=par, ls=ls: s.activation(
             out=t.xnbuf[par][:], in_=t.xstage[par][:], func=AF.Identity,
             scale=ls[:, 10:11], bias=ls[:, 11:12])),
         waits=[(L, P.val(L))], inc=sem.xn)
    xnv = P.val(sem.xn)
    for c in range(KC):
        psl = t.pmain[c % 2][:, 0:128].bitcast(BF16)
        waits = [(sem.xn, xnv)]
        if P.val(sem.tevac) > 0:
            waits.append((sem.tevac, P.val(sem.tevac) - 1))
        P.op("tensor",
             (lambda te, c=c, par=par, psl=psl: te.transpose(
                 psl[:, 0:128],
                 t.xnbuf[par][:, c * 128:(c + 1) * 128], t.ident_bf[:])),
             waits=waits, inc=sem.tpe)
        P.op("scalar",
             (lambda s, psl=psl, slot=xnt_slot(c): s.copy(slot, psl[:, 0:128])),
             waits=[(sem.tpe, P.val(sem.tpe))], inc=sem.tevac)


def emit_anchor_gate(P, t, sem, zT_bf_anch, h8T_anch, limt_view):
    C = sem.gch
    for aj in range(4):
        P.chain("tensor",
                (lambda te, aj=aj: te.matmul(
                    t.pmain[2][:, 64 + aj * 8:64 + (aj + 1) * 8],
                    zT_bf_anch[:, aj * 128:(aj + 1) * 128],
                    limt_view, start=True, stop=True)), C)
        P.chain("tensor",
                (lambda te, aj=aj: te.transpose(
                    t.pmain[3][:, 256 + aj * 8:256 + (aj + 1) * 8],
                    h8T_anch[:, aj * 128:(aj + 1) * 128],
                    t.ident_f32[0:8, 0:8])), C)
    P.chain("scalar", lambda s: s.copy(t.dsbuf[:], t.pmain[2][:, 64:96]), C)
    P.chain("scalar", lambda s: s.copy(t.hsbuf[:], t.pmain[3][:, 256:288]), C)


def emit_locmax(P, t, sem):
    C = sem.gch
    P.chain("vector",
            lambda v: v.reduce_max(t.gsm[0][:, 0:1], t.hsbuf[:], axis=AX.X,
                                   apply_absolute_value=True), C)
    P.chain("vector",
            lambda v: v.reduce_max(t.gsm[0][:, 1:2], t.dsbuf[:], axis=AX.X,
                                   apply_absolute_value=True), C)
    P.chain("tensor",
            lambda te: te.transpose(t.pmain[3][0:2, 0:128], t.gsm[0][:, 0:2],
                                    t.ident_f32[:]), C)
    P.chain("scalar",
            lambda s: s.copy(t.max2x128[:], t.pmain[3][0:2, 0:128]), C)
    P.chain("vector",
            lambda v: v.reduce_max(t.locmax[:], t.max2x128[:], axis=AX.X), C)


def emit_l1_post(P, t, sem, gmax_dram, gsc_dram, bmat_dram, lim_dram):
    C = sem.gch
    g0 = t.gsm[0]
    P.op("sync", lambda sy: sy.dma_start(out=t.gmax8[:], in_=gmax_dram[:, :]),
         waits=[(sem.xn, P.val(sem.xn)), (sem.xns, P.val(sem.xns))],
         inc=sem.sld, inc_n=16)
    P.op("gpsimd",
         lambda g: g.dma_start(
             out=t.bmat_bf[:].rearrange("p (c n) -> p c n", c=1),
             in_=bmat_dram[:, :].rearrange("(c p) n -> p c n", c=1)),
         waits=[(sem.mpe, P.val(sem.mpe))], inc=sem.gld, inc_n=16)
    P.op("sync", lambda sy: sy.dma_start(out=t.lim_f32[:], in_=lim_dram[:, :]),
         waits=[(sem.xn, P.val(sem.xn)), (sem.xns, P.val(sem.xns))],
         inc=sem.sld, inc_n=16)
    P.op("vector", None,
         waits=[(sem.gld, P.val(sem.gld)), (sem.sld, P.val(sem.sld)),
                (sem.consts, P.val(sem.consts)), (sem.mevac, P.val(sem.mevac)),
                (sem.gch, P.val(sem.gch))],
         inc=C)
    P.chain("scalar", lambda s: s.activation(out=t.lim_bf[:], in_=t.lim_f32[:],
                                             func=AF.Identity,
                                             scale=t.cst[0:8, 2:3]), C)
    P.chain("tensor",
            lambda te: te.transpose(t.pgB[0:2, 0:8], t.gmax8[:],
                                    t.ident_f32[0:8, 0:8]), C)
    P.chain("scalar", lambda s: s.copy(t.max2x128[:, 0:8], t.pgB[0:2, 0:8]), C)
    P.chain("vector",
            lambda v: v.reduce_max(t.gsc2[:, 0:1], t.max2x128[:, 0:8],
                                   axis=AX.X), C)
    P.chain("vector",
            lambda v: v.tensor_scalar(out=t.gsc2[:, 1:2], in0=t.gsc2[:, 0:1],
                                      scalar1=1e-6, scalar2=None,
                                      op0=ALU.max), C)
    P.chain("sync",
            lambda sy: sy.dma_start(out=gsc_dram[:, :], in_=t.gsc2[:, 1:2]),
            C, inc_n=16)
    P.chain("sync",
            lambda sy: sy.dma_start(
                out=t.gscH[:], in_=gsc_dram[0:1, 0:1].to_broadcast((128, 1))),
            C, inc_n=16)
    P.chain("sync",
            lambda sy: sy.dma_start(
                out=t.gscD[:], in_=gsc_dram[1:2, 0:1].to_broadcast((128, 1))),
            C, inc_n=16)
    P.chain("vector", lambda v: v.tensor_copy(g0[:, 0:1], t.gscH[:]), C)
    emit_recip_refine(P, t, sem, g0, 0, 1)
    P.chain("vector", lambda v: v.tensor_copy(g0[:, 8:9], t.gscD[:]), C)
    emit_recip_refine(P, t, sem, g0, 8, 9)
    P.chain("vector",
            lambda v: v.tensor_scalar(out=g0[:, 16:17], in0=g0[:, 4:5],
                                      scalar1=0.5, scalar2=None,
                                      op0=ALU.mult), C)
    P.chain("vector",
            lambda v: v.tensor_scalar(out=g0[:, 17:18], in0=g0[:, 12:13],
                                      scalar1=0.5, scalar2=None,
                                      op0=ALU.mult), C)

    for aj in range(4):
        gs = t.gsm[1]
        sl = slice(aj * 8, (aj + 1) * 8)
        P.chain("vector",
                (lambda v, gs=gs, sl=sl: v.tensor_scalar(
                    out=gs[:, 16:24], in0=t.dsbuf[:, sl],
                    scalar1=g0[:, 17:18], scalar2=None, op0=ALU.mult)), C)
        P.chain("vector",
                (lambda v, gs=gs, sl=sl: v.scalar_tensor_tensor(
                    out=gs[:, 16:24], in0=t.hsbuf[:, sl], scalar=g0[:, 16:17],
                    in1=gs[:, 16:24], op0=ALU.mult, op1=ALU.add)), C)
        P.chain("vector",
                (lambda v, gs=gs: v.reduce_max(gs[:, 24:25], gs[:, 16:24],
                                               axis=AX.X, negate=True)), C)
        P.chain("scalar",
                (lambda s, gs=gs: s.activation(
                    out=gs[:, 32:40], in_=gs[:, 16:24], func=AF.Exp,
                    bias=gs[:, 24:25], scale=1.0)), C)
        P.chain("vector",
                (lambda v, gs=gs: v.reduce_sum(gs[:, 25:26], gs[:, 32:40],
                                               axis=AX.X)), C)
        emit_recip_refine(P, t, sem, gs, 25, 26)
        P.chain("vector",
                (lambda v, gs=gs: v.tensor_scalar(
                    out=gs[:, 40:48], in0=gs[:, 32:40],
                    scalar1=gs[:, 29:30], scalar2=None, op0=ALU.mult)), C)
        P.chain("vector",
                (lambda v, gs=gs: v.reduce_max(gs[:, 48:49], gs[:, 40:48],
                                               axis=AX.X)), C)
        P.chain("vector",
                (lambda v, gs=gs: v.tensor_scalar(
                    out=gs[:, 56:64], in0=gs[:, 40:48],
                    scalar1=gs[:, 48:49], scalar2=None, op0=ALU.is_ge)), C)
        P.chain("vector",
                (lambda v, gs=gs: v.reduce_sum(gs[:, 49:50], gs[:, 56:64],
                                               axis=AX.X)), C)
        P.chain("vector",
                (lambda v, gs=gs: v.scalar_tensor_tensor(
                    out=gs[:, 56:64], in0=gs[:, 56:64], scalar=-1e30,
                    in1=gs[:, 40:48], op0=ALU.mult, op1=ALU.add)), C)
        P.chain("vector",
                (lambda v, gs=gs: v.reduce_max(gs[:, 50:51], gs[:, 56:64],
                                               axis=AX.X)), C)
        P.chain("vector",
                (lambda v, gs=gs: v.tensor_scalar(
                    out=gs[:, 51:52], in0=gs[:, 49:50], scalar1=2.0,
                    scalar2=None, op0=ALU.is_ge)), C)
        P.chain("vector",
                (lambda v, gs=gs: v.tensor_scalar(
                    out=gs[:, 52:53], in0=gs[:, 48:49],
                    scalar1=gs[:, 50:51], scalar2=None, op0=ALU.subtract)), C)
        P.chain("vector",
                (lambda v, gs=gs: v.tensor_scalar(
                    out=gs[:, 53:54], in0=gs[:, 52:53],
                    scalar1=gs[:, 51:52], scalar2=None, op0=ALU.mult)), C)
        P.chain("vector",
                (lambda v, gs=gs: v.tensor_scalar(
                    out=gs[:, 54:55], in0=gs[:, 53:54],
                    scalar1=gs[:, 50:51], scalar2=None, op0=ALU.add)), C)
        P.chain("vector",
                (lambda v, gs=gs: v.tensor_scalar(
                    out=gs[:, 55:56], in0=gs[:, 54:55], scalar1=-2.0,
                    scalar2=None, op0=ALU.mult)), C)
        P.chain("scalar",
                (lambda s, gs=gs: s.activation(
                    out=gs[:, 56:64], in_=gs[:, 40:48], func=AF.Sigmoid,
                    bias=gs[:, 55:56], scale=t.cst[:, 1:2])), C)
        P.chain("vector",
                (lambda v, gs=gs: v.tensor_tensor(
                    out=gs[:, 40:48], in0=gs[:, 40:48], in1=gs[:, 56:64],
                    op=ALU.mult)), C)
        P.chain("vector",
                (lambda v, gs=gs: v.reduce_sum(gs[:, 25:26], gs[:, 40:48],
                                               axis=AX.X)), C)
        P.chain("vector",
                (lambda v, gs=gs: v.tensor_scalar(
                    out=gs[:, 26:27], in0=gs[:, 25:26], scalar1=1e-9,
                    scalar2=None, op0=ALU.add)), C)
        emit_recip_refine(P, t, sem, gs, 26, 27)
        P.chain("vector",
                (lambda v, gs=gs: v.tensor_scalar(
                    out=gs[:, 40:48], in0=gs[:, 40:48],
                    scalar1=gs[:, 30:31], scalar2=None, op0=ALU.mult)), C)
        P.chain("tensor",
                (lambda te, gs=gs: te.transpose(
                    t.pgB[0:8, 0:128], gs[:, 40:48], t.ident_f32[:])), C)
        P.chain("scalar",
                (lambda s, aj=aj: s.copy(
                    t.wfullT[:, aj * 256:(aj + 1) * 256]
                        .rearrange("p (a two) -> p a two", two=2),
                    t.pgB[0:8, 0:128][:, :, None].broadcast_to([8, 128, 2]))),
                C)
    P.chain("tensor",
            lambda te: te.matmul(t.pgA[0:32, 0:512], t.lim_bf[:],
                                 t.wfullT[:, 0:512], start=True, stop=True), C)
    P.chain("tensor",
            lambda te: te.matmul(t.pgC[0:32, 0:512], t.lim_bf[:],
                                 t.wfullT[:, 512:1024], start=True, stop=True),
            C)
    P.chain("vector",
            lambda v: v.tensor_tensor(out=t.zp_bf[:, 0:512],
                                      in0=t.zT_f32[:, 0:512],
                                      in1=t.pgA[0:32, 0:512], op=ALU.mult), C)
    P.chain("vector",
            lambda v: v.tensor_tensor(out=t.zp_bf[:, 512:1024],
                                      in0=t.zT_f32[:, 512:1024],
                                      in1=t.pgC[0:32, 0:512], op=ALU.mult), C)


def emit_main(P, t, sem, wa_dram, wf_dram, xin, xout, layer, extra_wait=(),
              ring_free_wait=(), x0_first_wait=()):
    for s in range(NSUP):
        for g in range(NG):
            for (ring, wd, wsems, cons_base) in (
                    (t.wring_f, wf_dram, sem.wdf, 0),
                    (t.wring_a, wa_dram, sem.wda, KC)):
                wsem = wsems[g]
                waits = []
                if s == 0:
                    waits.extend(ring_free_wait)
                if s > 0:
                    need = (s - 1) * 4 * KC + 2 * (cons_base + (g + 1) * GRP)
                    waits.append((sem.mmc, need))
                P.op("gpsimd",
                     (lambda gg, ring=ring, wd=wd, s=s, g=g: gg.dma_start(
                         out=ring[:, g * GRP * NW:(g + 1) * GRP * NW]
                             .rearrange("p (c n) -> p c n", c=GRP),
                         in_=wd[g * GRP * 128:(g + 1) * GRP * 128,
                                s * NW:(s + 1) * NW]
                             .rearrange("(c p) n -> p c n", c=GRP))),
                     waits=waits, inc=wsem, inc_n=16)
        for g in range(NG):
            P.op("vector",
                 (lambda v, s=s, g=g: v.tensor_tensor(
                     out=t.wring_a[:, g * GRP * NW:(g + 1) * GRP * NW]
                         .rearrange("p (c a b) -> p c a b", c=GRP, a=8),
                     in0=t.wring_a[:, g * GRP * NW:(g + 1) * GRP * NW]
                         .rearrange("p (c a b) -> p c a b", c=GRP, a=8),
                     in1=t.bmP[:].rearrange("p (c sp a) -> p c sp a", c=KC, sp=16)
                         [:, g * GRP:(g + 1) * GRP, s, :][:, :, :, None]
                         .broadcast_to([128, GRP, 8, 32]),
                     op=ALU.mult)),
                 waits=[(sem.wda[g], P.val(sem.wda[g])),
                        (sem.mevac, P.val(sem.mevac))],
                 inc=sem.wmask)
        P.op("sync",
             (lambda sy, s=s: sy.dma_start(
                 out=t.x0stage[s % 2][:].rearrange("p (tt n) -> p tt n", tt=TT),
                 in_=xin[:, s * NW:(s + 1) * NW]
                     .rearrange("(tt p) n -> p tt n", tt=TT))),
             waits=([(sem.evac, P.val(sem.evac) - TT)] if s >= 2
                    else list(x0_first_wait)),
             inc=sem.x0d, inc_n=16)
        x0v = P.val(sem.x0d)

        for ci in range(2 * KC):
            is_ffn = ci < KC
            c = ci if is_ffn else ci - KC
            ring = t.wring_f if is_ffn else t.wring_a
            g = c // GRP
            waits = []
            if is_ffn:
                waits.append((sem.wdf[g], P.val(sem.wdf[g])))
            else:
                waits.append((sem.wmask, P.val(sem.wmask) - (NG - 1 - g)))
            if ci == 0:
                if s == 0:
                    waits.append((sem.tevac, P.val(sem.tevac)))
                    waits.append((sem.gch, P.val(sem.gch)))
                    waits.extend(extra_wait)
                ev0 = P.val(sem.evac)
                for tt in range(TT):
                    w2 = list(waits)
                    if s > 0:
                        w2.append((sem.evac, ev0 - (TT - 1 - tt)))
                    P.op("tensor",
                         (lambda te, c=c, ring=ring, tt=tt: te.matmul(
                             t.pmain[tt][:, 0:NW],
                             t.xnT[:, c * 1024 + tt * 128:
                                   c * 1024 + (tt + 1) * 128],
                             ring[:, c * NW:(c + 1) * NW],
                             start=True, stop=False)),
                         waits=w2,
                         inc=(sem.mmc if tt == TT - 1 else None), inc_n=2)
                continue

            def _mms(te, c=c, ring=ring):
                last = None
                for tt in range(TT):
                    last = te.matmul(
                        t.pmain[tt][:, 0:NW],
                        t.xnT[:, c * 1024 + tt * 128: c * 1024 + (tt + 1) * 128],
                        ring[:, c * NW:(c + 1) * NW],
                        start=False, stop=False)
                return last
            P.op("tensor", _mms, waits=waits, inc=sem.mmc, inc_n=2)
        for tt in range(TT):
            def _fin(te, tt=tt, s=s, layer=layer):
                pm = t.pmain[tt][:, 0:NW]
                if layer == 0:
                    te.matmul(pm, t.hwT[:, tt * 128:(tt + 1) * 128],
                              t.bm2_bf[:, s * NW:(s + 1) * NW],
                              start=False, stop=False)
                else:
                    te.matmul(pm, t.zp_bf[:, tt * 128:(tt + 1) * 128],
                              t.bmat_bf[:, s * NW:(s + 1) * NW],
                              start=False, stop=False)
                return te.matmul(pm, t.onesrow_bf[0:1, 0:128],
                                 t.biasrow_bf[0:1, s * NW:(s + 1) * NW],
                                 start=False, stop=True)
            P.op("tensor", _fin,
                 waits=([(sem.xb, 1)] if s == 0 else []), inc=sem.mmf)
            P.op("vector",
                 (lambda v, tt=tt, s=s: v.tensor_tensor(
                     out=t.ostage[s % 2][:, tt * NW:(tt + 1) * NW],
                     in0=t.pmain[tt][:, 0:NW],
                     in1=t.x0stage[s % 2][:, tt * NW:(tt + 1) * NW],
                     op=ALU.add)),
                 waits=[(sem.mmf, P.val(sem.mmf)), (sem.x0d, x0v)]
                 + ([(sem.ost, P.val(sem.ost))] if s >= 2 else []),
                 inc=sem.evac)
        P.op("sync",
             (lambda sy, s=s: sy.dma_start(
                 out=xout[:, s * NW:(s + 1) * NW]
                     .rearrange("(tt p) n -> p tt n", tt=TT),
                 in_=t.ostage[s % 2][:].rearrange("p (tt n) -> p tt n", tt=TT))),
             waits=[(sem.evac, P.val(sem.evac))], inc=sem.ost, inc_n=16)



def alloc_common(nc, es, t):
    sb = nc.sbuf_tensor
    t.xnT = es.enter_context(sb("xnT", [128, KC * 1024], BF16)).ap()
    t.wring_a = es.enter_context(sb("wring_a", [128, 8192], BF16)).ap()
    t.wring_f = es.enter_context(sb("wring_f", [128, 8192], BF16)).ap()
    xbig = [es.enter_context(sb(f"xbig{i}", [128, D], F32)).ap()
            for i in range(2)]
    t.xstage = xbig
    t.ostage = [xb[:, 0:TT * NW] for xb in xbig]
    t.x0stage = [xb[:, TT * NW:2 * TT * NW] for xb in xbig]
    t.xck = [xb[:, 0:1024] for xb in xbig]
    mbig = es.enter_context(sb("mbig", [128, 2048], F32)).ap()
    t.mbig = mbig
    t.bmQ = es.enter_context(sb("bmQ", [128, 128], F32)).ap()
    t.Qbf = es.enter_context(sb("Qbf", [128, 128], BF16)).ap()
    t.bmP = es.enter_context(sb("bmP", [128, D], BF16)).ap()
    scratch16 = es.enter_context(sb("scratch16", [128, D], F32)).ap()
    t.E32 = scratch16.bitcast(BF16)[:, 0:D]
    t.brows0 = scratch16[0:1, :]
    t.xnbuf = [es.enter_context(sb(f"xnbuf{i}", [128, D], BF16)).ap()
               for i in range(2)]
    t.lnsc = [es.enter_context(sb(f"lnsc{i}", [128, 16], F32)).ap()
              for i in range(2)]
    t.lnsc8 = es.enter_context(sb("lnsc8", [128, 128], F32)).ap()
    t.ones_f32 = es.enter_context(sb("ones_f32", [1, 128], F32)).ap()
    t.ident_bf = es.enter_context(sb("ident_bf", [128, 128], BF16)).ap()
    t.ident_f32 = es.enter_context(sb("ident_f32", [128, 128], F32)).ap()
    t.gsm = [es.enter_context(sb(f"gsm{i}", [128, 64], F32)).ap()
             for i in range(2)]
    t.cst = es.enter_context(sb("cst", [128, 4], F32)).ap()
    t.zrow_bf = es.enter_context(sb("zrow_bf", [1, 512], BF16)).ap()
    t.onesrow_bf = es.enter_context(sb("onesrow_bf", [1, 128], BF16)).ap()
    t.biasrow_bf = es.enter_context(sb("biasrow_bf", [1, D], BF16)).ap()
    t.ones_bf = es.enter_context(sb("ones_bf", [128, 1], BF16)).ap()
    t.mscat = es.enter_context(sb("mscat", [1, 1024], F32)).ap()
    t.bm2_bf = t.mbig.bitcast(BF16)[:, 0:D]
    t.bmat_bf = t.mbig.bitcast(BF16)[0:32, 0:D]
    t.ah8_bf = es.enter_context(sb("ah8_bf", [128, KC * 40], BF16)).ap()
    t.limt_f32 = es.enter_context(sb("limt_f32", [32, 8], F32)).ap()
    t.limt_bf = es.enter_context(sb("limt_bf", [32, 8], BF16)).ap()
    t.b8col = es.enter_context(sb("b8col", [8, 1], F32)).ap()


def alloc_psums(nc, es, t):
    ps = nc.psum_tensor
    banks = [es.enter_context(ps(f"bank{i}", [128, 512], F32)).ap()
             for i in range(8)]
    t.pmain = banks
    t.pgA, t.pgB, t.pgC, t.pgD = banks[4], banks[5], banks[6], banks[7]


class SemNames:
    names = ("consts", "mred", "mpe", "mevac", "sq", "xn", "xns", "tpe",
             "tevac", "gw", "gqw", "gch", "gld", "bld", "wmask", "mmc", "mmf",
             "x0d", "evac", "ost", "fin", "lnch", "sld", "bsum", "zdone",
             "xb", "qkp")
    arrays = {"mdma": 4, "xdma": 2, "wda": NG, "wdf": NG, "msc": 2}


def make_sems(nc, es):
    sem = NS()
    for n in SemNames.names:
        setattr(sem, n, es.enter_context(nc.semaphore(n)))
    for n, k in SemNames.arrays.items():
        setattr(sem, n, [es.enter_context(nc.semaphore(f"{n}{i}"))
                         for i in range(k)])
    return sem



def build_launch_a():
    nc = bass.Bass("TRN2", target_bir_lowering=False, debug=False)
    dt = nc.dram_tensor
    x0 = dt("x0", [TOK, D], F32, kind="ExternalInput").ap()
    wt_l0a = dt("wt_l0a", [D, D], F32, kind="ExternalInput").ap()
    wt_l0f = dt("wt_l0f", [D, D], F32, kind="ExternalInput").ap()
    maskt0 = dt("maskt0", [1024, 16384], I32, kind="ExternalInput").ap()
    wqk = dt("wqk", [D, 512], F32, kind="ExternalInput").ap()
    a2t = dt("a2t", [D, 128], F32, kind="ExternalInput").ap()
    bm2 = dt("bm2", [128, D], F32, kind="ExternalInput").ap()
    b_l0a = dt("b_l0a", [1, D], F32, kind="ExternalInput").ap()
    b_l0f = dt("b_l0f", [1, D], F32, kind="ExternalInput").ap()
    ah8 = dt("ah8", [D, 40], F32, kind="ExternalInput").ap()
    b_l1f = dt("b_l1f", [1, D], F32, kind="ExternalInput").ap()
    limt = dt("limt", [32, 8], F32, kind="ExternalInput").ap()
    e32h = dt("e32h", [128, D], F32, kind="ExternalInput").ap()

    x1 = dt("x1", [TOK, D], F32, kind="ExternalOutput").ap()
    locmax = dt("locmax", [2, 1], F32, kind="ExternalOutput").ap()
    bmq_d = dt("bmq_d", [128, 128], F32).ap()

    es = contextlib.ExitStack()
    t = NS()
    alloc_common(nc, es, t)
    t.bmq_d = bmq_d
    sb = nc.sbuf_tensor
    t.a2t_bf = es.enter_context(sb("a2t_bf", [128, D], BF16)).ap()
    t.brows1 = t.xstage[0][0:1, :]
    t.gq = es.enter_context(sb("gq", [128, 256], F32)).ap()
    t.gqk = es.enter_context(sb("gqk", [128, 256], F32)).ap()
    t.hw_tok = es.enter_context(sb("hw_tok", [128, 128], BF16)).ap()
    t.hwT = es.enter_context(sb("hwT", [128, 1024], BF16)).ap()
    t.locmax = es.enter_context(sb("locmax_sb", [2, 1], F32)).ap()
    bmPf = t.bmP.bitcast(F32)
    t.zh_tok = bmPf[:, 0:40]
    t.zTa_bf = t.bmP[0:32, 2048:2560]
    t.h8Ta = bmPf[0:8, 64:576]
    t.dsbuf = bmPf[:, 1024:1056]
    t.hsbuf = bmPf[:, 1056:1088]
    t.max2x128 = bmPf[0:2, 1100:1228]
    alloc_psums(nc, es, t)
    sem = make_sems(nc, es)
    block = es.enter_context(nc.Block())

    with es:
        P = Prog()
        emit_consts(P, t, sem)
        P.op("gpsimd",
             lambda g: g.dma_start(
                 out=t.ah8_bf[:].rearrange("p (c n) -> p c n", c=KC),
                 in_=ah8[:, :].rearrange("(c p) n -> p c n", c=KC)),
             inc=sem.gqw, inc_n=16)
        for bt in range(2):
            emit_mask2_group(P, t, sem, maskt0, bt)
        P.op("gpsimd",
             lambda g: g.dma_start(
                 out=t.wring_a[:].rearrange("p (c n) -> p c n", c=16),
                 in_=wqk[0:2048, :].rearrange("(c p) n -> p c n", c=16)),
             inc=sem.gqw, inc_n=16)
        P.op("gpsimd",
             lambda g: g.dma_start(
                 out=t.wring_f[:].rearrange("p (c n) -> p c n", c=16),
                 in_=wqk[2048:4096, :].rearrange("(c p) n -> p c n", c=16)),
             inc=sem.gqw, inc_n=16)
        P.op("gpsimd",
             lambda g: g.dma_start(
                 out=t.E32[:].rearrange("p (c n) -> p c n", c=1),
                 in_=e32h[:, :].rearrange("(c p) n -> p c n", c=1)),
             inc=sem.gqw, inc_n=16)
        P.op("gpsimd",
             lambda g: g.dma_start(
                 out=t.a2t_bf[:, 0:D].rearrange("p (c n) -> p c n", c=KC),
                 in_=a2t[:, :].rearrange("(c p) n -> p c n", c=KC)),
             inc=sem.gw, inc_n=16)
        P.op("sync", lambda sy: sy.dma_start(out=t.limt_f32[:], in_=limt[:, :]),
             inc=sem.sld, inc_n=16)
        P.op("sync",
             lambda sy: sy.dma_start(
                 out=t.b8col[:], in_=b_l1f[0:1, 0:8]
                     .rearrange("x (p y) -> p (x y)", p=8)),
             inc=sem.sld, inc_n=16)
        P.chain("scalar", lambda s: s.copy(t.limt_bf[:], t.limt_f32[:]),
                sem.gch, waits=[(sem.sld, P.val(sem.sld))])

        st = NS()
        st.rel, st.xnv, st.xnsv, st.tpev, st.tev = {}, {}, {}, {}, {}
        nbt = 2
        for tt in range(TT):
            emit_ln_stats(P, t, sem, x0[tt * 128:(tt + 1) * 128, :], tt, st)
            if tt % 2 == 1 or tt >= 4:
                emit_mask2_group(P, t, sem, maskt0, nbt)
                nbt += 1
        st.statsL = P.val(sem.lnch)
        for c in range(KC):
            emit_ln_chunk(P, t, sem, x0, c, st)
            if c in (4, 10, 16, 22, 28):
                emit_mask2_group(P, t, sem, maskt0, nbt)
                nbt += 1
        fp = P.val(sem.gch)
        for pr in range(4):
            fp = emit_qk_pair(P, t, sem, pr, fp)
            if pr >= 1:
                emit_mask2_group(P, t, sem, maskt0, nbt)
                nbt += 1
        ring_free = P.val(sem.qkp)
        emit_hw_phase(P, t, sem)
        emit_bmp(P, t, sem)
        P.op("gpsimd",
             lambda g: g.dma_start(
                 out=t.bm2_bf[:].rearrange("p (c n) -> p c n", c=1),
                 in_=bm2[:, :].rearrange("(c p) n -> p c n", c=1)),
             waits=[(sem.mpe, P.val(sem.mpe))], inc=sem.gw, inc_n=16)
        P.op("sync", lambda sy: sy.dma_start(out=t.brows0[0:1, :], in_=b_l0a[:, :]),
             waits=[(sem.mpe, P.val(sem.mpe))], inc=sem.bld, inc_n=16)
        P.op("sync", lambda sy: sy.dma_start(out=t.brows1[0:1, :], in_=b_l0f[:, :]),
             inc=sem.bld, inc_n=16)
        P.op("vector",
             lambda v: v.tensor_tensor(out=t.brows0[0:1, :], in0=t.brows0[0:1, :],
                                       in1=t.brows1[0:1, :], op=ALU.add),
             waits=[(sem.bld, P.val(sem.bld))], inc=sem.bsum)
        P.op("scalar", lambda s: s.copy(t.biasrow_bf[0:1, :], t.brows0[0:1, :]),
             waits=[(sem.bsum, 1)], inc=sem.xb)
        emit_main(P, t, sem, wt_l0a, wt_l0f, x0, x1, layer=0,
                  extra_wait=[(sem.bld, P.val(sem.bld)), (sem.bsum, 1),
                              (sem.gw, P.val(sem.gw)),
                              (sem.mevac, P.val(sem.mevac))],
                  ring_free_wait=[(sem.qkp, ring_free)],
                  x0_first_wait=[(sem.sq, 8), (sem.lnch, st.statsL),
                                 (sem.bsum, 1)])
        P.op("sync", None, waits=[(sem.ost, P.val(sem.ost))], inc=sem.fin)
        finv = P.val(sem.fin)
        for e in ("scalar", "vector", "tensor", "gpsimd"):
            P.op(e, None, waits=[(sem.fin, finv)])
        x1odd = x1[1::2, :]

        def slot_a(c, at):
            if c < 16:
                return t.wring_a[:, c * 512 + at * 128: c * 512 + (at + 1) * 128]
            return t.wring_f[:, (c - 16) * 512 + at * 128:
                             (c - 16) * 512 + (at + 1) * 128]
        t.zsrc = slot_a
        for at in range(4):
            emit_ln_tile(P, t, sem, x1odd[at * 128:(at + 1) * 128, :], at % 2,
                         (lambda c, at=at: slot_a(c, at)),
                         first_waits=[(sem.fin, finv)])
            emit_zpre_tile(P, t, sem, at, t.zTa_bf, None, t.h8Ta)
        emit_anchor_gate(P, t, sem, t.zTa_bf, t.h8Ta, t.limt_bf[:])
        emit_locmax(P, t, sem)
        P.op("sync", lambda sy: sy.dma_start(out=locmax[:, :], in_=t.locmax[:]),
             waits=[(sem.gch, P.val(sem.gch))], inc=sem.ost, inc_n=16)
        P.op("sync", None, waits=[(sem.ost, P.val(sem.ost))])

        for name in ENGS:
            getattr(block, name)(lambda eng, name=name: P.replay(name, eng))
    return nc


def build_launch_b():
    nc = bass.Bass("TRN2", target_bir_lowering=False, debug=False)
    dt = nc.dram_tensor
    x1 = dt("x1", [TOK, D], F32, kind="ExternalInput").ap()
    gmax8 = dt("gmax8", [8, 2], F32, kind="ExternalInput").ap()
    wt_l1a = dt("wt_l1a", [D, D], F32, kind="ExternalInput").ap()
    wt_l1f = dt("wt_l1f", [D, D], F32, kind="ExternalInput").ap()
    maskt1 = dt("maskt1", [1024, 16384], I32, kind="ExternalInput").ap()
    ah8 = dt("ah8", [D, 40], F32, kind="ExternalInput").ap()
    limt = dt("limt", [32, 8], F32, kind="ExternalInput").ap()
    lim = dt("lim", [8, 32], F32, kind="ExternalInput").ap()
    bmat = dt("bmat", [32, D], F32, kind="ExternalInput").ap()
    b_l1f = dt("b_l1f", [1, D], F32, kind="ExternalInput").ap()
    e32h = dt("e32h", [128, D], F32, kind="ExternalInput").ap()

    y = dt("y", [TOK, D], F32, kind="ExternalOutput").ap()
    gsc_dram = dt("gsc_scratch", [2, 1], F32).ap()
    bmq_d = dt("bmq_d", [128, 128], F32).ap()

    es = contextlib.ExitStack()
    t = NS()
    alloc_common(nc, es, t)
    t.bmq_d = bmq_d
    sb = nc.sbuf_tensor
    t.zp_bf = es.enter_context(sb("zp_bf", [32, 1024], BF16)).ap()
    t.zT_f32 = es.enter_context(sb("zT_f32", [32, 1024], F32)).ap()
    t.zT_bf = es.enter_context(sb("zT_bf", [32, 1024], BF16)).ap()
    t.h8T = es.enter_context(sb("h8T", [8, 1024], F32)).ap()
    t.zh_tok = es.enter_context(sb("zh_tok", [128, 40], F32)).ap()
    t.dsbuf = es.enter_context(sb("dsbuf", [128, 32], F32)).ap()
    t.hsbuf = es.enter_context(sb("hsbuf", [128, 32], F32)).ap()
    xb0, xb1 = t.xstage[0], t.xstage[1]
    t.wfullT = xb0.bitcast(BF16)[0:8, 5120:6144]
    t.max2x128 = xb1[0:2, 1064:1192]
    t.gmax8 = xb1[0:8, 1192:1194]
    t.gsc2 = xb1[0:2, 1200:1202]
    t.gscH = xb1[:, 1208:1209]
    t.gscD = xb1[:, 1209:1210]
    t.lim_f32 = xb1[0:8, 1216:1248]
    t.lim_bf = xb1.bitcast(BF16)[0:8, 2496:2528]
    alloc_psums(nc, es, t)
    sem = make_sems(nc, es)
    block = es.enter_context(nc.Block())

    with es:
        P = Prog()
        emit_consts(P, t, sem)
        P.op("gpsimd",
             lambda g: g.dma_start(
                 out=t.ah8_bf[:].rearrange("p (c n) -> p c n", c=KC),
                 in_=ah8[:, :].rearrange("(c p) n -> p c n", c=KC)),
             inc=sem.gqw, inc_n=16)
        for bt in range(4):
            emit_mask2_group(P, t, sem, maskt1, bt)
        P.op("gpsimd",
             lambda g: g.dma_start(
                 out=t.E32[:].rearrange("p (c n) -> p c n", c=1),
                 in_=e32h[:, :].rearrange("(c p) n -> p c n", c=1)),
             inc=sem.gqw, inc_n=16)
        P.op("sync", lambda sy: sy.dma_start(out=t.limt_f32[:], in_=limt[:, :]),
             inc=sem.sld, inc_n=16)
        sldv = P.val(sem.sld)
        P.op("sync",
             lambda sy: sy.dma_start(
                 out=t.b8col[:], in_=b_l1f[0:1, 0:8]
                     .rearrange("x (p y) -> p (x y)", p=8)),
             inc=sem.sld, inc_n=16)
        b8v = P.val(sem.sld)
        P.chain("scalar", lambda s: s.copy(t.limt_bf[:], t.limt_f32[:]),
                sem.gch, waits=[(sem.sld, b8v)])

        st = NS()
        st.rel, st.xnv, st.xnsv, st.tpev, st.tev = {}, {}, {}, {}, {}
        nbt = 4
        for tt in range(TT):
            emit_ln_stats(P, t, sem, x1[tt * 128:(tt + 1) * 128, :], tt, st)
            if tt % 2 == 1 or tt >= 4:
                emit_mask2_group(P, t, sem, maskt1, nbt)
                nbt += 1
        st.statsL = P.val(sem.lnch)
        for c in range(KC):
            emit_ln_chunk(P, t, sem, x1, c, st)
            if c >= 1:
                emit_z_chunk_mm(P, t, sem, c - 1, st, first=(c == 1))
            if c in (4, 10, 16, 22, 28):
                emit_mask2_group(P, t, sem, maskt1, nbt)
                nbt += 1
        emit_z_chunk_mm(P, t, sem, KC - 1, st, first=False)
        emit_z_close(P, t, sem)
        for tt in range(TT):
            emit_z_evac(P, t, sem, tt, t.zT_bf, t.zT_f32, t.h8T, b8v)
        while nbt < 16:
            emit_mask2_group(P, t, sem, maskt1, nbt)
            nbt += 1
        emit_bmp(P, t, sem)
        emit_anchor_gate(P, t, sem, t.zT_bf[:, 1::2], t.h8T[:, 1::2],
                         t.limt_bf[:])
        emit_l1_post(P, t, sem, gmax8, gsc_dram, bmat, lim)
        P.op("sync", lambda sy: sy.dma_start(out=t.brows0[0:1, :], in_=b_l1f[:, :]),
             waits=[(sem.mpe, P.val(sem.mpe))], inc=sem.bld, inc_n=16)
        P.op("scalar", lambda s: s.copy(t.biasrow_bf[0:1, :], t.brows0[0:1, :]),
             waits=[(sem.bld, P.val(sem.bld))], inc=sem.xb)
        emit_main(P, t, sem, wt_l1a, wt_l1f, x1, y, layer=1,
                  extra_wait=[(sem.bld, P.val(sem.bld)),
                              (sem.mevac, P.val(sem.mevac))],
                  x0_first_wait=[(sem.gch, P.val(sem.gch)), (sem.sq, 8),
                                 (sem.lnch, st.statsL)])
        P.op("sync", None, waits=[(sem.ost, P.val(sem.ost))])

        for name in ENGS:
            getattr(block, name)(lambda eng, name=name: P.replay(name, eng))
    return nc



_NCA = None
_NCB = None


def _e32_host():
    e = np.zeros((128, D), np.float32)
    e[np.arange(D) // 32, np.arange(D)] = 1.0
    return e


def kernel(L0_attn_data, L0_attn_mask, L0_attn_weight, L0_attn_bias,
           L0_ffn_W, L0_ffn_b, L0_ffn_Wq, L0_ffn_Wk, L0_ffn_A, L0_ffn_Bm,
           L1_attn_weight, L1_attn_mask, L1_ffn_W, L1_ffn_b, L1_ffn_A,
           L1_ffn_Bmat, L1_ffn_LiMEs):
    global _NCA, _NCB
    f32 = np.float32
    cat = np.ascontiguousarray

    x0 = np.asarray(L0_attn_data, f32).reshape(NCORES * TOK, D)
    wt_l0a = cat(np.asarray(L0_attn_weight, f32).T)
    wt_l0f = cat(np.asarray(L0_ffn_W, f32).T)
    wt_l1a = cat(np.asarray(L1_attn_weight, f32).T)
    wt_l1f = cat(np.asarray(L1_ffn_W, f32).T)

    def _maskT2(m):
        return cat(np.asarray(m, np.int32).reshape(128, 32, 128, 32)
                   .transpose(1, 3, 2, 0).reshape(1024, 16384))
    maskt0 = _maskT2(L0_attn_mask)
    maskt1 = _maskT2(L1_attn_mask)
    wqk = cat(np.concatenate([np.asarray(L0_ffn_Wq, f32).T,
                              np.asarray(L0_ffn_Wk, f32).T], axis=1))
    a2t = cat(np.asarray(L0_ffn_A, f32).reshape(128, D).T)
    bm2 = cat(np.asarray(L0_ffn_Bm, f32).transpose(0, 2, 1).reshape(128, D))
    b_l0a = cat(np.asarray(L0_attn_bias, f32)[None, :])
    b_l0f = cat(np.asarray(L0_ffn_b, f32)[None, :])
    ah8 = cat(np.concatenate([np.asarray(L1_ffn_A, f32),
                              np.asarray(L1_ffn_W, f32)[0:8, :].T], axis=1))
    b_l1f = cat(np.asarray(L1_ffn_b, f32)[None, :])
    lim = cat(np.asarray(L1_ffn_LiMEs, f32))
    limt = cat(lim.T)
    bmat = cat(np.asarray(L1_ffn_Bmat, f32))
    e32h = _e32_host()

    if _NCA is None:
        _NCA = build_launch_a()
    in_maps_a = []
    for c in range(NCORES):
        in_maps_a.append(dict(
            x0=cat(x0[c * TOK:(c + 1) * TOK]), wt_l0a=wt_l0a, wt_l0f=wt_l0f,
            maskt0=maskt0, wqk=wqk, a2t=a2t, bm2=bm2, b_l0a=b_l0a,
            b_l0f=b_l0f, ah8=ah8, b_l1f=b_l1f, limt=limt, e32h=e32h))
    res_a = run_bass_kernel_spmd(_NCA, in_maps_a, core_ids=list(range(NCORES)))

    gmax8 = np.concatenate([r["locmax"].reshape(1, 2) for r in res_a.results],
                           axis=0)

    if _NCB is None:
        _NCB = build_launch_b()
    in_maps_b = []
    for c in range(NCORES):
        in_maps_b.append(dict(
            x1=res_a.results[c]["x1"], gmax8=gmax8, wt_l1a=wt_l1a,
            wt_l1f=wt_l1f, maskt1=maskt1, ah8=ah8, limt=limt, lim=lim,
            bmat=bmat, b_l1f=b_l1f, e32h=e32h))
    res_b = run_bass_kernel_spmd(_NCB, in_maps_b, core_ids=list(range(NCORES)))

    y = np.concatenate([r["y"] for r in res_b.results], axis=0)
    return y.reshape(2, NCORES * TOK // 2, D)


# version 15
# speedup vs baseline: 1.3109x; 1.0064x over previous
import contextlib
import numpy as np
import ml_dtypes
import concourse.bass as bass
import concourse.mybir as mybir
from concourse.bass_utils import run_bass_kernel_spmd

F32 = mybir.dt.float32
BF16 = mybir.dt.bfloat16
I32 = mybir.dt.int32
AF = mybir.ActivationFunctionType
ALU = mybir.AluOpType
AX = mybir.AxisListType

NCORES = 8
TOK = 1024
D = 4096
KC = 32
TT = 8
NSUP = 16
NW = 256
GRP = 8
NG = KC // GRP
INV_SQRT_DK = 0.17677669529663687
LN_EPS = 1e-5

ENGS = ("sync", "gpsimd", "scalar", "vector", "tensor")


class Prog:

    def __init__(self):
        self.ops = {e: [] for e in ENGS}
        self.cnt = {}

    def val(self, sem):
        return self.cnt.get(sem, 0)

    def op(self, eng, build, waits=(), inc=None, inc_n=1):
        tgt = None
        if inc is not None:
            self.cnt[inc] = self.cnt.get(inc, 0) + inc_n
            tgt = self.cnt[inc]
        self.ops[eng].append((tuple(waits), build, inc, inc_n))
        return tgt

    def chain(self, eng, build, sem, inc_n=1, waits=()):
        return self.op(eng, build, waits=[(sem, self.val(sem))] + list(waits),
                       inc=sem, inc_n=inc_n)

    def replay(self, eng_name, eng):
        for waits, build, inc, inc_n in self.ops[eng_name]:
            for sem, v in waits:
                if v > 0:
                    eng.wait_ge(sem, v)
            inst = build(eng) if build is not None else None
            if inc is not None:
                if inst is None:
                    inst = eng.nop()
                inst.then_inc(inc, inc_n)


class NS:
    pass



def emit_consts(P, t, sem):
    C = sem.consts

    def _ident_ops(ap, n):
        yield lambda g: g.memset(ap, 0.0)
        yield lambda g: g.affine_select(out=ap, in_=ap,
                                        compare_op=ALU.not_equal, fill=1.0,
                                        base=0, pattern=[[-1, n]],
                                        channel_multiplier=1)

    ops = []
    ops += list(_ident_ops(t.ident_bf[:], 128))
    ops += list(_ident_ops(t.ident_f32[:], 128))
    ops.append(lambda g: g.memset(t.ones_f32[:], 1.0))
    ops.append(lambda g: g.memset(t.zrow_bf[:], 0.0))
    ops.append(lambda g: g.memset(t.ones_bf[:], 1.0))
    ops.append(lambda g: g.memset(t.onesrow_bf[:], 1.0))
    ops.append(lambda g: g.memset(t.cst[:, 0:1], INV_SQRT_DK))
    ops.append(lambda g: g.memset(t.cst[:, 1:2], 2.0))
    ops.append(lambda g: g.memset(t.cst[:, 2:3], 0.5))
    ops.append(lambda g: g.memset(t.cst[:, 3:4], LN_EPS))
    for f in ops:
        P.chain("gpsimd", f, C)


def emit_mask2_group(P, t, sem, maskT2, bt):
    mt = [t.mbig.bitcast(BF16)[:, i * 1024:(i + 1) * 1024] for i in range(4)]
    strip_banks = [t.pgA, t.pgB]
    for qc in range(8):
        i = bt * 8 + qc
        par = i % 4
        waits = []
        if i >= 4:
            waits.append((sem.mpe, P.val(sem.mpe) - 3))
        P.op("gpsimd",
             (lambda g, bt=bt, qc=qc, par=par: g.dma_start(
                 out=mt[par][:],
                 in_=maskT2[qc * 128:(qc + 1) * 128,
                            bt * 1024:(bt + 1) * 1024])),
             waits=waits, inc=sem.mdma[par], inc_n=16)
        dmav = P.val(sem.mdma[par])

        def _mm(te, bt=bt, qc=qc, par=par):
            last = None
            for j in range(2):
                last = te.matmul(
                    strip_banks[j][0:1, 0:512],
                    t.ones_bf[:, 0:1],
                    mt[par][:, j * 512:(j + 1) * 512],
                    start=(qc == 0), stop=(qc == 7))
            return last
        mmw = [(sem.mdma[par], dmav)]
        if qc == 0:
            mmw.append((sem.consts, P.val(sem.consts)))
            if bt >= 1:
                mmw.append((sem.mevac, P.val(sem.mevac)))
        P.op("tensor", _mm, waits=mmw, inc=sem.mpe)
    for j in range(2):
        sidx = 2 * bt + j
        par = sidx % 2
        waits = [(sem.mpe, P.val(sem.mpe))]
        if sidx >= 2:
            waits.append((sem.msc[par], P.val(sem.msc[par])))
        P.op("scalar",
             (lambda s, j=j, par=par: s.copy(
                 t.mscat[0:1, par * 512:(par + 1) * 512],
                 strip_banks[j][0:1, 0:512])),
             waits=waits, inc=sem.mevac)
        P.op("gpsimd",
             (lambda g, sidx=sidx, par=par: g.dma_start(
                 out=t.bmq_d[4 * sidx:4 * (sidx + 1), :]
                     .rearrange("(x a) n -> x a n", x=1),
                 in_=t.mscat[0:1, par * 512:(par + 1) * 512]
                     .rearrange("x (a n) -> x a n", a=4))),
             waits=[(sem.mevac, P.val(sem.mevac))],
             inc=sem.msc[par], inc_n=16)


def emit_bmp(P, t, sem):
    P.op("gpsimd", lambda g: g.dma_start(out=t.bmQ[:], in_=t.bmq_d[:, :]),
         waits=[(sem.msc[0], P.val(sem.msc[0])),
                (sem.msc[1], P.val(sem.msc[1]))],
         inc=sem.msc[0], inc_n=16)
    P.op("vector",
         lambda v: v.tensor_scalar(out=t.Qbf[:], in0=t.bmQ[:], scalar1=1.0,
                                   scalar2=None, op0=ALU.is_ge),
         waits=[(sem.msc[0], P.val(sem.msc[0]))],
         inc=sem.mred)
    qready = P.val(sem.mred)
    for c in range(KC):
        waits = [(sem.mred, qready)]
        if c < 2:
            waits.append((sem.gqw, P.val(sem.gqw)))
        if c >= 2:
            waits.append((sem.mevac, P.val(sem.mevac) - 1))
        psq = t.pgC if c % 2 == 0 else t.pgD
        P.op("tensor",
             (lambda te, c=c, psq=psq: te.matmul(
                 psq[:, 0:128], t.E32[:, c * 128:(c + 1) * 128],
                 t.Qbf[:], start=True, stop=True)),
             waits=waits, inc=sem.mpe)
        P.op("scalar",
             (lambda s, c=c, psq=psq: s.copy(
                 t.bmP[:, c * 128:(c + 1) * 128], psq[:, 0:128])),
             waits=[(sem.mpe, P.val(sem.mpe))], inc=sem.mevac)


def emit_ln_stats_load(P, t, sem, xin_tile, tt, st):
    par = tt % 2
    ls = t.lnsc8[:, tt * 16:(tt + 1) * 16]
    P.op("sync",
         (lambda sy, par=par, xin_tile=xin_tile: sy.dma_start(
             out=t.xstage[par][:], in_=xin_tile)),
         waits=st.rel.get(par, []), inc=sem.xdma[par], inc_n=16)
    ld = (sem.xdma[par], P.val(sem.xdma[par]))
    sqv = P.op("scalar",
               (lambda s, par=par, ls=ls: s.activation(
                   out=t.xnbuf[par][:], in_=t.xstage[par][:], func=AF.Square,
                   accum_out=ls[:, 1:2])),
               waits=[ld], inc=sem.sq)
    P.op("vector",
         (lambda v, par=par, ls=ls: v.reduce_sum(
             ls[:, 0:1], t.xstage[par][:], axis=AX.X)),
         waits=[ld], inc=sem.lns[tt])
    st.rel[par] = [(sem.sq, sqv), (sem.lns[tt], 1)]
    st.sqv[tt] = sqv


def emit_ln_stats_smalls(P, t, sem, st):
    def _ls(tt):
        return t.lnsc8[:, tt * 16:(tt + 1) * 16]
    hops = [
        ("vector", (lambda tt: (lambda v, ls=_ls(tt): v.tensor_scalar(
            out=ls[:, 2:4], in0=ls[:, 0:2], scalar1=1.0 / D,
            scalar2=None, op0=ALU.mult)))),
        ("vector", (lambda tt: (lambda v, ls=_ls(tt): v.tensor_scalar(
            out=ls[:, 4:5], in0=ls[:, 2:3], scalar1=ls[:, 2:3],
            scalar2=None, op0=ALU.mult)))),
        ("vector", (lambda tt: (lambda v, ls=_ls(tt): v.tensor_scalar(
            out=ls[:, 5:6], in0=ls[:, 3:4], scalar1=ls[:, 4:5],
            scalar2=LN_EPS, op0=ALU.subtract, op1=ALU.add)))),
        ("scalar", (lambda tt: (lambda s, ls=_ls(tt): s.activation(
            out=ls[:, 6:7], in_=ls[:, 5:6], func=AF.Sqrt)))),
        ("vector", (lambda tt: (lambda v, ls=_ls(tt): v.reciprocal(
            ls[:, 10:11], ls[:, 6:7])))),
        ("vector", (lambda tt: (lambda v, ls=_ls(tt): v.tensor_scalar(
            out=ls[:, 11:12], in0=ls[:, 2:3], scalar1=ls[:, 10:11],
            scalar2=-1.0, op0=ALU.mult, op1=ALU.mult)))),
    ]
    for hi, (eng, mk) in enumerate(hops):
        for tt in range(TT):
            extra = [(sem.sq, st.sqv[tt])] if hi == 0 else []
            P.chain(eng, mk(tt), sem.lns[tt], waits=extra)
    for tt in range(TT):
        st.lnsv[tt] = P.val(sem.lns[tt])


def emit_ln_chunk(P, t, sem, xin, c, st):
    par = c % 2
    waits = []
    if c < 2:
        waits += st.rel[par]
    else:
        waits.append((sem.xn, st.xnv[c - 2]))
        waits.append((sem.xns, st.xnsv[c - 2]))
    P.op("sync",
         (lambda sy, par=par, c=c, xin=xin: sy.dma_start(
             out=t.xck[par][:].rearrange("p (tt n) -> p tt n", tt=TT),
             in_=xin[:, c * 128:(c + 1) * 128]
                 .rearrange("(tt p) n -> p tt n", tt=TT))),
         waits=waits, inc=sem.xdma[par], inc_n=16)
    dv = (sem.xdma[par], P.val(sem.xdma[par]))
    tvals = []
    for tt in range(TT):
        ls = t.lnsc8[:, tt * 16:(tt + 1) * 16]
        w = [dv]
        if c < 2:
            w.append((sem.lns[tt], st.lnsv[tt]))
        else:
            w.append((sem.tpe, st.tpev[c - 2]))
        if tt < 4:
            v = P.op("vector",
                     (lambda v, par=par, tt=tt, ls=ls: v.tensor_scalar(
                         out=t.xnbuf[par][:, tt * 128:(tt + 1) * 128],
                         in0=t.xck[par][:, tt * 128:(tt + 1) * 128],
                         scalar1=ls[:, 10:11], scalar2=ls[:, 11:12],
                         op0=ALU.mult, op1=ALU.add)),
                     waits=w, inc=sem.xn)
            tvals.append((sem.xn, v))
        else:
            v = P.op("scalar",
                     (lambda s, par=par, tt=tt, ls=ls: s.activation(
                         out=t.xnbuf[par][:, tt * 128:(tt + 1) * 128],
                         in_=t.xck[par][:, tt * 128:(tt + 1) * 128],
                         func=AF.Identity, scale=ls[:, 10:11],
                         bias=ls[:, 11:12])),
                     waits=w, inc=sem.xns)
            tvals.append((sem.xns, v))
    st.xnv[c] = P.val(sem.xn)
    st.xnsv[c] = P.val(sem.xns)
    bank = t.pmain[par].bitcast(BF16)
    for tt in range(TT):
        w = [tvals[tt]]
        if c < 2:
            w.append((sem.consts, P.val(sem.consts)))
        else:
            w.append((sem.tevac, st.tev[c - 2]))
        P.op("tensor",
             (lambda te, par=par, tt=tt, bank=bank: te.transpose(
                 bank[:, tt * 128:(tt + 1) * 128],
                 t.xnbuf[par][:, tt * 128:(tt + 1) * 128], t.ident_bf[:])),
             waits=w, inc=sem.tpe)
    st.tpev[c] = P.val(sem.tpe)
    P.op("scalar",
         (lambda s, c=c, bank=bank: s.copy(
             t.xnT[:, c * 1024:(c + 1) * 1024], bank[:, 0:1024])),
         waits=[(sem.tpe, P.val(sem.tpe))], inc=sem.tevac)
    st.tev[c] = P.val(sem.tevac)


def emit_qk_pair(P, t, sem, pr, free_prev):
    C = sem.gch
    gs = t.gsm[0]

    def _qk(te, pr=pr):
        last = None
        for c in range(KC):
            w = t.wring_a if c < 16 else t.wring_f
            cc = c % 16
            for i in range(2):
                tt = 2 * pr + i
                last = te.matmul(
                    t.pmain[2 + i][:, 0:512],
                    t.xnT[:, c * 1024 + tt * 128: c * 1024 + (tt + 1) * 128],
                    w[:, cc * 512:(cc + 1) * 512],
                    start=(c == 0), stop=(c == KC - 1))
        return last
    qv = P.op("tensor", _qk,
              waits=[(C, free_prev), (sem.gqw, P.val(sem.gqw)),
                     (sem.tevac, P.val(sem.tevac))],
              inc=sem.qkp)
    bases = (0, 32)
    for i in range(2):
        bank, base = t.pmain[2 + i], bases[i]
        P.chain("scalar",
                (lambda s, bank=bank: s.activation(
                    out=t.gq[:], in_=bank[:, 0:256], func=AF.Identity,
                    scale=t.cst[:, 0:1])), C,
                waits=([(sem.qkp, qv)] if i == 0 else []))
        P.chain("vector",
                (lambda v, bank=bank: v.tensor_tensor(
                    out=t.gqk[:], in0=bank[:, 256:512], in1=t.gq[:],
                    op=ALU.mult)), C)
        P.chain("vector",
                (lambda v, gs=gs, base=base: v.tensor_reduce(
                    out=gs[:, base:base + 8][:, :, None],
                    in_=t.gqk[:].rearrange("p (a b) -> p a b", a=8),
                    axis=AX.X, op=ALU.add)), C)
    free_val = P.val(C)
    for i in range(2):
        base, tt = bases[i], 2 * pr + i
        P.chain("vector",
                (lambda v, gs=gs, base=base: v.reduce_max(
                    gs[:, base + 8:base + 9], gs[:, base:base + 8],
                    axis=AX.X, negate=True)), C)
        P.chain("scalar",
                (lambda s, gs=gs, base=base: s.activation(
                    out=gs[:, base + 16:base + 24], in_=gs[:, base:base + 8],
                    func=AF.Exp, bias=gs[:, base + 8:base + 9], scale=1.0)), C)
        P.chain("vector",
                (lambda v, gs=gs, base=base: v.reduce_sum(
                    gs[:, base + 9:base + 10], gs[:, base + 16:base + 24],
                    axis=AX.X)), C)
        P.chain("vector",
                (lambda v, gs=gs, base=base: v.reciprocal(
                    gs[:, base + 13:base + 14], gs[:, base + 9:base + 10])), C)
        P.chain("vector",
                (lambda v, gs=gs, base=base, tt=tt: v.tensor_scalar(
                    out=t.gsm[1][:, 8 * tt:8 * (tt + 1)],
                    in0=gs[:, base + 16:base + 24],
                    scalar1=gs[:, base + 13:base + 14], scalar2=None,
                    op0=ALU.mult)), C)
    return free_val


def emit_recip_refine(P, t, sem, gs, src_col, dst_base):
    C = sem.gch
    P.chain("vector",
            (lambda v: v.reciprocal(gs[:, dst_base:dst_base + 1],
                                    gs[:, src_col:src_col + 1])), C)
    P.chain("vector",
            (lambda v: v.tensor_scalar(
                out=gs[:, dst_base + 1:dst_base + 2],
                in0=gs[:, src_col:src_col + 1],
                scalar1=gs[:, dst_base:dst_base + 1], scalar2=None,
                op0=ALU.mult)), C)
    P.chain("vector",
            (lambda v: v.tensor_scalar(
                out=gs[:, dst_base + 2:dst_base + 3],
                in0=gs[:, dst_base + 1:dst_base + 2],
                scalar1=-1.0, scalar2=2.0, op0=ALU.mult, op1=ALU.add)), C)
    P.chain("vector",
            (lambda v: v.tensor_scalar(
                out=gs[:, dst_base + 3:dst_base + 4],
                in0=gs[:, dst_base:dst_base + 1],
                scalar1=gs[:, dst_base + 2:dst_base + 3], scalar2=None,
                op0=ALU.mult)), C)


def emit_hw_phase(P, t, sem, hooks=None):
    C = sem.gch
    gwv = P.val(sem.gw)
    for tt in range(TT):
        def _h(te, tt=tt):
            last = None
            for c in range(KC):
                last = te.matmul(
                    t.pmain[2][:, 0:128],
                    t.xnT[:, c * 1024 + tt * 128: c * 1024 + (tt + 1) * 128],
                    t.a2t_bf[:, c * 128:(c + 1) * 128],
                    start=(c == 0), stop=(c == KC - 1))
            return last
        P.chain("tensor", _h, C, waits=[(sem.gw, gwv)])
        P.chain("vector",
                (lambda v, tt=tt: v.tensor_tensor(
                    out=t.hw_tok[:].rearrange("p (a b) -> p a b", a=8),
                    in0=t.pmain[2][:, 0:128].rearrange("p (a b) -> p a b", a=8),
                    in1=t.gsm[1][:, 8 * tt:8 * (tt + 1)][:, :, None]
                        .broadcast_to([128, 8, 16]),
                    op=ALU.mult)), C)
        P.chain("tensor",
                lambda te: te.transpose(
                    t.pmain[3][:, 0:128].bitcast(BF16)[:, 0:128],
                    t.hw_tok[:], t.ident_bf[:]), C)
        P.chain("scalar",
                (lambda s, tt=tt: s.copy(
                    t.hwT[:, tt * 128:(tt + 1) * 128],
                    t.pmain[3][:, 0:128].bitcast(BF16)[:, 0:128])), C)
        if hooks and tt in hooks:
            hooks[tt]()


def emit_z_chunk_mm(P, t, sem, c, st, first):
    def _z(te, c=c, first=first):
        last = None
        for tt in range(TT):
            last = te.matmul(
                t.pmain[2][:, tt * 40:(tt + 1) * 40],
                t.xnT[:, c * 1024 + tt * 128: c * 1024 + (tt + 1) * 128],
                t.ah8_bf[:, c * 40:(c + 1) * 40],
                start=(first and tt == 0), stop=False)
        return last
    waits = [(sem.tevac, st.tev[c])]
    if first:
        waits.append((sem.gqw, P.val(sem.gqw)))
    P.op("tensor", _z, waits=waits)


def emit_z_close(P, t, sem):
    P.op("tensor",
         lambda te: te.matmul(t.pmain[2][:, 0:320], t.zrow_bf[0:1, 0:128],
                              t.zrow_bf[0:1, 0:320], start=False, stop=True),
         waits=[], inc=sem.zdone)


def emit_z_evac(P, t, sem, tt, zT_bf, zT_f32, h8T, sldv):
    C = sem.gch
    P.chain("scalar",
            (lambda s, tt=tt: s.copy(t.zh_tok[:],
                                     t.pmain[2][:, tt * 40:(tt + 1) * 40])), C,
            waits=([(sem.zdone, 1)] if tt == 0 else []))
    P.chain("tensor",
            lambda te: te.transpose(t.pmain[3][0:32, 0:128], t.zh_tok[:, 0:32],
                                    t.ident_f32[:]), C)

    def _zev(s, tt=tt):
        last = s.copy(zT_bf[:, tt * 128:(tt + 1) * 128],
                      t.pmain[3][0:32, 0:128])
        if zT_f32 is not None:
            last = s.copy(zT_f32[:, tt * 128:(tt + 1) * 128],
                          t.pmain[3][0:32, 0:128])
        return last
    P.chain("scalar", _zev, C)
    P.chain("tensor",
            lambda te: te.transpose(t.pmain[3][0:8, 128:256],
                                    t.zh_tok[:, 32:40], t.ident_f32[:]), C)
    P.chain("scalar",
            (lambda s, tt=tt: s.activation(
                out=h8T[:, tt * 128:(tt + 1) * 128],
                in_=t.pmain[3][0:8, 128:256], func=AF.Identity,
                bias=t.b8col[:, 0:1], scale=1.0)), C,
            waits=([(sem.sld, sldv)] if tt == 0 else []))


def emit_zpre_tile(P, t, sem, tt, zT_bf, zT_f32, h8T):
    C = sem.gch

    def _zmm(te, tt=tt):
        last = None
        for c in range(KC):
            last = te.matmul(
                t.pmain[2][:, 0:40],
                t.zsrc(c, tt),
                t.ah8_bf[:, c * 40:(c + 1) * 40],
                start=(c == 0), stop=(c == KC - 1))
        return last
    P.chain("tensor", _zmm, C,
            waits=[(sem.gqw, P.val(sem.gqw)), (sem.tevac, P.val(sem.tevac)),
                   (sem.sld, P.val(sem.sld))])
    P.chain("scalar", lambda s: s.copy(t.zh_tok[:], t.pmain[2][:, 0:40]), C)
    P.chain("tensor",
            lambda te: te.transpose(t.pmain[3][0:32, 0:128], t.zh_tok[:, 0:32],
                                    t.ident_f32[:]), C)

    def _zev(s, tt=tt):
        last = s.copy(zT_bf[:, tt * 128:(tt + 1) * 128],
                      t.pmain[3][0:32, 0:128])
        if zT_f32 is not None:
            last = s.copy(zT_f32[:, tt * 128:(tt + 1) * 128],
                          t.pmain[3][0:32, 0:128])
        return last
    P.chain("scalar", _zev, C)
    P.chain("tensor",
            lambda te: te.transpose(t.pmain[3][0:8, 128:256],
                                    t.zh_tok[:, 32:40], t.ident_f32[:]), C)
    P.chain("scalar",
            (lambda s, tt=tt: s.activation(
                out=h8T[:, tt * 128:(tt + 1) * 128],
                in_=t.pmain[3][0:8, 128:256], func=AF.Identity,
                bias=t.b8col[:, 0:1], scale=1.0)), C)


def emit_ln_tile(P, t, sem, xin_tile, par, xnt_slot, first_waits=()):
    L = sem.lnch
    ls = t.lnsc[par]
    waits = list(first_waits)
    if P.val(sem.xn) >= 2:
        waits.append((sem.xn, P.val(sem.xn) - 1))
    P.op("sync",
         (lambda sy, par=par, xin_tile=xin_tile: sy.dma_start(
             out=t.xstage[par][:], in_=xin_tile)),
         waits=waits, inc=sem.xdma[par], inc_n=16)
    ld = (sem.xdma[par], P.val(sem.xdma[par]))
    sq_waits = [ld]
    if P.val(sem.tevac) >= 2 * KC:
        sq_waits.append((sem.tevac, P.val(sem.tevac) - KC))
    P.op("scalar",
         (lambda s, par=par, ls=ls: s.activation(
             out=t.xnbuf[par][:], in_=t.xstage[par][:], func=AF.Square,
             accum_out=ls[:, 1:2])),
         waits=sq_waits, inc=sem.sq)
    P.op("vector",
         (lambda v, par=par, ls=ls: v.reduce_sum(
             ls[:, 0:1], t.xstage[par][:], axis=AX.X)),
         waits=[ld, (L, P.val(L))], inc=L)
    P.op("vector",
         (lambda v, ls=ls: v.tensor_scalar(
             out=ls[:, 2:4], in0=ls[:, 0:2], scalar1=1.0 / D,
             scalar2=None, op0=ALU.mult)),
         waits=[(L, P.val(L)), (sem.sq, P.val(sem.sq))], inc=L)
    P.chain("vector",
            (lambda v, ls=ls: v.tensor_scalar(
                out=ls[:, 4:5], in0=ls[:, 2:3], scalar1=ls[:, 2:3],
                scalar2=None, op0=ALU.mult)), L)
    P.chain("vector",
            (lambda v, ls=ls: v.tensor_scalar(
                out=ls[:, 5:6], in0=ls[:, 3:4], scalar1=ls[:, 4:5],
                scalar2=LN_EPS, op0=ALU.subtract, op1=ALU.add)), L)
    P.chain("scalar",
            (lambda s, ls=ls: s.activation(out=ls[:, 6:7], in_=ls[:, 5:6],
                                           func=AF.Sqrt)), L)
    P.chain("vector",
            (lambda v, ls=ls: v.reciprocal(ls[:, 10:11], ls[:, 6:7])), L)
    P.chain("vector",
            (lambda v, ls=ls: v.tensor_scalar(
                out=ls[:, 11:12], in0=ls[:, 2:3], scalar1=ls[:, 10:11],
                scalar2=-1.0, op0=ALU.mult, op1=ALU.mult)), L)
    P.op("scalar",
         (lambda s, par=par, ls=ls: s.activation(
             out=t.xnbuf[par][:], in_=t.xstage[par][:], func=AF.Identity,
             scale=ls[:, 10:11], bias=ls[:, 11:12])),
         waits=[(L, P.val(L))], inc=sem.xn)
    xnv = P.val(sem.xn)
    for c in range(KC):
        psl = t.pmain[c % 2][:, 0:128].bitcast(BF16)
        waits = [(sem.xn, xnv)]
        if P.val(sem.tevac) > 0:
            waits.append((sem.tevac, P.val(sem.tevac) - 1))
        P.op("tensor",
             (lambda te, c=c, par=par, psl=psl: te.transpose(
                 psl[:, 0:128],
                 t.xnbuf[par][:, c * 128:(c + 1) * 128], t.ident_bf[:])),
             waits=waits, inc=sem.tpe)
        P.op("scalar",
             (lambda s, psl=psl, slot=xnt_slot(c): s.copy(slot, psl[:, 0:128])),
             waits=[(sem.tpe, P.val(sem.tpe))], inc=sem.tevac)


def emit_anchor_gate(P, t, sem, zT_bf_anch, h8T_anch, limt_view):
    C = sem.gch
    for aj in range(4):
        P.chain("tensor",
                (lambda te, aj=aj: te.matmul(
                    t.pmain[2][:, 64 + aj * 8:64 + (aj + 1) * 8],
                    zT_bf_anch[:, aj * 128:(aj + 1) * 128],
                    limt_view, start=True, stop=True)), C)
        P.chain("tensor",
                (lambda te, aj=aj: te.transpose(
                    t.pmain[3][:, 256 + aj * 8:256 + (aj + 1) * 8],
                    h8T_anch[:, aj * 128:(aj + 1) * 128],
                    t.ident_f32[0:8, 0:8])), C)
    P.chain("scalar", lambda s: s.copy(t.dsbuf[:], t.pmain[2][:, 64:96]), C)
    P.chain("scalar", lambda s: s.copy(t.hsbuf[:], t.pmain[3][:, 256:288]), C)


def emit_locmax(P, t, sem):
    C = sem.gch
    P.chain("vector",
            lambda v: v.reduce_max(t.gsm[0][:, 0:1], t.hsbuf[:], axis=AX.X,
                                   apply_absolute_value=True), C)
    P.chain("vector",
            lambda v: v.reduce_max(t.gsm[0][:, 1:2], t.dsbuf[:], axis=AX.X,
                                   apply_absolute_value=True), C)
    P.chain("tensor",
            lambda te: te.transpose(t.pmain[3][0:2, 0:128], t.gsm[0][:, 0:2],
                                    t.ident_f32[:]), C)
    P.chain("scalar",
            lambda s: s.copy(t.max2x128[:], t.pmain[3][0:2, 0:128]), C)
    P.chain("vector",
            lambda v: v.reduce_max(t.locmax[:], t.max2x128[:], axis=AX.X), C)


def emit_l1_post(P, t, sem, gmax_dram, gsc_dram, bmat_dram, lim_dram):
    C = sem.gch
    g0 = t.gsm[0]
    P.op("sync", lambda sy: sy.dma_start(out=t.gmax8[:], in_=gmax_dram[:, :]),
         waits=[(sem.xn, P.val(sem.xn)), (sem.xns, P.val(sem.xns))],
         inc=sem.sld, inc_n=16)
    P.op("gpsimd",
         lambda g: g.dma_start(
             out=t.bmat_bf[:].rearrange("p (c n) -> p c n", c=1),
             in_=bmat_dram[:, :].rearrange("(c p) n -> p c n", c=1)),
         waits=[(sem.mpe, P.val(sem.mpe))], inc=sem.gld, inc_n=16)
    P.op("sync", lambda sy: sy.dma_start(out=t.lim_f32[:], in_=lim_dram[:, :]),
         waits=[(sem.xn, P.val(sem.xn)), (sem.xns, P.val(sem.xns))],
         inc=sem.sld, inc_n=16)
    P.op("vector", None,
         waits=[(sem.gld, P.val(sem.gld)), (sem.sld, P.val(sem.sld)),
                (sem.consts, P.val(sem.consts)), (sem.mevac, P.val(sem.mevac)),
                (sem.gch, P.val(sem.gch))],
         inc=C)
    P.chain("scalar", lambda s: s.activation(out=t.lim_bf[:], in_=t.lim_f32[:],
                                             func=AF.Identity,
                                             scale=t.cst[0:8, 2:3]), C)
    P.chain("tensor",
            lambda te: te.transpose(t.pgB[0:2, 0:8], t.gmax8[:],
                                    t.ident_f32[0:8, 0:8]), C)
    P.chain("scalar", lambda s: s.copy(t.max2x128[:, 0:8], t.pgB[0:2, 0:8]), C)
    P.chain("vector",
            lambda v: v.reduce_max(t.gsc2[:, 0:1], t.max2x128[:, 0:8],
                                   axis=AX.X), C)
    P.chain("vector",
            lambda v: v.tensor_scalar(out=t.gsc2[:, 1:2], in0=t.gsc2[:, 0:1],
                                      scalar1=1e-6, scalar2=None,
                                      op0=ALU.max), C)
    P.chain("sync",
            lambda sy: sy.dma_start(out=gsc_dram[:, :], in_=t.gsc2[:, 1:2]),
            C, inc_n=16)
    P.chain("sync",
            lambda sy: sy.dma_start(
                out=t.gscH[:], in_=gsc_dram[0:1, 0:1].to_broadcast((128, 1))),
            C, inc_n=16)
    P.chain("sync",
            lambda sy: sy.dma_start(
                out=t.gscD[:], in_=gsc_dram[1:2, 0:1].to_broadcast((128, 1))),
            C, inc_n=16)
    P.chain("vector", lambda v: v.tensor_copy(g0[:, 0:1], t.gscH[:]), C)
    emit_recip_refine(P, t, sem, g0, 0, 1)
    P.chain("vector", lambda v: v.tensor_copy(g0[:, 8:9], t.gscD[:]), C)
    emit_recip_refine(P, t, sem, g0, 8, 9)
    P.chain("vector",
            lambda v: v.tensor_scalar(out=g0[:, 16:17], in0=g0[:, 4:5],
                                      scalar1=0.5, scalar2=None,
                                      op0=ALU.mult), C)
    P.chain("vector",
            lambda v: v.tensor_scalar(out=g0[:, 17:18], in0=g0[:, 12:13],
                                      scalar1=0.5, scalar2=None,
                                      op0=ALU.mult), C)

    for aj in range(4):
        gs = t.gsm[1]
        sl = slice(aj * 8, (aj + 1) * 8)
        P.chain("vector",
                (lambda v, gs=gs, sl=sl: v.tensor_scalar(
                    out=gs[:, 16:24], in0=t.dsbuf[:, sl],
                    scalar1=g0[:, 17:18], scalar2=None, op0=ALU.mult)), C)
        P.chain("vector",
                (lambda v, gs=gs, sl=sl: v.scalar_tensor_tensor(
                    out=gs[:, 16:24], in0=t.hsbuf[:, sl], scalar=g0[:, 16:17],
                    in1=gs[:, 16:24], op0=ALU.mult, op1=ALU.add)), C)
        P.chain("vector",
                (lambda v, gs=gs: v.reduce_max(gs[:, 24:25], gs[:, 16:24],
                                               axis=AX.X, negate=True)), C)
        P.chain("scalar",
                (lambda s, gs=gs: s.activation(
                    out=gs[:, 32:40], in_=gs[:, 16:24], func=AF.Exp,
                    bias=gs[:, 24:25], scale=1.0)), C)
        P.chain("vector",
                (lambda v, gs=gs: v.reduce_sum(gs[:, 25:26], gs[:, 32:40],
                                               axis=AX.X)), C)
        emit_recip_refine(P, t, sem, gs, 25, 26)
        P.chain("vector",
                (lambda v, gs=gs: v.tensor_scalar(
                    out=gs[:, 40:48], in0=gs[:, 32:40],
                    scalar1=gs[:, 29:30], scalar2=None, op0=ALU.mult)), C)
        P.chain("vector",
                (lambda v, gs=gs: v.reduce_max(gs[:, 48:49], gs[:, 40:48],
                                               axis=AX.X)), C)
        P.chain("vector",
                (lambda v, gs=gs: v.tensor_scalar(
                    out=gs[:, 56:64], in0=gs[:, 40:48],
                    scalar1=gs[:, 48:49], scalar2=None, op0=ALU.is_ge)), C)
        P.chain("vector",
                (lambda v, gs=gs: v.reduce_sum(gs[:, 49:50], gs[:, 56:64],
                                               axis=AX.X)), C)
        P.chain("vector",
                (lambda v, gs=gs: v.scalar_tensor_tensor(
                    out=gs[:, 56:64], in0=gs[:, 56:64], scalar=-1e30,
                    in1=gs[:, 40:48], op0=ALU.mult, op1=ALU.add)), C)
        P.chain("vector",
                (lambda v, gs=gs: v.reduce_max(gs[:, 50:51], gs[:, 56:64],
                                               axis=AX.X)), C)
        P.chain("vector",
                (lambda v, gs=gs: v.tensor_scalar(
                    out=gs[:, 51:52], in0=gs[:, 49:50], scalar1=2.0,
                    scalar2=None, op0=ALU.is_ge)), C)
        P.chain("vector",
                (lambda v, gs=gs: v.tensor_scalar(
                    out=gs[:, 52:53], in0=gs[:, 48:49],
                    scalar1=gs[:, 50:51], scalar2=None, op0=ALU.subtract)), C)
        P.chain("vector",
                (lambda v, gs=gs: v.tensor_scalar(
                    out=gs[:, 53:54], in0=gs[:, 52:53],
                    scalar1=gs[:, 51:52], scalar2=None, op0=ALU.mult)), C)
        P.chain("vector",
                (lambda v, gs=gs: v.tensor_scalar(
                    out=gs[:, 54:55], in0=gs[:, 53:54],
                    scalar1=gs[:, 50:51], scalar2=None, op0=ALU.add)), C)
        P.chain("vector",
                (lambda v, gs=gs: v.tensor_scalar(
                    out=gs[:, 55:56], in0=gs[:, 54:55], scalar1=-2.0,
                    scalar2=None, op0=ALU.mult)), C)
        P.chain("scalar",
                (lambda s, gs=gs: s.activation(
                    out=gs[:, 56:64], in_=gs[:, 40:48], func=AF.Sigmoid,
                    bias=gs[:, 55:56], scale=t.cst[:, 1:2])), C)
        P.chain("vector",
                (lambda v, gs=gs: v.tensor_tensor(
                    out=gs[:, 40:48], in0=gs[:, 40:48], in1=gs[:, 56:64],
                    op=ALU.mult)), C)
        P.chain("vector",
                (lambda v, gs=gs: v.reduce_sum(gs[:, 25:26], gs[:, 40:48],
                                               axis=AX.X)), C)
        P.chain("vector",
                (lambda v, gs=gs: v.tensor_scalar(
                    out=gs[:, 26:27], in0=gs[:, 25:26], scalar1=1e-9,
                    scalar2=None, op0=ALU.add)), C)
        emit_recip_refine(P, t, sem, gs, 26, 27)
        P.chain("vector",
                (lambda v, gs=gs: v.tensor_scalar(
                    out=gs[:, 40:48], in0=gs[:, 40:48],
                    scalar1=gs[:, 30:31], scalar2=None, op0=ALU.mult)), C)
        P.chain("tensor",
                (lambda te, gs=gs: te.transpose(
                    t.pgB[0:8, 0:128], gs[:, 40:48], t.ident_f32[:])), C)
        P.chain("scalar",
                (lambda s, aj=aj: s.copy(
                    t.wfullT[:, aj * 256:(aj + 1) * 256]
                        .rearrange("p (a two) -> p a two", two=2),
                    t.pgB[0:8, 0:128][:, :, None].broadcast_to([8, 128, 2]))),
                C)
    P.chain("tensor",
            lambda te: te.matmul(t.pgA[0:32, 0:512], t.lim_bf[:],
                                 t.wfullT[:, 0:512], start=True, stop=True), C)
    P.chain("tensor",
            lambda te: te.matmul(t.pgC[0:32, 0:512], t.lim_bf[:],
                                 t.wfullT[:, 512:1024], start=True, stop=True),
            C)
    P.chain("vector",
            lambda v: v.tensor_tensor(out=t.zp_bf[:, 0:512],
                                      in0=t.zT_f32[:, 0:512],
                                      in1=t.pgA[0:32, 0:512], op=ALU.mult), C)
    P.chain("vector",
            lambda v: v.tensor_tensor(out=t.zp_bf[:, 512:1024],
                                      in0=t.zT_f32[:, 512:1024],
                                      in1=t.pgC[0:32, 0:512], op=ALU.mult), C)


def emit_main(P, t, sem, wa_dram, wf_dram, xin, xout, layer, extra_wait=(),
              ring_free_wait=(), x0_first_wait=()):
    for s in range(NSUP):
        for g in range(NG):
            for (ring, wd, wsems, cons_base) in (
                    (t.wring_f, wf_dram, sem.wdf, 0),
                    (t.wring_a, wa_dram, sem.wda, KC)):
                wsem = wsems[g]
                waits = []
                if s == 0:
                    waits.extend(ring_free_wait)
                if s > 0:
                    need = (s - 1) * 4 * KC + 2 * (cons_base + (g + 1) * GRP)
                    waits.append((sem.mmc, need))
                P.op("gpsimd",
                     (lambda gg, ring=ring, wd=wd, s=s, g=g: gg.dma_start(
                         out=ring[:, g * GRP * NW:(g + 1) * GRP * NW]
                             .rearrange("p (c n) -> p c n", c=GRP),
                         in_=wd[g * GRP * 128:(g + 1) * GRP * 128,
                                s * NW:(s + 1) * NW]
                             .rearrange("(c p) n -> p c n", c=GRP))),
                     waits=waits, inc=wsem, inc_n=16)
        for g in range(NG):
            P.op("vector",
                 (lambda v, s=s, g=g: v.tensor_tensor(
                     out=t.wring_a[:, g * GRP * NW:(g + 1) * GRP * NW]
                         .rearrange("p (c a b) -> p c a b", c=GRP, a=8),
                     in0=t.wring_a[:, g * GRP * NW:(g + 1) * GRP * NW]
                         .rearrange("p (c a b) -> p c a b", c=GRP, a=8),
                     in1=t.bmP[:].rearrange("p (c sp a) -> p c sp a", c=KC, sp=16)
                         [:, g * GRP:(g + 1) * GRP, s, :][:, :, :, None]
                         .broadcast_to([128, GRP, 8, 32]),
                     op=ALU.mult)),
                 waits=[(sem.wda[g], P.val(sem.wda[g])),
                        (sem.mevac, P.val(sem.mevac))],
                 inc=sem.wmask)
        P.op("sync",
             (lambda sy, s=s: sy.dma_start(
                 out=t.x0stage[s % 2][:].rearrange("p (tt n) -> p tt n", tt=TT),
                 in_=xin[:, s * NW:(s + 1) * NW]
                     .rearrange("(tt p) n -> p tt n", tt=TT))),
             waits=([(sem.evac, P.val(sem.evac) - TT)] if s >= 2
                    else list(x0_first_wait)),
             inc=sem.x0d, inc_n=16)
        x0v = P.val(sem.x0d)

        for ci in range(2 * KC):
            is_ffn = ci < KC
            c = ci if is_ffn else ci - KC
            ring = t.wring_f if is_ffn else t.wring_a
            g = c // GRP
            waits = []
            if is_ffn:
                waits.append((sem.wdf[g], P.val(sem.wdf[g])))
            else:
                waits.append((sem.wmask, P.val(sem.wmask) - (NG - 1 - g)))
            if ci == 0:
                if s == 0:
                    waits.append((sem.tevac, P.val(sem.tevac)))
                    waits.append((sem.gch, P.val(sem.gch)))
                    waits.extend(extra_wait)
                ev0 = P.val(sem.evac)
                for tt in range(TT):
                    w2 = list(waits)
                    if s > 0:
                        w2.append((sem.evac, ev0 - (TT - 1 - tt)))
                    P.op("tensor",
                         (lambda te, c=c, ring=ring, tt=tt: te.matmul(
                             t.pmain[tt][:, 0:NW],
                             t.xnT[:, c * 1024 + tt * 128:
                                   c * 1024 + (tt + 1) * 128],
                             ring[:, c * NW:(c + 1) * NW],
                             start=True, stop=False)),
                         waits=w2,
                         inc=(sem.mmc if tt == TT - 1 else None), inc_n=2)
                continue

            def _mms(te, c=c, ring=ring):
                last = None
                for tt in range(TT):
                    last = te.matmul(
                        t.pmain[tt][:, 0:NW],
                        t.xnT[:, c * 1024 + tt * 128: c * 1024 + (tt + 1) * 128],
                        ring[:, c * NW:(c + 1) * NW],
                        start=False, stop=False)
                return last
            P.op("tensor", _mms, waits=waits, inc=sem.mmc, inc_n=2)
        for tt in range(TT):
            def _fin(te, tt=tt, s=s, layer=layer):
                pm = t.pmain[tt][:, 0:NW]
                if layer == 0:
                    te.matmul(pm, t.hwT[:, tt * 128:(tt + 1) * 128],
                              t.bm2_bf[:, s * NW:(s + 1) * NW],
                              start=False, stop=False)
                else:
                    te.matmul(pm, t.zp_bf[:, tt * 128:(tt + 1) * 128],
                              t.bmat_bf[:, s * NW:(s + 1) * NW],
                              start=False, stop=False)
                return te.matmul(pm, t.onesrow_bf[0:1, 0:128],
                                 t.biasrow_bf[0:1, s * NW:(s + 1) * NW],
                                 start=False, stop=True)
            P.op("tensor", _fin,
                 waits=([(sem.xb, 1)] if s == 0 else []), inc=sem.mmf)
            P.op("vector",
                 (lambda v, tt=tt, s=s: v.tensor_tensor(
                     out=t.ostage[s % 2][:, tt * NW:(tt + 1) * NW],
                     in0=t.pmain[tt][:, 0:NW],
                     in1=t.x0stage[s % 2][:, tt * NW:(tt + 1) * NW],
                     op=ALU.add)),
                 waits=[(sem.mmf, P.val(sem.mmf)), (sem.x0d, x0v)]
                 + ([(sem.ost, P.val(sem.ost))] if s >= 2 else []),
                 inc=sem.evac)
        P.op("sync",
             (lambda sy, s=s: sy.dma_start(
                 out=xout[:, s * NW:(s + 1) * NW]
                     .rearrange("(tt p) n -> p tt n", tt=TT),
                 in_=t.ostage[s % 2][:].rearrange("p (tt n) -> p tt n", tt=TT))),
             waits=[(sem.evac, P.val(sem.evac))], inc=sem.ost, inc_n=16)



def alloc_common(nc, es, t):
    sb = nc.sbuf_tensor
    t.xnT = es.enter_context(sb("xnT", [128, KC * 1024], BF16)).ap()
    t.wring_a = es.enter_context(sb("wring_a", [128, 8192], BF16)).ap()
    t.wring_f = es.enter_context(sb("wring_f", [128, 8192], BF16)).ap()
    xbig = [es.enter_context(sb(f"xbig{i}", [128, D], F32)).ap()
            for i in range(2)]
    t.xstage = xbig
    t.ostage = [xb[:, 0:TT * NW] for xb in xbig]
    t.x0stage = [xb[:, TT * NW:2 * TT * NW] for xb in xbig]
    t.xck = [xb[:, 0:1024] for xb in xbig]
    mbig = es.enter_context(sb("mbig", [128, 2048], F32)).ap()
    t.mbig = mbig
    t.bmQ = es.enter_context(sb("bmQ", [128, 128], F32)).ap()
    t.Qbf = es.enter_context(sb("Qbf", [128, 128], BF16)).ap()
    t.bmP = es.enter_context(sb("bmP", [128, D], BF16)).ap()
    scratch16 = es.enter_context(sb("scratch16", [128, D], F32)).ap()
    t.E32 = scratch16.bitcast(BF16)[:, 0:D]
    t.brows0 = scratch16[0:1, :]
    t.xnbuf = [es.enter_context(sb(f"xnbuf{i}", [128, D], BF16)).ap()
               for i in range(2)]
    t.lnsc = [es.enter_context(sb(f"lnsc{i}", [128, 16], F32)).ap()
              for i in range(2)]
    t.lnsc8 = es.enter_context(sb("lnsc8", [128, 128], F32)).ap()
    t.ones_f32 = es.enter_context(sb("ones_f32", [1, 128], F32)).ap()
    t.ident_bf = es.enter_context(sb("ident_bf", [128, 128], BF16)).ap()
    t.ident_f32 = es.enter_context(sb("ident_f32", [128, 128], F32)).ap()
    t.gsm = [es.enter_context(sb(f"gsm{i}", [128, 64], F32)).ap()
             for i in range(2)]
    t.cst = es.enter_context(sb("cst", [128, 4], F32)).ap()
    t.zrow_bf = es.enter_context(sb("zrow_bf", [1, 512], BF16)).ap()
    t.onesrow_bf = es.enter_context(sb("onesrow_bf", [1, 128], BF16)).ap()
    t.biasrow_bf = es.enter_context(sb("biasrow_bf", [1, D], BF16)).ap()
    t.ones_bf = es.enter_context(sb("ones_bf", [128, 1], BF16)).ap()
    t.mscat = es.enter_context(sb("mscat", [1, 1024], F32)).ap()
    t.bm2_bf = t.mbig.bitcast(BF16)[:, 0:D]
    t.bmat_bf = t.mbig.bitcast(BF16)[0:32, 0:D]
    t.ah8_bf = es.enter_context(sb("ah8_bf", [128, KC * 40], BF16)).ap()
    t.limt_f32 = es.enter_context(sb("limt_f32", [32, 8], F32)).ap()
    t.limt_bf = es.enter_context(sb("limt_bf", [32, 8], BF16)).ap()
    t.b8col = es.enter_context(sb("b8col", [8, 1], F32)).ap()


def alloc_psums(nc, es, t):
    ps = nc.psum_tensor
    banks = [es.enter_context(ps(f"bank{i}", [128, 512], F32)).ap()
             for i in range(8)]
    t.pmain = banks
    t.pgA, t.pgB, t.pgC, t.pgD = banks[4], banks[5], banks[6], banks[7]


class SemNames:
    names = ("consts", "mred", "mpe", "mevac", "sq", "xn", "xns", "tpe",
             "tevac", "gw", "gqw", "gch", "gld", "bld", "wmask", "mmc", "mmf",
             "x0d", "evac", "ost", "fin", "lnch", "sld", "bsum", "zdone",
             "xb", "qkp")
    arrays = {"mdma": 4, "xdma": 2, "wda": NG, "wdf": NG, "msc": 2,
              "lns": TT}


def make_sems(nc, es):
    sem = NS()
    for n in SemNames.names:
        setattr(sem, n, es.enter_context(nc.semaphore(n)))
    for n, k in SemNames.arrays.items():
        setattr(sem, n, [es.enter_context(nc.semaphore(f"{n}{i}"))
                         for i in range(k)])
    return sem



def build_launch_a():
    nc = bass.Bass("TRN2", target_bir_lowering=False, debug=False)
    dt = nc.dram_tensor
    x0 = dt("x0", [TOK, D], F32, kind="ExternalInput").ap()
    wt_l0a = dt("wt_l0a", [D, D], F32, kind="ExternalInput").ap()
    wt_l0f = dt("wt_l0f", [D, D], F32, kind="ExternalInput").ap()
    maskt0 = dt("maskt0", [1024, 16384], I32, kind="ExternalInput").ap()
    wqk = dt("wqk", [D, 512], F32, kind="ExternalInput").ap()
    a2t = dt("a2t", [D, 128], F32, kind="ExternalInput").ap()
    bm2 = dt("bm2", [128, D], F32, kind="ExternalInput").ap()
    b_l0a = dt("b_l0a", [1, D], F32, kind="ExternalInput").ap()
    b_l0f = dt("b_l0f", [1, D], F32, kind="ExternalInput").ap()
    ah8 = dt("ah8", [D, 40], F32, kind="ExternalInput").ap()
    b_l1f = dt("b_l1f", [1, D], F32, kind="ExternalInput").ap()
    limt = dt("limt", [32, 8], F32, kind="ExternalInput").ap()
    e32h = dt("e32h", [128, D], F32, kind="ExternalInput").ap()

    x1 = dt("x1", [TOK, D], F32, kind="ExternalOutput").ap()
    locmax = dt("locmax", [2, 1], F32, kind="ExternalOutput").ap()
    bmq_d = dt("bmq_d", [128, 128], F32).ap()

    es = contextlib.ExitStack()
    t = NS()
    alloc_common(nc, es, t)
    t.bmq_d = bmq_d
    sb = nc.sbuf_tensor
    t.a2t_bf = es.enter_context(sb("a2t_bf", [128, D], BF16)).ap()
    t.brows1 = t.xstage[0][0:1, :]
    t.gq = es.enter_context(sb("gq", [128, 256], F32)).ap()
    t.gqk = es.enter_context(sb("gqk", [128, 256], F32)).ap()
    t.hw_tok = es.enter_context(sb("hw_tok", [128, 128], BF16)).ap()
    t.hwT = es.enter_context(sb("hwT", [128, 1024], BF16)).ap()
    t.locmax = es.enter_context(sb("locmax_sb", [2, 1], F32)).ap()
    bmPf = t.bmP.bitcast(F32)
    t.zh_tok = bmPf[:, 0:40]
    t.zTa_bf = t.bmP[0:32, 2048:2560]
    t.h8Ta = bmPf[0:8, 64:576]
    t.dsbuf = bmPf[:, 1024:1056]
    t.hsbuf = bmPf[:, 1056:1088]
    t.max2x128 = bmPf[0:2, 1100:1228]
    alloc_psums(nc, es, t)
    sem = make_sems(nc, es)
    block = es.enter_context(nc.Block())

    with es:
        P = Prog()
        emit_consts(P, t, sem)
        P.op("gpsimd",
             lambda g: g.dma_start(
                 out=t.ah8_bf[:].rearrange("p (c n) -> p c n", c=KC),
                 in_=ah8[:, :].rearrange("(c p) n -> p c n", c=KC)),
             inc=sem.gqw, inc_n=16)
        for bt in range(2):
            emit_mask2_group(P, t, sem, maskt0, bt)
        P.op("gpsimd",
             lambda g: g.dma_start(
                 out=t.wring_a[:].rearrange("p (c n) -> p c n", c=16),
                 in_=wqk[0:2048, :].rearrange("(c p) n -> p c n", c=16)),
             inc=sem.gqw, inc_n=16)
        P.op("gpsimd",
             lambda g: g.dma_start(
                 out=t.wring_f[:].rearrange("p (c n) -> p c n", c=16),
                 in_=wqk[2048:4096, :].rearrange("(c p) n -> p c n", c=16)),
             inc=sem.gqw, inc_n=16)
        P.op("gpsimd",
             lambda g: g.dma_start(
                 out=t.E32[:].rearrange("p (c n) -> p c n", c=1),
                 in_=e32h[:, :].rearrange("(c p) n -> p c n", c=1)),
             inc=sem.gqw, inc_n=16)
        P.op("gpsimd",
             lambda g: g.dma_start(
                 out=t.a2t_bf[:, 0:D].rearrange("p (c n) -> p c n", c=KC),
                 in_=a2t[:, :].rearrange("(c p) n -> p c n", c=KC)),
             inc=sem.gw, inc_n=16)
        P.op("sync", lambda sy: sy.dma_start(out=t.limt_f32[:], in_=limt[:, :]),
             inc=sem.sld, inc_n=16)
        P.op("sync",
             lambda sy: sy.dma_start(
                 out=t.b8col[:], in_=b_l1f[0:1, 0:8]
                     .rearrange("x (p y) -> p (x y)", p=8)),
             inc=sem.sld, inc_n=16)
        P.chain("scalar", lambda s: s.copy(t.limt_bf[:], t.limt_f32[:]),
                sem.gch, waits=[(sem.sld, P.val(sem.sld))])

        st = NS()
        st.rel, st.xnv, st.xnsv, st.tpev, st.tev = {}, {}, {}, {}, {}
        st.sqv, st.lnsv = {}, {}
        nbt = 2
        for tt in range(TT):
            emit_ln_stats_load(P, t, sem, x0[tt * 128:(tt + 1) * 128, :], tt, st)
            if tt % 2 == 1 or tt >= 4:
                emit_mask2_group(P, t, sem, maskt0, nbt)
                nbt += 1
        emit_ln_stats_smalls(P, t, sem, st)
        for c in range(KC):
            emit_ln_chunk(P, t, sem, x0, c, st)
            if c in (4, 10, 16, 22, 28):
                emit_mask2_group(P, t, sem, maskt0, nbt)
                nbt += 1
        fp = P.val(sem.gch)
        for pr in range(4):
            fp = emit_qk_pair(P, t, sem, pr, fp)
            if pr >= 1:
                emit_mask2_group(P, t, sem, maskt0, nbt)
                nbt += 1
        ring_free = P.val(sem.qkp)
        emit_hw_phase(P, t, sem)
        emit_bmp(P, t, sem)
        P.op("gpsimd",
             lambda g: g.dma_start(
                 out=t.bm2_bf[:].rearrange("p (c n) -> p c n", c=1),
                 in_=bm2[:, :].rearrange("(c p) n -> p c n", c=1)),
             waits=[(sem.mpe, P.val(sem.mpe))], inc=sem.gw, inc_n=16)
        P.op("sync", lambda sy: sy.dma_start(out=t.brows0[0:1, :], in_=b_l0a[:, :]),
             waits=[(sem.mpe, P.val(sem.mpe))], inc=sem.bld, inc_n=16)
        P.op("sync", lambda sy: sy.dma_start(out=t.brows1[0:1, :], in_=b_l0f[:, :]),
             inc=sem.bld, inc_n=16)
        P.op("vector",
             lambda v: v.tensor_tensor(out=t.brows0[0:1, :], in0=t.brows0[0:1, :],
                                       in1=t.brows1[0:1, :], op=ALU.add),
             waits=[(sem.bld, P.val(sem.bld))], inc=sem.bsum)
        P.op("scalar", lambda s: s.copy(t.biasrow_bf[0:1, :], t.brows0[0:1, :]),
             waits=[(sem.bsum, 1)], inc=sem.xb)
        emit_main(P, t, sem, wt_l0a, wt_l0f, x0, x1, layer=0,
                  extra_wait=[(sem.bld, P.val(sem.bld)), (sem.bsum, 1),
                              (sem.gw, P.val(sem.gw)),
                              (sem.mevac, P.val(sem.mevac))],
                  ring_free_wait=[(sem.qkp, ring_free)],
                  x0_first_wait=[(sem.sq, 8), (sem.lns[6], 1),
                                 (sem.lns[7], 1), (sem.bsum, 1)])
        P.op("sync", None, waits=[(sem.ost, P.val(sem.ost))], inc=sem.fin)
        finv = P.val(sem.fin)
        for e in ("scalar", "vector", "tensor", "gpsimd"):
            P.op(e, None, waits=[(sem.fin, finv)])
        x1odd = x1[1::2, :]

        def slot_a(c, at):
            if c < 16:
                return t.wring_a[:, c * 512 + at * 128: c * 512 + (at + 1) * 128]
            return t.wring_f[:, (c - 16) * 512 + at * 128:
                             (c - 16) * 512 + (at + 1) * 128]
        t.zsrc = slot_a
        for at in range(4):
            emit_ln_tile(P, t, sem, x1odd[at * 128:(at + 1) * 128, :], at % 2,
                         (lambda c, at=at: slot_a(c, at)),
                         first_waits=[(sem.fin, finv)])
            emit_zpre_tile(P, t, sem, at, t.zTa_bf, None, t.h8Ta)
        emit_anchor_gate(P, t, sem, t.zTa_bf, t.h8Ta, t.limt_bf[:])
        emit_locmax(P, t, sem)
        P.op("sync", lambda sy: sy.dma_start(out=locmax[:, :], in_=t.locmax[:]),
             waits=[(sem.gch, P.val(sem.gch))], inc=sem.ost, inc_n=16)
        P.op("sync", None, waits=[(sem.ost, P.val(sem.ost))])

        for name in ENGS:
            getattr(block, name)(lambda eng, name=name: P.replay(name, eng))
    return nc


def build_launch_b():
    nc = bass.Bass("TRN2", target_bir_lowering=False, debug=False)
    dt = nc.dram_tensor
    x1 = dt("x1", [TOK, D], F32, kind="ExternalInput").ap()
    gmax8 = dt("gmax8", [8, 2], F32, kind="ExternalInput").ap()
    wt_l1a = dt("wt_l1a", [D, D], F32, kind="ExternalInput").ap()
    wt_l1f = dt("wt_l1f", [D, D], F32, kind="ExternalInput").ap()
    maskt1 = dt("maskt1", [1024, 16384], I32, kind="ExternalInput").ap()
    ah8 = dt("ah8", [D, 40], F32, kind="ExternalInput").ap()
    limt = dt("limt", [32, 8], F32, kind="ExternalInput").ap()
    lim = dt("lim", [8, 32], F32, kind="ExternalInput").ap()
    bmat = dt("bmat", [32, D], F32, kind="ExternalInput").ap()
    b_l1f = dt("b_l1f", [1, D], F32, kind="ExternalInput").ap()
    e32h = dt("e32h", [128, D], F32, kind="ExternalInput").ap()

    y = dt("y", [TOK, D], F32, kind="ExternalOutput").ap()
    gsc_dram = dt("gsc_scratch", [2, 1], F32).ap()
    bmq_d = dt("bmq_d", [128, 128], F32).ap()

    es = contextlib.ExitStack()
    t = NS()
    alloc_common(nc, es, t)
    t.bmq_d = bmq_d
    sb = nc.sbuf_tensor
    t.zp_bf = es.enter_context(sb("zp_bf", [32, 1024], BF16)).ap()
    t.zT_f32 = es.enter_context(sb("zT_f32", [32, 1024], F32)).ap()
    t.zT_bf = es.enter_context(sb("zT_bf", [32, 1024], BF16)).ap()
    t.h8T = es.enter_context(sb("h8T", [8, 1024], F32)).ap()
    t.zh_tok = es.enter_context(sb("zh_tok", [128, 40], F32)).ap()
    t.dsbuf = es.enter_context(sb("dsbuf", [128, 32], F32)).ap()
    t.hsbuf = es.enter_context(sb("hsbuf", [128, 32], F32)).ap()
    xb0, xb1 = t.xstage[0], t.xstage[1]
    t.wfullT = xb0.bitcast(BF16)[0:8, 5120:6144]
    t.max2x128 = xb1[0:2, 1064:1192]
    t.gmax8 = xb1[0:8, 1192:1194]
    t.gsc2 = xb1[0:2, 1200:1202]
    t.gscH = xb1[:, 1208:1209]
    t.gscD = xb1[:, 1209:1210]
    t.lim_f32 = xb1[0:8, 1216:1248]
    t.lim_bf = xb1.bitcast(BF16)[0:8, 2496:2528]
    alloc_psums(nc, es, t)
    sem = make_sems(nc, es)
    block = es.enter_context(nc.Block())

    with es:
        P = Prog()
        emit_consts(P, t, sem)
        P.op("gpsimd",
             lambda g: g.dma_start(
                 out=t.ah8_bf[:].rearrange("p (c n) -> p c n", c=KC),
                 in_=ah8[:, :].rearrange("(c p) n -> p c n", c=KC)),
             inc=sem.gqw, inc_n=16)
        for bt in range(4):
            emit_mask2_group(P, t, sem, maskt1, bt)
        P.op("gpsimd",
             lambda g: g.dma_start(
                 out=t.E32[:].rearrange("p (c n) -> p c n", c=1),
                 in_=e32h[:, :].rearrange("(c p) n -> p c n", c=1)),
             inc=sem.gqw, inc_n=16)
        P.op("sync", lambda sy: sy.dma_start(out=t.limt_f32[:], in_=limt[:, :]),
             inc=sem.sld, inc_n=16)
        sldv = P.val(sem.sld)
        P.op("sync",
             lambda sy: sy.dma_start(
                 out=t.b8col[:], in_=b_l1f[0:1, 0:8]
                     .rearrange("x (p y) -> p (x y)", p=8)),
             inc=sem.sld, inc_n=16)
        b8v = P.val(sem.sld)
        P.chain("scalar", lambda s: s.copy(t.limt_bf[:], t.limt_f32[:]),
                sem.gch, waits=[(sem.sld, b8v)])

        st = NS()
        st.rel, st.xnv, st.xnsv, st.tpev, st.tev = {}, {}, {}, {}, {}
        st.sqv, st.lnsv = {}, {}
        nbt = 4
        for tt in range(TT):
            emit_ln_stats_load(P, t, sem, x1[tt * 128:(tt + 1) * 128, :], tt, st)
            if tt % 2 == 1 or tt >= 4:
                emit_mask2_group(P, t, sem, maskt1, nbt)
                nbt += 1
        emit_ln_stats_smalls(P, t, sem, st)
        for c in range(KC):
            emit_ln_chunk(P, t, sem, x1, c, st)
            if c >= 1:
                emit_z_chunk_mm(P, t, sem, c - 1, st, first=(c == 1))
            if c in (4, 10, 16, 22, 28):
                emit_mask2_group(P, t, sem, maskt1, nbt)
                nbt += 1
        emit_z_chunk_mm(P, t, sem, KC - 1, st, first=False)
        emit_z_close(P, t, sem)
        for tt in range(TT):
            emit_z_evac(P, t, sem, tt, t.zT_bf, t.zT_f32, t.h8T, b8v)
        while nbt < 16:
            emit_mask2_group(P, t, sem, maskt1, nbt)
            nbt += 1
        emit_bmp(P, t, sem)
        emit_anchor_gate(P, t, sem, t.zT_bf[:, 1::2], t.h8T[:, 1::2],
                         t.limt_bf[:])
        emit_l1_post(P, t, sem, gmax8, gsc_dram, bmat, lim)
        P.op("sync", lambda sy: sy.dma_start(out=t.brows0[0:1, :], in_=b_l1f[:, :]),
             waits=[(sem.mpe, P.val(sem.mpe))], inc=sem.bld, inc_n=16)
        P.op("scalar", lambda s: s.copy(t.biasrow_bf[0:1, :], t.brows0[0:1, :]),
             waits=[(sem.bld, P.val(sem.bld))], inc=sem.xb)
        emit_main(P, t, sem, wt_l1a, wt_l1f, x1, y, layer=1,
                  extra_wait=[(sem.bld, P.val(sem.bld)),
                              (sem.mevac, P.val(sem.mevac))],
                  x0_first_wait=[(sem.gch, P.val(sem.gch)), (sem.sq, 8),
                                 (sem.lns[6], 1), (sem.lns[7], 1)])
        P.op("sync", None, waits=[(sem.ost, P.val(sem.ost))])

        for name in ENGS:
            getattr(block, name)(lambda eng, name=name: P.replay(name, eng))
    return nc



_NCA = None
_NCB = None


def _e32_host():
    e = np.zeros((128, D), np.float32)
    e[np.arange(D) // 32, np.arange(D)] = 1.0
    return e


def kernel(L0_attn_data, L0_attn_mask, L0_attn_weight, L0_attn_bias,
           L0_ffn_W, L0_ffn_b, L0_ffn_Wq, L0_ffn_Wk, L0_ffn_A, L0_ffn_Bm,
           L1_attn_weight, L1_attn_mask, L1_ffn_W, L1_ffn_b, L1_ffn_A,
           L1_ffn_Bmat, L1_ffn_LiMEs):
    global _NCA, _NCB
    f32 = np.float32
    cat = np.ascontiguousarray

    x0 = np.asarray(L0_attn_data, f32).reshape(NCORES * TOK, D)
    wt_l0a = cat(np.asarray(L0_attn_weight, f32).T)
    wt_l0f = cat(np.asarray(L0_ffn_W, f32).T)
    wt_l1a = cat(np.asarray(L1_attn_weight, f32).T)
    wt_l1f = cat(np.asarray(L1_ffn_W, f32).T)

    def _maskT2(m):
        return cat(np.asarray(m, np.int32).reshape(128, 32, 128, 32)
                   .transpose(1, 3, 2, 0).reshape(1024, 16384))
    maskt0 = _maskT2(L0_attn_mask)
    maskt1 = _maskT2(L1_attn_mask)
    wqk = cat(np.concatenate([np.asarray(L0_ffn_Wq, f32).T,
                              np.asarray(L0_ffn_Wk, f32).T], axis=1))
    a2t = cat(np.asarray(L0_ffn_A, f32).reshape(128, D).T)
    bm2 = cat(np.asarray(L0_ffn_Bm, f32).transpose(0, 2, 1).reshape(128, D))
    b_l0a = cat(np.asarray(L0_attn_bias, f32)[None, :])
    b_l0f = cat(np.asarray(L0_ffn_b, f32)[None, :])
    ah8 = cat(np.concatenate([np.asarray(L1_ffn_A, f32),
                              np.asarray(L1_ffn_W, f32)[0:8, :].T], axis=1))
    b_l1f = cat(np.asarray(L1_ffn_b, f32)[None, :])
    lim = cat(np.asarray(L1_ffn_LiMEs, f32))
    limt = cat(lim.T)
    bmat = cat(np.asarray(L1_ffn_Bmat, f32))
    e32h = _e32_host()

    if _NCA is None:
        _NCA = build_launch_a()
    in_maps_a = []
    for c in range(NCORES):
        in_maps_a.append(dict(
            x0=cat(x0[c * TOK:(c + 1) * TOK]), wt_l0a=wt_l0a, wt_l0f=wt_l0f,
            maskt0=maskt0, wqk=wqk, a2t=a2t, bm2=bm2, b_l0a=b_l0a,
            b_l0f=b_l0f, ah8=ah8, b_l1f=b_l1f, limt=limt, e32h=e32h))
    res_a = run_bass_kernel_spmd(_NCA, in_maps_a, core_ids=list(range(NCORES)))

    gmax8 = np.concatenate([r["locmax"].reshape(1, 2) for r in res_a.results],
                           axis=0)

    if _NCB is None:
        _NCB = build_launch_b()
    in_maps_b = []
    for c in range(NCORES):
        in_maps_b.append(dict(
            x1=res_a.results[c]["x1"], gmax8=gmax8, wt_l1a=wt_l1a,
            wt_l1f=wt_l1f, maskt1=maskt1, ah8=ah8, limt=limt, lim=lim,
            bmat=bmat, b_l1f=b_l1f, e32h=e32h))
    res_b = run_bass_kernel_spmd(_NCB, in_maps_b, core_ids=list(range(NCORES)))

    y = np.concatenate([r["y"] for r in res_b.results], axis=0)
    return y.reshape(2, NCORES * TOK // 2, D)
